# version 14
# baseline (speedup 1.0000x reference)
import contextlib
import types

import numpy as np
import ml_dtypes
import concourse.bass as bass
import concourse.mybir as mybir
from concourse.bass_utils import run_bass_kernel_spmd

F32 = mybir.dt.float32
BF16 = mybir.dt.bfloat16
AF = mybir.ActivationFunctionType
ALU = mybir.AluOpType
AX = mybir.AxisListType
PM2 = mybir.MatmulPerfMode.DoubleRow

BF = ml_dtypes.bfloat16

D = 4096
T = 8192
TC = 1024
NCORE = 8
NDT = D // 128
NTT = TC // 128
EPS = 1e-5



def pack_lhsT(wT):
    Din, O = wT.shape
    dt_, oc_ = Din // 128, O // 128
    return np.ascontiguousarray(
        wT.reshape(dt_, 128, oc_, 128).transpose(2, 1, 0, 3).reshape(oc_, 128, Din))


def _host_tables():
    t = {}
    L = D
    s = np.arange(L)
    p_of, j_of = s // 32, s % 32
    l_of_r = np.empty(L, dtype=np.int64)
    l_of_r[128 * j_of + p_of] = s

    Omega = np.exp(-2j * np.pi * (np.arange(L) / L))
    tt = (1.0 - Omega) / (1.0 + Omega)
    cc = 2.0 / (1.0 + Omega)

    t["t_r_row"] = tt.real.reshape(1, L).astype(np.float32).copy()
    t["t_i_row"] = tt.imag.reshape(1, L).astype(np.float32).copy()
    lpj = 32 * np.arange(128)[:, None] + np.arange(32)[None, :]
    t["cl_r"] = cc.real[lpj].astype(np.float32).copy()
    t["cl_i"] = cc.imag[lpj].astype(np.float32).copy()

    a = np.arange(128)
    ang_a = 2 * np.pi * np.outer(l_of_r, a) / L
    t["cA"] = (np.cos(ang_a) / L).astype(BF).copy()
    t["sA"] = (np.sin(ang_a) / L).astype(BF).copy()

    m = np.arange(L)
    ang_m = 2 * np.pi * np.outer(l_of_r, m) / L
    cM = np.cos(ang_m).astype(BF)
    sM = np.sin(ang_m).astype(BF)
    cs = np.empty((32, 128, 4, 2048), dtype=BF)
    for a in range(4):
        cs[:, :, a, 0:1024] = cM.reshape(32, 128, 4096)[:, :, 1024 * a:1024 * a + 1024]
        cs[:, :, a, 1024:2048] = sM.reshape(32, 128, 4096)[:, :, 1024 * a:1024 * a + 1024]
    t["csM"] = np.ascontiguousarray(cs.reshape(4096, 8192))

    t["trimask"] = (np.arange(128)[None, :] >= np.arange(128)[:, None]).astype(BF).copy()
    t["id_bf"] = np.eye(128).astype(BF).copy()
    t["ones_p1_bf"] = np.ones((128, 1), dtype=BF)
    t["ones_1p_f32"] = np.ones((1, 128), dtype=np.float32)
    E8 = np.zeros((8, 128), dtype=np.float32)
    for e in range(8):
        E8[e, 16 * e:16 * e + 16] = 2.0
    t["E8"] = E8.astype(BF).copy()
    return t


def _strip_tokens(c):
    return np.concatenate([np.arange(512 * c, 512 * c + 512),
                           np.arange(512 * (15 - c), 512 * (15 - c) + 512)])



class Ctx:

    def __init__(self, nc, stack):
        self.nc = nc
        self.stack = stack
        self.cnt = {}
        self.sems = {}

    def sem(self, name):
        h = self.stack.enter_context(self.nc.semaphore(name))
        self.cnt[name] = 0
        self.sems[name] = h
        return h

    def inc(self, instr, name, n=1):
        self.cnt[name] += n
        instr.then_inc(self.sems[name], n)
        return instr

    def wait(self, eng, name, val=None):
        v = self.cnt[name] if val is None else val
        if v > 0:
            eng.wait_ge(self.sems[name], v)

    def val(self, name):
        return self.cnt[name]


class PsRes:
    _n = 0

    def __init__(self, ctx, ap):
        self.ctx = ctx
        self.ap = ap
        i = PsRes._n
        PsRes._n += 1
        self.tname = f"psT{i}"
        self.cname = f"psC{i}"
        ctx.sem(self.tname)
        ctx.sem(self.cname)

    def prod_wait(self, eng):
        self.ctx.wait(eng, self.cname)

    def prod_done(self, instr):
        return self.ctx.inc(instr, self.tname)

    def cons_wait(self, eng):
        self.ctx.wait(eng, self.tname)

    def cons_done(self, instr):
        return self.ctx.inc(instr, self.cname)


class SerialEng:

    def __init__(self, eng, spacer_ap):
        self._eng = eng
        self._sp = spacer_ap

    def __getattr__(self, name):
        f = getattr(self._eng, name)

        def emit(*a, **k):
            instr = f(*a, **k)
            self._eng.tensor_copy(self._sp, self._sp)
            return instr
        return emit



def build_launch1(taps=(), etaps=False):
    nc = bass.Bass("TRN2", target_bir_lowering=False)
    dram = {}
    din, dout, dint = _mk_dram_helpers(nc, dram)

    din("x0c", [TC, D], F32)
    for nm, shp in [("lam", [128, 1]), ("vc_rT", [64, 64]), ("vc_iT", [64, 64]),
                    ("bpv", [64, 1]), ("ct_r", [64, 1]), ("ct_i", [64, 1]),
                    ("p_r", [64, 1]), ("p_i", [64, 1]), ("q_r", [64, 1]),
                    ("q_i", [64, 1]), ("dpar", [1, 1]), ("logstep", [1, 1]),
                    ("b_base_pk", [128, 32]), ("wrou_pk", [128, 256]),
                    ("aab_pk", [128, 4096]),
                    ("ball", [128, D]), ("t_r_row", [1, D]), ("t_i_row", [1, D]),
                    ("cl_r", [128, 32]), ("cl_i", [128, 32]),
                    ("ones_1p_f32", [1, 128])]:
        din(nm, shp, F32)
    din("cA", [D, 128], BF16)
    din("sA", [D, 128], BF16)
    din("wbase_blk", [NDT, 128, D], F32)
    for nm, shp in [("csM", [D, 2 * D]), ("trimask", [128, 128]),
                    ("id_bf", [128, 128]), ("ones_p1_bf", [128, 1]),
                    ("E8", [8, 128])]:
        din(nm, shp, BF16)

    dout("x1_out", [D, TC], BF16)
    dout("xn1_out", [D, TC], BF16)
    for _dn in ["dbg_m_ln0", "dbg_r_ln0", "dbg_s_ln0"]:
        dout(_dn, [1, TC], F32)
    dint("kbounce", [8, D], F32)
    dint("xs0", [D, TC], BF16)

    tap_out = {}
    for tname, shape, dt in taps:
        tap_out[tname] = dout("tap_" + tname, list(shape), dt)

    with contextlib.ExitStack() as stack:
        n = _alloc_l1(nc, stack, dram)
        ctx = n.ctx = Ctx(nc, stack)
        for s_ in ["sW", "sG", "sO", "sS", "sV", "sT", "sWt", "sVq", "sWd",
                   "sWl", "sWf", "sTg", "sVp", "sTp", "sVx", "sOx", "sWr",
                   "sVr", "sWk", "sVk", "sZ", "sWt0", "sWt1", "sWd0", "sWd1",
                   "sG0", "sG1", "sWl0", "sWl1", "sWx0", "sWx1"]:
            ctx.sem(s_)
        n.PA = PsRes(ctx, n.psA)
        n.PB = PsRes(ctx, n.psB)
        n.PE = PsRes(ctx, n.psE)
        n.PF = PsRes(ctx, n.psF)
        n.PT1 = PsRes(ctx, n.psT1)
        n.PT2 = PsRes(ctx, n.psT2)

        with nc.Block() as block:
            n.block = block
            emit_const_loads(n)
            emit_ssm_prep(n)
            if etaps:
                et = {}
                for nm, shape, dt in [("tvalP", [128, 4096], BF16),
                                      ("tvalQ", [128, 4096], BF16),
                                      ("ar_i", [128, 32], F32),
                                      ("cscrF", [128, 512], F32),
                                      ("sc11", [1, 8], F32)]:
                    et[nm] = dout("etap_" + nm, shape, dt)

                def esy(sy):
                    ctx.wait(sy, "sV", n.ssm_done_sV)
                    ctx.inc(sy.dma_start(out=et["tvalP"], in_=n.bufA[:, 0:4096]), "sZ", 16)
                    ctx.inc(sy.dma_start(out=et["tvalQ"], in_=n.bufA[:, 4096:8192]), "sZ", 16)
                    ctx.inc(sy.dma_start(out=et["ar_i"], in_=n.ar_i[:]), "sZ", 16)
                    ctx.inc(sy.dma_start(out=et["cscrF"], in_=n.cscr[:, 512:1024]), "sZ", 16)
                    ctx.inc(sy.dma_start(out=et["sc11"], in_=n.sc11[0:1, 0:8]), "sZ", 16)
                    ctx.wait(sy, "sZ")
                n.block.sync(esy)
            emit_tbuild(n)
            emit_prologue(n)
            emit_lnstats(n, n.A, dbg="ln0")
            emit_apply_ln(n, n.A, n.B)
            emit_ffn0(n)
            emit_conv(n)
            emit_x1(n)
            emit_taps(n, taps, tap_out)

    return nc, dram


def _mk_dram_helpers(nc, dram):
    def din(name, shape, dt):
        dram[name] = nc.dram_tensor(name, list(shape), dt, kind="ExternalInput").ap()
        return dram[name]

    def dout(name, shape, dt):
        dram[name] = nc.dram_tensor(name, list(shape), dt, kind="ExternalOutput").ap()
        return dram[name]

    def dint(name, shape, dt, **kw):
        dram[name] = nc.dram_tensor(name, list(shape), dt, kind="Internal", **kw).ap()
        return dram[name]
    return din, dout, dint


def _alloc_l1(nc, stack, dram, fused=False):
    n = types.SimpleNamespace()
    n.nc = nc
    n.dram = dram
    sb = lambda name, shape, dt: stack.enter_context(nc.sbuf_tensor(name, list(shape), dt))
    ps = lambda name, shape, dt: stack.enter_context(nc.psum_tensor(name, list(shape), dt))

    n.bufA = sb("bufA", [128, NDT * TC], BF16)
    n.bufB = sb("bufB", [128, NDT * TC], BF16)
    n.A = lambda d: n.bufA[:, d * TC:(d + 1) * TC]
    n.B = lambda d: n.bufB[:, d * TC:(d + 1) * TC]
    n.tvalP = lambda j: n.bufA[:, 128 * j:128 * j + 128]
    n.tvalQ = lambda j: n.bufA[:, 4096 + 128 * j:4096 + 128 * j + 128]
    n.trig = lambda s_: n.bufA[:, 8192 + 128 * s_:8192 + 128 * s_ + 128]
    n.stage = lambda s_: n.bufB[:, D * s_:D * s_ + D]

    if fused:
        n.wbuf = sb("wbuf", [128, 4 * D], BF16)
        n.Tmat = n.wbuf[:, 2 * D:3 * D]
    else:
        n.Tmat = sb("Tmat", [128, D], BF16)
        n.wbuf = sb("wbuf", [128, 2 * D], BF16)
    n.dftbuf = n.wbuf
    n.wrb = sb("wrb", [128, 8 * NDT], BF16)
    n.aab = sb("aab", [128, D], BF16)
    n.blb = sb("blb", [128, D], BF16)
    if fused:
        n.cscr = sb("cscr", [128, 6 * 512], F32)
        n.r_sum = n.cscr[0:1, 0:1024]
        n.r_ssq = n.cscr[0:1, 1024:2048]
        n.r_var = n.cscr[0:1, 2048:3072]
        n.r_rstd = n.r_var
        n.r_mean = n.r_sum
        n.rowb = n.cscr[0:1, 1536:2560]
        n.krr = sb("krr", [128, 512], F32)
        n.kchunk = n.krr[0:8, :]
        n.sspacer = n.cscr[0:1, 1536:2048]
    else:
        n.cscr = sb("cscr", [128, 3 * 512], F32)
        n.kchunk = sb("kchunk", [8, 512], F32)
        n.sspacer = n.cscr[0:1, 1024:1536]
    n.sscr = n.cscr[:, 1024:1536]
    n.vcT = n.cscr[0:64, 0:128]
    n.karr = n.cscr[:, 512:768]
    n.svec = sb("svec", [64, 16], F32)
    n.svec2 = sb("svec2", [64, 16], F32)
    n.vstack = sb("vstack", [128, 8], F32)
    n.clr = sb("clr", [128, 32], F32)
    n.cli = sb("cli", [128, 32], F32)
    n.ar_r = sb("ar_r", [128, 32], F32)
    n.ar_i = sb("ar_i", [128, 32], F32)
    n.ar_in = sb("ar_in", [128, 32], F32)
    n.lamn = sb("lamn", [128, 1], F32)
    if not fused:
        n.rowb = sb("rowb", [1, 1024], F32)
    n.sc11 = sb("sc11", [1, 8], F32)
    n.dvec = sb("dvec", [128, 1], F32)

    n.sqst = sb("sqst", [128, 2 * TC], BF16)
    n.rstage = sb("rstage", [128, 2 * TC], BF16)
    n.ctall = n.sqst[0:8, 0:TC]
    n.cexp = n.rstage[:, 0:TC]
    n.hbuf = n.rstage[:, TC:2 * TC]
    n.hsc = n.hbuf
    if not fused:
        n.r_sum = sb("r_sum", [1, TC], F32)
        n.r_ssq = sb("r_ssq", [1, TC], F32)
        n.r_rstd = sb("r_rstd", [1, TC], F32)
        n.r_mean = n.r_sum
        n.r_var = n.r_ssq
    n.mrep = sb("mrep", [128, TC], BF16)
    n.rrep = sb("rrep", [128, TC], BF16)
    n.bbuf = sb("bbuf", [128, NDT], F32)
    n.lgst = sb("lgst", [128, 64], F32)
    n.ebuf = sb("ebuf", [128, 64], F32)
    n.m1b = sb("m1b", [128, 8], F32)
    n.nm1b = sb("nm1b", [128, 8], F32)
    n.m2b = sb("m2b", [128, 8], F32)
    n.rcb = sb("rcb", [128, 8], F32)
    n.cmbt = sb("cmbt", [128, 64], BF16)

    n.sb_id = sb("sb_id", [128, 128], BF16)
    n.sb_ones_p1 = sb("sb_ones_p1", [128, 1], BF16)
    n.sb_ones_1p = sb("sb_ones_1p", [1, 128], F32)
    n.sb_tri = sb("sb_tri", [128, 128], BF16)
    n.sb_E8 = sb("sb_E8", [8, 128], BF16)

    n.psA = ps("psA", [128, 512], F32)
    n.psB = ps("psB", [128, 512], F32)
    n.psE = ps("psE", [128, 1024], F32)
    n.psF = ps("psF", [128, 1024], F32)
    n.psT1 = ps("psT1", [128, 1024], BF16)
    n.psT2 = ps("psT2", [128, 1024], BF16)
    return n



def emit_const_loads(n):
    ctx, d = n.ctx, n.dram

    def f(sy):
        for dst, src in [(n.sb_id, d["id_bf"]), (n.sb_ones_p1, d["ones_p1_bf"]),
                         (n.sb_ones_1p, d["ones_1p_f32"]), (n.sb_tri, d["trimask"]),
                         (n.sb_E8, d["E8"]), (n.lamn, d["lam"]),
                         (n.clr, d["cl_r"]), (n.cli, d["cl_i"])]:
            ctx.inc(sy.dma_start(out=dst[:], in_=src), "sW", 16)
        for i, src in enumerate(["bpv", "ct_r", "ct_i", "p_r", "p_i", "q_r", "q_i"]):
            ctx.inc(sy.dma_start(out=n.svec[:, i:i + 1], in_=d[src]), "sW", 16)
        ctx.inc(sy.dma_start(out=n.sc11[:, 0:1], in_=d["logstep"]), "sW", 16)
        ctx.inc(sy.dma_start(out=n.sc11[:, 1:2], in_=d["dpar"]), "sW", 16)
        ctx.inc(sy.dma_start(out=n.vcT[:, 0:64], in_=d["vc_rT"]), "sW", 16)
        ctx.inc(sy.dma_start(out=n.vcT[:, 64:128], in_=d["vc_iT"]), "sW", 16)
        ctx.inc(sy.dma_start(out=n.bbuf[:, 0:32], in_=d["b_base_pk"]), "sW", 16)
    n.block.sync(f)

    def g(gq):
        ctx.inc(gq.dma_start(out=n.wrb[:], in_=d["wrou_pk"]), "sG", 16)
        ctx.inc(gq.dma_start(out=n.aab[:], in_=d["aab_pk"]), "sG", 16)
        ctx.inc(gq.dma_start(out=n.blb[:], in_=d["ball"]), "sG", 16)
    n.block.gpsimd(g)
    n.const_sW = ctx.val("sW")
    n.const_sG = ctx.val("sG")


def emit_ssm_prep(n):
    ctx, d = n.ctx, n.dram

    def sc(s):
        ctx.wait(s, "sW", n.const_sW)
        s.activation(n.sc11[0:1, 2:3], n.sc11[0:1, 0:1], AF.Exp, scale=-1.0)
        ctx.inc(s.copy(n.sspacer[:], n.sspacer[:]), "sS")
    n.block.scalar(sc)

    def v1(v):
        ctx.wait(v, "sS")
        v.tensor_scalar_mul(n.sc11[0:1, 3:4], n.sc11[0:1, 2:3], 2.0)
        ctx.inc(v.tensor_scalar_mul(n.lamn[:], n.lamn[:], -1.0), "sV")
    n.block.vector(v1)
    g_ready = ctx.val("sV")

    def t1(t):
        ctx.wait(t, "sV", g_ready)
        n.PA.prod_wait(t)
        t.matmul(n.psA[0:64, 0:1], n.vcT[:, 0:64], n.svec[:, 0:1], start=True, stop=True)
        n.PA.prod_done(
            t.matmul(n.psA[0:64, 1:2], n.vcT[:, 64:128], n.svec[:, 0:1],
                     start=True, stop=True))
    n.block.tensor(t1)

    def v2(v):
        rv = v
        n.PA.cons_wait(v)
        v = SerialEng(v, n.mrep[:, 0:512])
        v.tensor_copy(n.svec[:, 7:8], n.psA[0:64, 0:1])
        n.PA.cons_done(v.tensor_copy(n.svec[:, 8:9], n.psA[0:64, 1:2]))
        C = lambda i: n.svec[:, i:i + 1]
        W = lambda i: n.svec2[:, i:i + 1]
        BR, BI, CR, CI = C(7), C(8), C(1), C(2)
        PR, PI, QR, QI = C(3), C(4), C(5), C(6)
        pairs = [(CR, CI, BR, BI, 0), (CR, CI, PR, PI, 2),
                 (QR, QI, BR, BI, 4), (QR, QI, PR, PI, 6)]
        for (xr, xi, yr, yi, o) in pairs:
            v.tensor_mul(W(o), xr, yr)
            v.tensor_mul(W(8), xi, yi)
            v.tensor_add(W(o), W(o), W(8))
            v.tensor_mul(W(o + 1), xr, yi)
            v.tensor_mul(W(8), xi, yr)
            v.tensor_sub(W(o + 1), W(o + 1), W(8))
        for cidx in range(4):
            vr, vi = W(2 * cidx), W(2 * cidx + 1)
            v.tensor_copy(n.vstack[0:64, 2 * cidx:2 * cidx + 1], vr)
            v.tensor_copy(n.vstack[64:128, 2 * cidx:2 * cidx + 1], vi)
            v.tensor_copy(n.vstack[0:64, 2 * cidx + 1:2 * cidx + 2], vi)
            v.tensor_scalar_mul(W(9), vr, -1.0)
            v.tensor_copy(n.vstack[64:128, 2 * cidx + 1:2 * cidx + 2], W(9))
        ctx.inc(rv.tensor_copy(n.mrep[:, 0:512], n.mrep[:, 0:512]), "sV")
    n.block.vector(v2)
    vstack_done = ctx.val("sV")

    CS = lambda i: n.cscr[:, 512 * i:512 * i + 512]
    RB = lambda ch, h: n.rowb[0:1, 512 * h:512 * h + 512]
    pe_t0 = ctx.val(n.PE.tname)
    wk0 = ctx.val("sWk")
    for ch in range(8):
        cs = slice(512 * ch, 512 * ch + 512)

        def sy2(sy, ch=ch, cs=cs):
            if ch >= 1:
                sy.wait_ge(ctx.sems[n.PE.tname], pe_t0 + ch)
            ctx.inc(sy.dma_start(out=RB(ch, 0), in_=d["t_r_row"][0:1, cs]), "sWr", 16)
            ctx.inc(sy.dma_start(out=RB(ch, 1), in_=d["t_i_row"][0:1, cs]), "sWr", 16)
        n.block.sync(sy2)

        def vr(v, ch=ch):
            ctx.wait(v, "sWr")
            if ch == 0:
                ctx.wait(v, "sV", vstack_done)
            sl = n.rowb[0:1, 0:1024]
            ctx.inc(v.tensor_scalar(sl, sl, n.sc11[0:1, 3:4], None, ALU.mult), "sVr")
        n.block.vector(vr)

        def t2(t, ch=ch):
            n.PE.prod_wait(t)
            ctx.wait(t, "sVr")
            t.matmul(n.psE[0:64, 0:512], n.sb_ones_1p[:, 0:64], RB(ch, 0),
                     start=True, stop=True)
            n.PE.prod_done(
                t.matmul(n.psE[64:128, 0:512], n.sb_ones_1p[:, 0:64], RB(ch, 1),
                         start=True, stop=True))
        n.block.tensor(t2)

        def v3(v, ch=ch):
            n.PE.cons_wait(v)
            if ch > 0:
                ctx.wait(v, n.PF.tname)
            n.PE.cons_done(
                v.tensor_scalar(CS(0), n.psE[:, 0:512], n.lamn[:], None, ALU.add))
            v.tensor_copy(CS(2)[0:64, :], CS(0)[64:128, :])
            v.tensor_mul(CS(2)[0:64, :], CS(2)[0:64, :], CS(2)[0:64, :])
            v.tensor_mul(CS(1)[0:64, :], CS(0)[0:64, :], CS(0)[0:64, :])
            v.tensor_add(CS(2)[0:64, :], CS(2)[0:64, :], CS(1)[0:64, :])
            v.reciprocal(CS(2)[0:64, :], CS(2)[0:64, :])
            v.tensor_mul(CS(1)[0:64, :], CS(0)[0:64, :], CS(2)[0:64, :])
            v.tensor_copy(CS(2)[64:128, :], CS(2)[0:64, :])
            ctx.inc(v.tensor_mul(CS(1)[64:128, :], CS(0)[64:128, :], CS(2)[64:128, :]),
                    "sVq")
        n.block.vector(v3)
        ii_done = ctx.val("sVq")

        def t3(t, ii_done=ii_done):
            ctx.wait(t, "sVq", ii_done)
            n.PF.prod_wait(t)
            n.PF.prod_done(
                t.matmul(n.psF[0:8, 0:512], n.vstack[:], CS(1), start=True, stop=True))
        n.block.tensor(t3)

        def v4(v, ch=ch):
            n.PF.cons_wait(v)
            if ch >= 1:
                v.wait_ge(ctx.sems["sWk"], wk0 + 16 * ch)
            n.PF.cons_done(v.tensor_copy(n.kchunk[:], n.psF[0:8, 0:512]))
            ctx.inc(v.tensor_copy(n.mrep[:, 0:512], n.mrep[:, 0:512]), "sVk")
        n.block.vector(v4)

        def sy3(sy, ch=ch, cs=cs):
            ctx.wait(sy, "sVk")
            ctx.inc(sy.dma_start(out=d["kbounce"][:, cs], in_=n.kchunk[:]),
                    "sWk", 16)
        n.block.sync(sy3)

    def sy4(sy):
        ctx.wait(sy, "sWk")
        for comp in range(8):
            ctx.inc(sy.dma_start(
                out=n.karr[:, 32 * comp:32 * comp + 32],
                in_=d["kbounce"][comp:comp + 1, :].rearrange("o (p j) -> (o p) j", p=128)),
                "sW", 16)
    n.block.sync(sy4)
    karr_in = ctx.val("sW")

    def v5(v):
        rv = v
        ctx.wait(v, "sW", karr_in)
        v = SerialEng(v, n.mrep[:, 0:512])
        K = lambda c: n.karr[:, 32 * c:32 * c + 32]
        k00r, k00i, k01r, k01i, k10r, k10i, k11r, k11i = [K(c) for c in range(8)]
        S = lambda i: n.sscr[:, 64 * i:64 * i + 32]
        v.tensor_scalar_add(S(0), k11r, 1.0)
        v.tensor_mul(S(1), S(0), S(0))
        v.tensor_mul(S(2), k11i, k11i)
        v.tensor_add(S(1), S(1), S(2))
        v.reciprocal(S(1), S(1))
        v.tensor_mul(S(2), k01r, k10r)
        v.tensor_mul(S(3), k01i, k10i)
        v.tensor_sub(S(2), S(2), S(3))
        v.tensor_mul(S(3), k01r, k10i)
        v.tensor_mul(S(4), k01i, k10r)
        v.tensor_add(S(3), S(3), S(4))
        v.tensor_mul(S(4), S(2), S(0))
        v.tensor_mul(S(5), S(3), k11i)
        v.tensor_add(S(4), S(4), S(5))
        v.tensor_mul(S(4), S(4), S(1))
        v.tensor_mul(S(5), S(3), S(0))
        v.tensor_mul(S(6), S(2), k11i)
        v.tensor_sub(S(5), S(5), S(6))
        v.tensor_mul(S(5), S(5), S(1))
        v.tensor_sub(S(4), k00r, S(4))
        v.tensor_sub(S(5), k00i, S(5))
        v.tensor_mul(n.ar_r[:], n.clr[:], S(4))
        v.tensor_mul(S(6), n.cli[:], S(5))
        v.tensor_sub(n.ar_r[:], n.ar_r[:], S(6))
        v.tensor_mul(n.ar_i[:], n.clr[:], S(5))
        v.tensor_mul(S(6), n.cli[:], S(4))
        v.tensor_add(n.ar_i[:], n.ar_i[:], S(6))
        v.tensor_scalar_mul(n.ar_in[:], n.ar_i[:], -1.0)
        ctx.inc(rv.tensor_copy(n.mrep[:, 0:512], n.mrep[:, 0:512]), "sV")
    n.block.vector(v5)
    ar_done = ctx.val("sV")

    wt0 = [ctx.val("sWt0"), ctx.val("sWt1")]

    def sy3(sy):
        for j in range(32):
            s0 = 2 * (j % 2)
            if j >= 2:
                ctx.wait(sy, "sVq", ii_done + (j - 1))
            ctx.inc(sy.dma_start(out=n.trig(s0), in_=d["cA"][128 * j:128 * j + 128, :]),
                    f"sWt{j % 2}", 16)
            ctx.inc(sy.dma_start(out=n.trig(s0 + 1), in_=d["sA"][128 * j:128 * j + 128, :]),
                    f"sWt{j % 2}", 16)
    n.block.sync(sy3)

    def v6(v):
        ctx.wait(v, "sV", ar_done)
        F = lambda i: n.cscr[:, 640 * i:640 * i + 128]
        for j in range(32):
            s0 = 2 * (j % 2)
            v.wait_ge(ctx.sems[f"sWt{j % 2}"], wt0[j % 2] + 32 * (j // 2 + 1))
            arj, aij, anj = n.ar_r[:, j:j + 1], n.ar_i[:, j:j + 1], n.ar_in[:, j:j + 1]
            v.tensor_scalar(F(0), n.trig(s0), arj, None, ALU.mult)
            v.scalar_tensor_tensor(n.tvalP(j), n.trig(s0 + 1), aij, F(0),
                                   ALU.mult, ALU.add)
            v.tensor_scalar(F(1), n.trig(s0 + 1), arj, None, ALU.mult)
            ctx.inc(v.scalar_tensor_tensor(n.tvalQ(j), n.trig(s0), anj, F(1),
                                           ALU.mult, ALU.add), "sVq")
        ctx.inc(v.tensor_copy(n.mrep[:, 0:512], n.mrep[:, 0:512]), "sV")
    n.block.vector(v6)
    n.ssm_done_sV = ctx.val("sV")


def emit_tbuild(n):
    ctx, d = n.ctx, n.dram
    wd0 = [ctx.val("sWd0"), ctx.val("sWd1")]
    td0 = ctx.val("sTg")

    def sy(sy_):
        for a in range(4):
            for j in range(32):
                it = a * 32 + j
                slot = it % 2
                if it >= 2:
                    ctx.wait(sy_, "sTg", td0 + it - it // 32)
                ctx.inc(sy_.dma_start(
                    out=n.dftbuf[:, 2048 * slot:2048 * slot + 2048],
                    in_=d["csM"][128 * j:128 * j + 128,
                                 2048 * a:2048 * a + 2048]),
                    f"sWd{slot}", 16)
    n.block.sync(sy)

    for a in range(4):
        def t(te, a=a):
            if a == 0:
                ctx.wait(te, "sV", n.ssm_done_sV)
            n.PA.prod_wait(te)
            n.PB.prod_wait(te)
            for j in range(32):
                it = a * 32 + j
                slot = it % 2
                sl = n.dftbuf[:, 2048 * slot:2048 * slot + 2048]
                te.wait_ge(ctx.sems[f"sWd{slot}"], wd0[slot] + 16 * (it // 2 + 1))
                te.matmul(n.psA[:, 0:512], n.tvalP(j), sl[:, 0:512],
                          start=(j == 0), stop=False)
                te.matmul(n.psB[:, 0:512], n.tvalP(j), sl[:, 512:1024],
                          start=(j == 0), stop=False)
                mma = te.matmul(n.psA[:, 0:512], n.tvalQ(j), sl[:, 1024:1536],
                                start=False, stop=(j == 31))
                mmb = te.matmul(n.psB[:, 0:512], n.tvalQ(j), sl[:, 1536:2048],
                                start=False, stop=(j == 31))
                if j < 31:
                    ctx.inc(mmb, "sTg")
            n.PA.prod_done(mma)
            n.PB.prod_done(mmb)
        n.block.tensor(t)

        def v(ve, a=a):
            n.PA.cons_wait(ve)
            n.PA.cons_done(
                ve.tensor_copy(n.Tmat[:, 1024 * a:1024 * a + 512],
                               n.psA[:, 0:512]))
            n.PB.cons_wait(ve)
            n.PB.cons_done(
                ve.tensor_copy(n.Tmat[:, 1024 * a + 512:1024 * a + 1024],
                               n.psB[:, 0:512]))
        n.block.vector(v)

    def v2(ve):
        ctx.inc(ve.tensor_mul(n.Tmat[:, 0:128], n.Tmat[:, 0:128], n.sb_tri[:]), "sV")
    n.block.vector(v2)
    n.tbuild_done_sV = ctx.val("sV")
    n.tbuild_pbc = ctx.val(n.PB.cname)


def emit_prologue(n):
    ctx, d = n.ctx, n.dram
    g0 = [ctx.val("sG0"), ctx.val("sG1")]
    t1a = ctx.val(n.PT1.tname)
    t2a = ctx.val(n.PT2.tname)

    def g(gq):
        for tc in range(NTT):
            if tc >= 2:
                gq.wait_ge(ctx.sems[n.PT1.tname], t1a + 2 * (tc - 1))
                gq.wait_ge(ctx.sems[n.PT2.tname], t2a + 2 * (tc - 1))
            ctx.inc(gq.dma_start(out=n.stage(tc % 2),
                                 in_=d["x0c"][128 * tc:128 * tc + 128, :]),
                    f"sG{tc % 2}", 16)
    n.block.gpsimd(g)

    for tc in range(NTT):
        for grp in range(4):
            res = n.PT1 if grp % 2 == 0 else n.PT2

            def t(te, tc=tc, grp=grp, res=res):
                if tc == 0 and grp == 0:
                    ctx.wait(te, "sV", n.tbuild_done_sV)
                if grp == 0:
                    te.wait_ge(ctx.sems[f"sG{tc % 2}"],
                               g0[tc % 2] + 16 * (tc // 2 + 1))
                res.prod_wait(te)
                for k in range(8):
                    dt_ = grp * 8 + k
                    mm = te.transpose(
                        res.ap[:, 128 * k:128 * k + 128],
                        n.stage(tc % 2)[:, 128 * dt_:128 * dt_ + 128], n.sb_id[:])
                res.prod_done(mm)
            n.block.tensor(t)

            def v(ve, tc=tc, grp=grp, res=res):
                res.cons_wait(ve)
                for k in range(8):
                    dt_ = grp * 8 + k
                    last = ve.tensor_copy(n.A(dt_)[:, 128 * tc:128 * tc + 128],
                                          res.ap[:, 128 * k:128 * k + 128])
                res.cons_done(last)
            n.block.vector(v)

    def vfin(ve):
        ctx.inc(ve.tensor_copy(n.mrep[:, 0:512], n.mrep[:, 0:512]), "sV")
    n.block.vector(vfin)
    x0T_done = ctx.val("sV")

    def so(sq):
        ctx.wait(sq, "sV", x0T_done)
        for dt_ in range(NDT):
            ctx.inc(sq.dma_start(out=d["xs0"][128 * dt_:128 * dt_ + 128, :],
                                 in_=n.A(dt_)), "sO", 16)
    n.block.scalar(so)
    n.xs0_done_sO = ctx.val("sO")


def emit_lnstats(n, X, dbg=None):
    ctx = n.ctx
    ts0 = ctx.val("sTg")

    for dt_ in range(NDT):
        def v(ve, dt_=dt_):
            if dt_ >= 2:
                ctx.wait(ve, "sTg", ts0 + dt_ - 1)
            ctx.inc(ve.tensor_mul(n.sqst[:, TC * (dt_ % 2):TC * (dt_ % 2) + TC],
                                  X(dt_), X(dt_)), "sVq")
        n.block.vector(v)

        def t(te, dt_=dt_):
            if dt_ == 0:
                n.PE.prod_wait(te)
                n.PF.prod_wait(te)
            ctx.wait(te, "sVq")
            first, last = dt_ == 0, dt_ == NDT - 1
            sq = n.sqst[:, TC * (dt_ % 2):TC * (dt_ % 2) + TC]
            te.matmul(n.psE[0:1, 0:512], n.sb_ones_p1[:], X(dt_)[:, 0:512],
                      start=first, stop=last)
            mmE = te.matmul(n.psE[0:1, 512:1024], n.sb_ones_p1[:], X(dt_)[:, 512:1024],
                            start=first, stop=last)
            mm3 = te.matmul(n.psF[0:1, 0:512], n.sb_ones_p1[:], sq[:, 0:512],
                            start=first, stop=last)
            mmF = te.matmul(n.psF[0:1, 512:1024], n.sb_ones_p1[:], sq[:, 512:1024],
                            start=first, stop=last)
            if last:
                n.PE.prod_done(mmE)
                n.PF.prod_done(mmF)
            else:
                ctx.inc(mm3, "sTg")
        n.block.tensor(t)

    def v2(ve):
        n.PE.cons_wait(ve)
        n.PF.cons_wait(ve)
        ve.tensor_copy(n.r_sum[:], n.psE[0:1, :])
        n.PE.cons_done(ve.tensor_copy(n.r_ssq[:], n.psF[0:1, :]))
        n.PF.cons_done(ve.tensor_scalar_mul(n.r_mean[:], n.r_sum[:], 1.0 / D))
        ve.tensor_mul(n.r_rstd[:], n.r_mean[:], n.r_mean[:])
        ve.scalar_tensor_tensor(n.r_var[:], n.r_ssq[:], 1.0 / D, n.r_rstd[:],
                                ALU.mult, ALU.subtract)
        ctx.inc(ve.tensor_scalar_add(n.r_var[:], n.r_var[:], EPS), "sV")
    n.block.vector(v2)

    def sc(s):
        ctx.wait(s, "sV")
        s.activation(n.r_rstd[:], n.r_var[:], AF.Ln)
        s.activation(n.r_rstd[:], n.r_rstd[:], AF.Exp, scale=-0.5)
        ctx.inc(s.copy(n.sspacer[:], n.sspacer[:]), "sS")
    n.block.scalar(sc)
    rstd_done = ctx.val("sS")

    def t2(te):
        ctx.wait(te, "sS", rstd_done)
        n.PE.prod_wait(te)
        te.matmul(n.psE[:, 0:512], n.sb_ones_1p[:], n.r_mean[0:1, 0:512],
                  start=True, stop=True)
        n.PE.prod_done(
            te.matmul(n.psE[:, 512:1024], n.sb_ones_1p[:], n.r_mean[0:1, 512:1024],
                      start=True, stop=True))
        n.PF.prod_wait(te)
        te.matmul(n.psF[:, 0:512], n.sb_ones_1p[:], n.r_rstd[0:1, 0:512],
                  start=True, stop=True)
        n.PF.prod_done(
            te.matmul(n.psF[:, 512:1024], n.sb_ones_1p[:], n.r_rstd[0:1, 512:1024],
                      start=True, stop=True))
    n.block.tensor(t2)

    def v3(ve):
        n.PE.cons_wait(ve)
        n.PE.cons_done(ve.tensor_copy(n.mrep[:], n.psE[:]))
        n.PF.cons_wait(ve)
        n.PF.cons_done(ve.tensor_copy(n.rrep[:], n.psF[:]))
        ctx.inc(ve.tensor_copy(n.mrep[:, 0:512], n.mrep[:, 0:512]), "sV")
    n.block.vector(v3)

    if dbg is not None and ("dbg_m_" + dbg) in n.dram:
        def dv(sq):
            ctx.wait(sq, "sV")
            ctx.inc(sq.dma_start(out=n.dram["dbg_m_" + dbg], in_=n.r_mean[:]), "sO", 16)
            ctx.inc(sq.dma_start(out=n.dram["dbg_r_" + dbg], in_=n.r_rstd[:]), "sO", 16)
            ctx.inc(sq.dma_start(out=n.dram["dbg_s_" + dbg], in_=n.r_ssq[:]), "sO", 16)
        n.block.scalar(dv)


def emit_apply_ln(n, src, dst, wait_sOx=None):
    ctx = n.ctx

    def v(ve):
        if wait_sOx is not None:
            ctx.wait(ve, "sOx", wait_sOx)
        for dt_ in range(NDT):
            sl = n.sqst[:, TC * (dt_ % 2):TC * (dt_ % 2) + TC]
            ve.tensor_tensor(sl, src(dt_), n.mrep[:], ALU.subtract)
            ctx.inc(ve.tensor_tensor(dst(dt_), sl, n.rrep[:], ALU.mult), "sV")
    n.block.vector(v)
    n.xn_done_sV = ctx.val("sV")


def emit_ffn0(n):
    ctx, d = n.ctx, n.dram

    def t1(te):
        ctx.wait(te, "sV", n.xn_done_sV)
        ctx.wait(te, "sG", n.const_sG)
        n.PA.prod_wait(te)
        for tt in range(NTT):
            for dt_ in range(NDT):
                mm = te.matmul(n.psA[:, 8 * tt:8 * tt + 8],
                               n.B(dt_)[:, 128 * tt:128 * tt + 128],
                               n.wrb[:, 8 * dt_:8 * dt_ + 8],
                               start=(dt_ == 0), stop=(dt_ == NDT - 1))
        n.PA.prod_done(mm)
        n.PB.prod_wait(te)
        n.PB.prod_done(te.matmul(n.psB[:, 0:1], n.sb_ones_1p[:], n.sc11[0:1, 1:2],
                                 start=True, stop=True))
    n.block.tensor(t1)

    def v1(ve):
        n.PB.cons_wait(ve)
        n.PB.cons_done(ve.tensor_copy(n.dvec[:], n.psB[:, 0:1]))
        n.PA.cons_wait(ve)
        n.PA.cons_done(ve.tensor_copy(n.lgst[:], n.psA[:, 0:64]))
        vs = SerialEng(ve, n.mrep[:, 0:512])
        for tt in range(NTT):
            lg = n.lgst[:, 8 * tt:8 * tt + 8]
            vs.reduce_max(n.m1b[:, tt:tt + 1], lg, axis=AX.X)
        vs.tensor_scalar_mul(n.nm1b[:], n.m1b[:], -1.0)
        ctx.inc(ve.tensor_copy(n.mrep[:, 0:512], n.mrep[:, 0:512]), "sV")
    n.block.vector(v1)
    r1 = ctx.val("sV")

    def s1(s):
        ctx.wait(s, "sV", r1)
        for tt in range(NTT):
            s.activation(n.ebuf[:, 8 * tt:8 * tt + 8],
                         n.lgst[:, 8 * tt:8 * tt + 8], AF.Exp,
                         bias=n.nm1b[:, tt:tt + 1])
        ctx.inc(s.copy(n.sspacer[:], n.sspacer[:]), "sS")
    n.block.scalar(s1)
    e_done = ctx.val("sS")

    def v2(ve):
        rv = ve
        ctx.wait(ve, "sS", e_done)
        ve = SerialEng(ve, n.mrep[:, 0:512])
        SS = lambda i: n.sscr[:, 64 * i:64 * i + 8]
        for tt in range(NTT):
            lg = n.lgst[:, 8 * tt:8 * tt + 8]
            eb = n.ebuf[:, 8 * tt:8 * tt + 8]
            ve.tensor_scalar(SS(0), lg, n.m1b[:, tt:tt + 1], None, ALU.is_ge)
            ve.scalar_tensor_tensor(SS(1), SS(0), -1e30, lg, ALU.mult, ALU.add)
            ve.reduce_max(n.m2b[:, tt:tt + 1], SS(1), axis=AX.X)
            ve.tensor_scalar(SS(2), lg, n.m2b[:, tt:tt + 1], None, ALU.is_ge)
            ve.tensor_mul(eb, eb, SS(2))
            ve.reduce_sum(n.rcb[:, tt:tt + 1], eb, axis=AX.X)
        ve.reciprocal(n.rcb[:], n.rcb[:])
        for tt in range(NTT):
            eb = n.ebuf[:, 8 * tt:8 * tt + 8]
            ve.tensor_scalar(n.cmbt[:, 8 * tt:8 * tt + 8], eb,
                             n.rcb[:, tt:tt + 1], None, ALU.mult)
        ctx.inc(rv.tensor_copy(n.mrep[:, 0:512], n.mrep[:, 0:512]), "sV")
    n.block.vector(v2)
    cmb_done = ctx.val("sV")

    def t2(te):
        ctx.wait(te, "sV", cmb_done)
        n.PT1.prod_wait(te)
        for tt in range(NTT):
            mm = te.transpose(n.psT1[0:8, 128 * tt:128 * tt + 128],
                              n.cmbt[:, 8 * tt:8 * tt + 8], n.sb_id[:])
        n.PT1.prod_done(mm)
    n.block.tensor(t2)

    def v3(ve):
        n.PT1.cons_wait(ve)
        n.PT1.cons_done(ve.tensor_copy(n.ctall[:], n.psT1[0:8, :]))
        ctx.inc(ve.tensor_copy(n.mrep[:, 0:512], n.mrep[:, 0:512]), "sV")
    n.block.vector(v3)
    ct_done = ctx.val("sV")

    def t3(te):
        ctx.wait(te, "sV", ct_done)
        n.PE.prod_wait(te)
        te.matmul(n.psE[:, 0:512], n.sb_E8[:], n.ctall[:, 0:512], start=True, stop=True)
        n.PE.prod_done(te.matmul(n.psE[:, 512:1024], n.sb_E8[:], n.ctall[:, 512:1024],
                                 start=True, stop=True))
        n.PF.prod_wait(te)
        for dt_ in range(NDT):
            first, last = dt_ == 0, dt_ == NDT - 1
            te.matmul(n.psF[:, 0:512], n.aab[:, 128 * dt_:128 * dt_ + 128],
                      n.B(dt_)[:, 0:512], start=first, stop=last)
            mm = te.matmul(n.psF[:, 512:1024], n.aab[:, 128 * dt_:128 * dt_ + 128],
                           n.B(dt_)[:, 512:1024], start=first, stop=last)
        n.PF.prod_done(mm)
    n.block.tensor(t3)

    def v4(ve):
        n.PE.cons_wait(ve)
        n.PE.cons_done(ve.tensor_copy(n.cexp[:], n.psE[:]))
        n.PF.cons_wait(ve)
        n.PF.cons_done(ve.tensor_copy(n.hbuf[:], n.psF[:]))
        ctx.inc(ve.tensor_mul(n.hbuf[:], n.hbuf[:], n.cexp[:]), "sV")
    n.block.vector(v4)
    hsc_done = ctx.val("sV")

    wl0 = [ctx.val("sWl0"), ctx.val("sWl1")]
    wf0 = ctx.val("sWf")

    def g1(gq):
        for oc in range(NDT):
            if oc == 0:
                ctx.wait(gq, n.PB.cname, n.tbuild_pbc)
            if oc >= 2:
                ctx.wait(gq, "sWf", wf0 + oc - 1)
            ctx.inc(gq.dma_start(out=n.wbuf[:, D * (oc % 2):D * (oc % 2) + D],
                                 in_=d["wbase_blk"][oc]), f"sWl{oc % 2}", 16)
    n.block.gpsimd(g1)

    for oc in range(NDT):
        res = n.PE if oc % 2 == 0 else n.PF

        def t4(te, oc=oc, res=res):
            if oc == 0:
                ctx.wait(te, "sV", hsc_done)
            res.prod_wait(te)
            te.wait_ge(ctx.sems[f"sWl{oc % 2}"], wl0[oc % 2] + 16 * (oc // 2 + 1))
            wb = lambda dt_: n.wbuf[:, D * (oc % 2) + 128 * dt_:D * (oc % 2) + 128 * dt_ + 128]
            for dt_ in range(NDT):
                te.matmul(res.ap[:, 0:512], wb(dt_), n.B(dt_)[:, 0:512],
                          start=(dt_ == 0), stop=False)
                mw = te.matmul(res.ap[:, 512:1024], wb(dt_), n.B(dt_)[:, 512:1024],
                               start=(dt_ == 0), stop=False)
            ctx.inc(mw, "sWf")
            te.matmul(res.ap[:, 0:512], n.blb[:, 128 * oc:128 * oc + 128],
                      n.hsc[:, 0:512], start=False, stop=True)
            res.prod_done(
                te.matmul(res.ap[:, 512:1024], n.blb[:, 128 * oc:128 * oc + 128],
                          n.hsc[:, 512:1024], start=False, stop=True))
        n.block.tensor(t4)

        def v5(ve, oc=oc, res=res):
            res.cons_wait(ve)
            res.cons_done(ve.tensor_scalar(n.A(oc), res.ap[:], n.bbuf[:, oc:oc + 1],
                                           None, ALU.add))
        n.block.vector(v5)

    def v5f(ve):
        ctx.inc(ve.tensor_copy(n.mrep[:, 0:512], n.mrep[:, 0:512]), "sV")
    n.block.vector(v5f)
    n.ffn0_done_sV = ctx.val("sV")


def emit_conv(n):
    ctx = n.ctx

    for I in range(NDT):
        res = n.PE if I % 2 == 0 else n.PF

        def t(te, I=I, res=res):
            if I == 0:
                ctx.wait(te, "sV", n.ffn0_done_sV)
            res.prod_wait(te)
            for J in range(I + 1):
                dd = I - J
                first, last = J == 0, J == I
                te.matmul(res.ap[:, 0:512], n.Tmat[:, 128 * dd:128 * dd + 128],
                          n.B(J)[:, 0:512], start=first, stop=last)
                mm = te.matmul(res.ap[:, 512:1024], n.Tmat[:, 128 * dd:128 * dd + 128],
                               n.B(J)[:, 512:1024], start=first, stop=last)
            res.prod_done(mm)
        n.block.tensor(t)

        def v(ve, I=I, res=res):
            res.cons_wait(ve)
            sl = n.rstage[:, TC * (I % 2):TC * (I % 2) + TC]
            res.cons_done(ve.scalar_tensor_tensor(sl, n.B(I), n.dvec[:], res.ap[:],
                                                  ALU.mult, ALU.add))
            ctx.inc(ve.tensor_tensor(n.A(I), n.A(I), sl, ALU.add), "sV")
        n.block.vector(v)
    n.conv_done_sV = ctx.val("sV")


def emit_x1(n):
    ctx, d = n.ctx, n.dram
    w0 = [ctx.val("sWx0"), ctx.val("sWx1")]
    vx0 = ctx.val("sVx")
    ox0 = ctx.val("sOx")

    def sy(sy_):
        ctx.wait(sy_, "sO", n.xs0_done_sO)
        ctx.wait(sy_, "sV", n.conv_done_sV)
        for dt_ in range(NDT):
            if dt_ >= 2:
                ctx.wait(sy_, "sVx", vx0 + dt_ - 1)
            ctx.inc(sy_.dma_start(out=n.rstage[:, TC * (dt_ % 2):TC * (dt_ % 2) + TC],
                                  in_=d["xs0"][128 * dt_:128 * dt_ + 128, :]),
                    f"sWx{dt_ % 2}", 16)
    n.block.sync(sy)

    def v(ve):
        for dt_ in range(NDT):
            ve.wait_ge(ctx.sems[f"sWx{dt_ % 2}"], w0[dt_ % 2] + 16 * (dt_ // 2 + 1))
            sl = n.rstage[:, TC * (dt_ % 2):TC * (dt_ % 2) + TC]
            ctx.inc(ve.tensor_tensor(n.B(dt_), sl, n.A(dt_), ALU.add), "sVx")
    n.block.vector(v)

    def so(sq):
        for dt_ in range(NDT):
            ctx.wait(sq, "sVx", vx0 + dt_ + 1)
            ctx.inc(sq.dma_start(out=d["x1_out"][128 * dt_:128 * dt_ + 128, :],
                                 in_=n.B(dt_)), "sOx", 16)
    n.block.scalar(so)
    x1out_sOx = ctx.val("sOx")

    emit_lnstats(n, n.B)
    emit_apply_ln(n, n.B, n.B, wait_sOx=x1out_sOx)

    def so2(sq):
        ctx.wait(sq, "sV", n.xn_done_sV)
        for dt_ in range(NDT):
            ctx.inc(sq.dma_start(out=d["xn1_out"][128 * dt_:128 * dt_ + 128, :],
                                 in_=n.B(dt_)), "sOx", 16)
        ctx.wait(sq, "sOx")
    n.block.scalar(so2)


def emit_taps(n, taps, tap_out):
    if not taps:
        def fin(sq):
            for s_ in ["sV", "sT", "sS", "sO", "sOx", "sW", "sG", "sWt0",
                       "sWt1", "sWd0", "sWd1", "sG0", "sG1", "sWl0", "sWl1",
                       "sWx0", "sWx1"]:
                n.ctx.wait(sq, s_)
        n.block.scalar(fin)
        return
    ctx = n.ctx

    def f(sq):
        for s_ in ["sV", "sT", "sS", "sO", "sOx", "sW", "sG", "sVx", "sVq",
                   "sWt0", "sWt1", "sWd0", "sWd1", "sG0", "sG1", "sWl0",
                   "sWl1", "sWx0", "sWx1"]:
            ctx.wait(sq, s_)
        for tname, shape, dt in taps:
            src = resolve_tap(n, tname)
            ctx.inc(sq.dma_start(out=tap_out[tname], in_=src), "sO", 16)
        ctx.wait(sq, "sO")
    n.block.scalar(f)


def resolve_tap(n, tname):
    if tname.startswith("bufA"):
        d = int(tname.split("_")[1])
        return n.A(d)
    if tname.startswith("bufB"):
        d = int(tname.split("_")[1])
        return n.B(d)
    if tname in n.dram:
        return n.dram[tname]
    return getattr(n, tname)[:]



def _prep_launch1_inputs(inputs):
    t = _host_tables()
    x0 = np.asarray(inputs["L0_attn_u"]).reshape(T, D)

    lam_ri = np.asarray(inputs["L0_attn_Lambda_ri"])
    lam = np.concatenate([lam_ri[:, 0], lam_ri[:, 1]]).reshape(128, 1).astype(np.float32)
    vc = np.asarray(inputs["L0_attn_Vc_ri"])
    base = {
        "lam": lam,
        "vc_rT": np.ascontiguousarray(vc[:, :, 0].T.astype(np.float32)),
        "vc_iT": np.ascontiguousarray(vc[:, :, 1].T.astype(np.float32)),
        "bpv": np.asarray(inputs["L0_attn_Bp"]).reshape(64, 1).astype(np.float32),
        "ct_r": np.asarray(inputs["L0_attn_Ct_ri"]).reshape(64, 2)[:, 0:1].copy(),
        "ct_i": np.asarray(inputs["L0_attn_Ct_ri"]).reshape(64, 2)[:, 1:2].copy(),
        "p_r": np.asarray(inputs["L0_attn_p_ri"])[:, 0:1].copy(),
        "p_i": np.asarray(inputs["L0_attn_p_ri"])[:, 1:2].copy(),
        "q_r": np.asarray(inputs["L0_attn_q_ri"])[:, 0:1].copy(),
        "q_i": np.asarray(inputs["L0_attn_q_ri"])[:, 1:2].copy(),
        "dpar": np.asarray(inputs["L0_attn_D"]).reshape(1, 1).astype(np.float32),
        "logstep": np.asarray(inputs["L0_attn_log_step"]).reshape(1, 1).astype(np.float32),
        "b_base": np.asarray(inputs["L0_ffn_b_base"]).reshape(D, 1).astype(np.float32),
        "wrouT": np.ascontiguousarray(np.asarray(inputs["L0_ffn_W_router"]).T.astype(np.float32)),
        "aallT": np.ascontiguousarray(
            np.asarray(inputs["L0_ffn_A_stack"]).reshape(128, D).T.astype(np.float32)),
        "ball": np.ascontiguousarray(
            np.asarray(inputs["L0_ffn_B_stack"]).transpose(0, 2, 1).reshape(128, D).astype(np.float32)),
        "wbase_blk": pack_lhsT(np.ascontiguousarray(
            np.asarray(inputs["L0_ffn_W_base"]).T.astype(np.float32))),
        "t_r_row": t["t_r_row"], "t_i_row": t["t_i_row"],
        "cl_r": t["cl_r"], "cl_i": t["cl_i"], "cA": t["cA"], "sA": t["sA"],
        "csM": t["csM"], "trimask": t["trimask"],
        "id_bf": t["id_bf"], "ones_p1_bf": t["ones_p1_bf"],
        "ones_1p_f32": t["ones_1p_f32"], "E8": t["E8"],
    }
    base["b_base_pk"] = np.ascontiguousarray(base.pop("b_base").reshape(32, 128).T)
    base["wrou_pk"] = np.ascontiguousarray(
        base.pop("wrouT").reshape(32, 128, 8).transpose(1, 0, 2).reshape(128, 256))
    base["aab_pk"] = np.ascontiguousarray(
        base.pop("aallT").reshape(32, 128, 128).transpose(1, 0, 2).reshape(128, 4096))
    in_maps = []
    for c in range(NCORE):
        m = dict(base)
        m["x0c"] = np.ascontiguousarray(x0[_strip_tokens(c), :])
        in_maps.append(m)
    return in_maps






def _gcol(s):
    r = s if s <= 7 else 15 - s
    return 1024 * r + (0 if s <= 7 else 512)


def _ktile_list(s):
    out = []
    for sk in range(s + 1):
        base = _gcol(sk)
        for u in range(4):
            out.append((base + 128 * u, u if sk == s else None))
    return out


def build_launch2(taps=()):
    nc = bass.Bass("TRN2", target_bir_lowering=False)
    dram = {}
    din, dout, dint = _mk_dram_helpers(nc, dram)

    din("xn1g", [8 * D, TC], BF16)
    din("wq_blk", [4, 128, D], F32)
    din("wk_blk", [1, 128, D], F32)
    din("wv_blk", [1, 128, D], F32)
    din("bq", [512, 1], F32)
    din("bk", [128, 1], F32)
    din("bv", [128, 1], F32)
    din("ctabK", [128, T], BF16)
    din("stabK", [128, T], BF16)
    din("ctabQ", [128, T], BF16)
    din("stabQ", [128, T], BF16)
    din("Jmat", [128, 128], BF16)
    din("dmask", [4, 128, 512], BF16)
    din("id_bf", [128, 128], BF16)
    din("ones_p1_bf", [128, 1], BF16)
    din("ones_1p_f32", [1, 128], F32)
    dout("attnout", [512, T], BF16)

    tap_out = {}
    for tname, shape, dt in taps:
        tap_out[tname] = dout("tap_" + tname, list(shape), dt)

    with contextlib.ExitStack() as stack:
        n = types.SimpleNamespace()
        n.nc = nc
        n.dram = dram
        sb = lambda name, shape, dt: stack.enter_context(nc.sbuf_tensor(name, list(shape), dt))
        ps = lambda name, shape, dt: stack.enter_context(nc.psum_tensor(name, list(shape), dt))

        n.Kbuf = sb("Kbuf", [128, T], BF16)
        n.Vtok = sb("Vtok", [128, T], BF16)
        n.wqb = sb("wqb", [128, 4 * D], BF16)
        n.wkb = sb("wkb", [128, D], BF16)
        n.wvb = sb("wvb", [128, D], BF16)
        n.xbig = sb("xbig", [128, NDT * 512], BF16)
        n.tstg = sb("tstg", [128, 4 * 512], BF16)
        n.kst = sb("kst", [128, 512], F32)
        n.vst = sb("vst", [128, 512], BF16)
        n.qro = sb("qro", [128, 4 * 512], BF16)
        n.qtmp = sb("qtmp", [128, 512], F32)
        n.pbuf = sb("pbuf", [128, 4 * 512], BF16)
        n.pxtra = n.pbuf[:, 1536:2048]
        n.sqb = sb("sqb", [128, 512], BF16)
        n.sc2b = sb("sc2b", [1, 32], F32)
        n.rowz = sb("rowz", [1, 512], F32)
        n.sc2 = sb("sc2", [1, 16], F32)
        n.negb = sb("negb", [128, 16], F32)
        n.accden = sb("accden", [128, 512], F32)
        n.rrepA = sb("rrepA", [128, 512], F32)
        n.bqb = sb("bqb", [128, 4], F32)
        n.bkb = sb("bkb", [128, 1], F32)
        n.bvb = sb("bvb", [128, 1], F32)
        n.sb_J = sb("sb_J", [128, 128], BF16)
        n.sb_dm = sb("sb_dm", [128, 4 * 512], BF16)
        n.sb_id = sb("sb_id", [128, 128], BF16)
        n.sb_ones_p1 = sb("sb_ones_p1", [128, 1], BF16)
        n.sb_ones_1p = sb("sb_ones_1p", [1, 128], F32)

        n.psQ = ps("psQ", [128, 1024], F32)
        n.psS1 = ps("psS1", [128, 512], F32)
        n.psS2 = ps("psS2", [128, 512], F32)
        psFD = ps("psFD", [128, 1024], F32)
        n.psAV = psFD[:, 0:512]
        n.psD = psFD[:, 512:1024]
        n.psVT = ps("psVT", [128, 512], BF16)

        ctx = n.ctx = Ctx(nc, stack)
        for s_ in ["sW", "sG", "sO", "sS", "sV", "sT", "sX", "sKV", "sP",
                   "sGt0", "sGt1"]:
            ctx.sem(s_)
        n.PQ = PsRes(ctx, n.psQ)
        n.PS1 = PsRes(ctx, n.psS1)
        n.PS2 = PsRes(ctx, n.psS2)
        n.PAV = PsRes(ctx, n.psAV)
        n.PD = PsRes(ctx, n.psD)
        n.PVT = PsRes(ctx, n.psVT)
        n.fused = False

        with nc.Block() as block:
            n.block = block
            emit_l2_consts_sm(n)
            emit_l2_wload(n)
            emit_l2_kv(n)
            emit_l2_attn(n)
            emit_l2_taps(n, taps, tap_out)
    return nc, dram


def emit_l2_consts_sm(n, with_dm=True):
    ctx, d = n.ctx, n.dram

    def f(sy):
        consts = [(n.sb_J, d["Jmat"]),
                  (n.bkb, d["bk"]), (n.bvb, d["bv"])]
        if not n.fused:
            consts += [(n.sb_id, d["id_bf"]), (n.sb_ones_p1, d["ones_p1_bf"]),
                       (n.sb_ones_1p, d["ones_1p_f32"])]
        for dst, src in consts:
            ctx.inc(sy.dma_start(out=dst[:], in_=src), "sW", 16)
        if n.fused:
            ctx.inc(sy.dma_start(out=n.gidx[:], in_=d["gidx"]), "sW", 16)
        for i in range(4):
            if with_dm:
                ctx.inc(sy.dma_start(out=n.sb_dm[:, 512 * i:512 * i + 512],
                                     in_=d["dmask"][i]), "sW", 16)
            ctx.inc(sy.dma_start(out=n.bqb[:, i:i + 1],
                                 in_=d["bq"][128 * i:128 * i + 128, 0:1]), "sW", 16)
    n.block.sync(f)


def emit_l2_dm(n):
    ctx, d = n.ctx, n.dram

    def f(sy):
        ctx.wait(sy, "sV", n.conv_done_sV)
        for i in range(4):
            ctx.inc(sy.dma_start(out=n.sb_dm[:, 512 * i:512 * i + 512],
                                 in_=d["dmask"][i]), "sW", 16)
    n.block.sync(f)


def emit_l2_wload(n):
    ctx, d = n.ctx, n.dram

    def g(gq):
        if n.fused:
            ctx.wait(gq, "sVx")
            ctx.wait(gq, "sOx")
        ctx.inc(gq.dma_start(out=n.wkb[:], in_=d["wk_blk"][0]), "sG", 16)
        ctx.inc(gq.dma_start(out=n.wvb[:], in_=d["wv_blk"][0]), "sG", 16)
        for i in range(4):
            ctx.inc(gq.dma_start(out=n.wqb[:, D * i:D * i + D],
                                 in_=d["wq_blk"][i]), "sG", 16)
    n.block.gpsimd(g)
    n.c_sW = ctx.val("sW")
    n.c_sG = ctx.val("sG")


def emit_l2_kv(n):
    ctx, d = n.ctx, n.dram
    w0 = ctx.val("sW")
    gt0 = [ctx.val("sGt0"), ctx.val("sGt1")]
    tslot = lambda i: n.tstg[:, 512 * (i % 2):512 * (i % 2) + 512]
    tslot2 = lambda i: n.tstg[:, 1024 + 512 * (i % 2):1024 + 512 * (i % 2) + 512]
    xt = lambda dt_: n.xbig[:, 512 * dt_:512 * dt_ + 512]

    for tc in range(16):
        r, h2 = tc // 2, tc % 2

        def syx(sy_, tc=tc, r=r, h2=h2):
            if tc == 0:
                ctx.wait(sy_, "sW", n.c_sW)
                if n.fused:
                    ctx.wait(sy_, "sC1")
                    ctx.wait(sy_, "sOx")
                    ctx.wait(sy_, "sVx")
            else:
                ctx.wait(sy_, n.PS1.tname)
                ctx.wait(sy_, n.PS2.tname)
            for dt_ in range(NDT):
                ctx.inc(sy_.dma_start(
                    out=xt(dt_),
                    in_=d["xn1g"][D * r + 128 * dt_:D * r + 128 * dt_ + 128,
                                  512 * h2:512 * h2 + 512]), "sW", 16)
        n.block.sync(syx)

        def syt(sy_, tc=tc):
            if tc >= 2:
                ctx.wait(sy_, "sV")
            cs = slice(512 * tc, 512 * tc + 512)
            ctx.inc(sy_.dma_start(out=tslot(tc), in_=d["ctabK"][:, cs]),
                    f"sGt{tc % 2}", 16)
            ctx.inc(sy_.dma_start(out=tslot2(tc), in_=d["stabK"][:, cs]),
                    f"sGt{tc % 2}", 16)
        n.block.gpsimd(syt)

        def t(te, tc=tc):
            if tc == 0:
                ctx.wait(te, "sG", n.c_sG)
                if n.fused:
                    ctx.wait(te, n.PF.cname)
            te.wait_ge(ctx.sems["sW"], w0 + 16 * NDT * (tc + 1))
            n.PS1.prod_wait(te)
            n.PS2.prod_wait(te)
            for dt_ in range(NDT):
                first, last = dt_ == 0, dt_ == NDT - 1
                wk = n.wkb[:, 128 * dt_:128 * dt_ + 128]
                wv = n.wvb[:, 128 * dt_:128 * dt_ + 128]
                mmk = te.matmul(n.psS1[:, 0:512], wk, xt(dt_), start=first, stop=last)
                mmv = te.matmul(n.psS2[:, 0:512], wv, xt(dt_), start=first, stop=last)
            n.PS1.prod_done(mmk)
            n.PS2.prod_done(mmv)
        n.block.tensor(t)

        def v1(ve, tc=tc):
            n.PS1.cons_wait(ve)
            if tc > 0:
                ctx.wait(ve, n.PVT.tname)
                ctx.wait(ve, n.PAV.tname)
            n.PS1.cons_done(
                ve.tensor_scalar(n.kst[:], n.psS1[:, 0:512], n.bkb[:], None, ALU.add))
            n.PS2.cons_wait(ve)
            n.PS2.cons_done(
                ve.tensor_scalar(n.vst[:], n.psS2[:, 0:512], n.bvb[:], None, ALU.add))
            ctx.inc(ve.tensor_copy(n.pbuf[:, 0:512], n.kst[:]), "sV")
        n.block.vector(v1)

        def t2(te, tc=tc):
            ctx.wait(te, "sV")
            n.PAV.prod_wait(te)
            n.PAV.prod_done(
                te.matmul(n.psAV[:, 0:512], n.sb_J[:], n.pbuf[:, 0:512],
                          start=True, stop=True))
        n.block.tensor(t2)

        def v2(ve, tc=tc):
            n.PAV.cons_wait(ve)
            ve.wait_ge(ctx.sems[f"sGt{tc % 2}"], gt0[tc % 2] + 32 * (tc // 2 + 1))
            if tc > 0:
                ctx.wait(ve, n.PD.tname)
            cs = slice(512 * tc, 512 * tc + 512)
            ve.tensor_mul(n.qtmp[:], n.kst[:], tslot(tc))
            n.PAV.cons_done(
                ve.tensor_mul(n.kst[:], n.psAV[:, 0:512], tslot2(tc)))
            ctx.inc(ve.tensor_add(n.Kbuf[:, cs], n.qtmp[:], n.kst[:]), "sV")
            ctx.inc(ve.tensor_mul(n.sqb[:], n.Kbuf[:, cs], n.Kbuf[:, cs]), "sV")
        n.block.vector(v2)

        def t3(te, tc=tc):
            ctx.wait(te, "sV")
            n.PD.prod_wait(te)
            n.PD.prod_done(
                te.matmul(n.psD[0:1, 0:512], n.sb_ones_p1[:], n.sqb[:],
                          start=True, stop=True))
            n.PVT.prod_wait(te)
            for i in range(4):
                mm = te.transpose(n.psVT[:, 128 * i:128 * i + 128],
                                  n.vst[:, 128 * i:128 * i + 128], n.sb_id[:])
            n.PVT.prod_done(mm)
        n.block.tensor(t3)

        def v3(ve, tc=tc):
            n.PD.cons_wait(ve)
            n.PD.cons_done(ve.tensor_copy(n.rowz[:], n.psD[0:1, 0:512]))
            ve.reduce_max(n.sc2b[0:1, tc:tc + 1], n.rowz[:], axis=AX.X)
            n.PVT.cons_wait(ve)
            for i in range(4):
                kt = 4 * tc + i
                last = ve.tensor_copy(n.Vtok[:, 128 * kt:128 * kt + 128],
                                      n.psVT[:, 128 * i:128 * i + 128])
            n.PVT.cons_done(last)
        n.block.vector(v3)

    def v4(ve):
        ve.tensor_copy(n.rrepA[:], n.rrepA[:])
        ve.reduce_max(n.sc2[0:1, 0:1], n.sc2b[0:1, 0:16], axis=AX.X)
        ctx.inc(ve.tensor_copy(n.rrepA[:], n.rrepA[:]), "sV")
    n.block.vector(v4)

    def s1(s):
        ctx.wait(s, "sV")
        s.activation(n.sc2[0:1, 2:4], n.sc2[0:1, 0:2], AF.Sqrt)
        ctx.inc(s.copy(n.sspacer[:], n.sspacer[:]), "sS")
    n.block.scalar(s1)
    n.kmax_sS = ctx.val("sS")


def emit_l2_attn(n):
    ctx = n.ctx
    n.onesf = n.negb[:, 8:9]

    def vinit(ve):
        ctx.inc(ve.memset(n.onesf[:], 1.0), "sV")
    n.block.vector(vinit)
    for s in range(16):
        emit_l2_strip(n, s)


def emit_l2_strip(n, s):
    ctx, d = n.ctx, n.dram
    gc = _gcol(s)
    kts = _ktile_list(s)
    rblk = s if s <= 7 else 15 - s
    coff = 0 if s <= 7 else 512

    w0 = ctx.val("sW")
    xt = lambda dt_: n.xbig[:, 512 * dt_:512 * dt_ + 512]

    def syq(sy_):
        ctx.wait(sy_, n.PS1.tname)
        ctx.wait(sy_, n.PS2.tname)
        ctx.wait(sy_, n.PQ.tname)
        for dt_ in range(NDT):
            ctx.inc(sy_.dma_start(
                out=xt(dt_),
                in_=d["xn1g"][D * rblk + 128 * dt_:D * rblk + 128 * dt_ + 128,
                              coff:coff + 512]), "sW", 16)
    n.block.sync(syq)

    def syt(sy_):
        ctx.wait(sy_, "sV")
        cs = slice(gc, gc + 512)
        ctx.inc(sy_.dma_start(out=n.tstg[:, 0:512], in_=d["ctabQ"][:, cs]), "sG", 16)
        ctx.inc(sy_.dma_start(out=n.tstg[:, 512:1024], in_=d["stabQ"][:, cs]), "sG", 16)
    n.block.gpsimd(syt)

    for pas in range(2):
        def tq(te, pas=pas):
            if pas == 0:
                te.wait_ge(ctx.sems["sW"], w0 + 16 * NDT)
            n.PQ.prod_wait(te)
            for dt_ in range(NDT):
                first, last = dt_ == 0, dt_ == NDT - 1
                for qi in range(2):
                    qh = 2 * pas + qi
                    mm = te.matmul(n.psQ[:, 512 * qi:512 * qi + 512],
                                   n.wqb[:, D * qh + 128 * dt_:D * qh + 128 * dt_ + 128],
                                   xt(dt_), start=first, stop=last)
            n.PQ.prod_done(mm)
        n.block.tensor(tq)

        def vq(ve, pas=pas):
            n.PQ.cons_wait(ve)
            if pas == 0:
                ctx.wait(ve, "sG")
                ctx.wait(ve, "sO")
            for qi in range(2):
                qh = 2 * pas + qi
                ve.tensor_scalar(n.qtmp[:], n.psQ[:, 512 * qi:512 * qi + 512],
                                 n.bqb[:, qh:qh + 1], None, ALU.add)
                last = ve.tensor_copy(n.qro[:, 512 * qh:512 * qh + 512], n.qtmp[:])
            n.PQ.cons_done(last)
            ctx.inc(ve.tensor_copy(n.rrepA[:], n.rrepA[:]), "sV")
        n.block.vector(vq)

    for qh in range(4):
        def tj(te, qh=qh):
            ctx.wait(te, "sV")
            n.PS1.prod_wait(te)
            n.PS1.prod_done(
                te.matmul(n.psS1[:, 0:512], n.sb_J[:],
                          n.qro[:, 512 * qh:512 * qh + 512], start=True, stop=True))
        n.block.tensor(tj)

        def vj(ve, qh=qh):
            n.PS1.cons_wait(ve)
            sl = n.qro[:, 512 * qh:512 * qh + 512]
            ve.tensor_mul(n.qtmp[:], sl, n.tstg[:, 0:512])
            n.PS1.cons_done(ve.tensor_mul(n.kst[:], n.psS1[:, 0:512],
                                          n.tstg[:, 512:1024]))
            ve.tensor_add(sl, n.qtmp[:], n.kst[:])
            ctx.inc(ve.tensor_mul(n.sqb[:], sl, sl), "sV")
        n.block.vector(vj)

        def tb(te, qh=qh):
            ctx.wait(te, "sV")
            n.PD.prod_wait(te)
            n.PD.prod_done(
                te.matmul(n.psD[0:1, 0:512], n.sb_ones_p1[:], n.sqb[:],
                          start=True, stop=True))
        n.block.tensor(tb)

        def vb(ve, qh=qh):
            n.PD.cons_wait(ve)
            n.PD.cons_done(ve.tensor_copy(n.rowz[:], n.psD[0:1, 0:512]))
            ve.reduce_max(n.sc2[0:1, 8 + qh:9 + qh], n.rowz[:], axis=AX.X)
            ctx.inc(ve.tensor_copy(n.rrepA[:], n.rrepA[:]), "sV")
        n.block.vector(vb)

    def sball(sc):
        ctx.wait(sc, "sV")
        if s == 0:
            ctx.wait(sc, "sS", n.kmax_sS)
        sc.activation(n.sc2[0:1, 12:16], n.sc2[0:1, 8:12], AF.Sqrt)
        ctx.inc(sc.copy(n.sspacer[:], n.sspacer[:]), "sS")
    n.block.scalar(sball)

    def vball(ve):
        ctx.wait(ve, "sS")
        ve.tensor_scalar(n.sc2b[0:1, 24:28], n.sc2[0:1, 12:16],
                         n.sc2[0:1, 2:3], -1.0, ALU.mult, ALU.mult)
        ctx.inc(ve.tensor_copy(n.rrepA[:], n.rrepA[:]), "sV")
    n.block.vector(vball)

    def tneg(te):
        ctx.wait(te, "sV")
        n.PD.prod_wait(te)
        n.PD.prod_done(
            te.matmul(n.psD[:, 0:4], n.sb_ones_1p[:], n.sc2b[0:1, 24:28],
                      start=True, stop=True))
    n.block.tensor(tneg)

    def vneg(ve):
        n.PD.cons_wait(ve)
        n.PD.cons_done(ve.tensor_copy(n.negb[:, 0:4], n.psD[:, 0:4]))
        ctx.inc(ve.tensor_copy(n.rrepA[:], n.rrepA[:]), "sV")
    n.block.vector(vneg)
    n.negb_sV = ctx.val("sV")

    prev_drain = None
    for qh in range(4):
        tiles, drain = emit_l2_kloop(n, s, qh, kts)
        if prev_drain is not None:
            prev_drain()
        tiles()
        prev_drain = drain
    prev_drain()


def emit_l2_kloop(n, s, qh, kts):
    ctx, d = n.ctx, n.dram
    nk = len(kts)
    pslots = 4
    look = 3
    p0 = ctx.val("sP")
    ca0 = ctx.val(n.PS1.cname)
    cb0 = ctx.val(n.PS2.cname)
    ta0 = ctx.val(n.PS1.tname)
    tb0 = ctx.val(n.PS2.tname)
    pe_t0 = ctx.val(n.PE.tname)
    pe_c0 = ctx.val(n.PE.cname)
    sS0 = ctx.val("sS")
    slots = [n.pbuf[:, 0:512], n.pbuf[:, 512:1024], n.pbuf[:, 1024:1536],
             n.pxtra[:, 0:512]]
    pslot = lambda i: slots[i % pslots]
    banks = [n.psS1[:, 0:512], n.psS2[:, 0:512],
             n.psE[:, 0:512], n.psE[:, 512:1024]]
    bank = lambda i: banks[i % 4]
    prod_names = (n.PS1.tname, n.PS2.tname, n.PE.tname, n.PE.tname)
    cons_names = (n.PS1.cname, n.PS2.cname, n.PE.cname, n.PE.cname)
    cons_base = (ca0, cb0, pe_c0, pe_c0)

    def prod_val(i):
        b, pos = i % 4, i // 4
        return (ta0 + pos + 1, tb0 + pos + 1,
                pe_t0 + 2 * pos + 1, pe_t0 + 2 * pos + 2)[b]

    def cons_val(i):
        b, pos = i % 4, i // 4
        return (ca0 + pos + 1, cb0 + pos + 1,
                pe_c0 + 2 * pos + 1, pe_c0 + 2 * pos + 2)[b]

    def tS(te, j):
        kc = kts[j][0]
        if j < 4:
            te.wait_ge(ctx.sems[cons_names[j % 4]], cons_base[j % 4])
        else:
            te.wait_ge(ctx.sems[cons_names[j % 4]], cons_val(j - 4))
        ctx.inc(te.matmul(bank(j), n.Kbuf[:, kc:kc + 128],
                          n.qro[:, 512 * qh:512 * qh + 512],
                          start=True, stop=True), prod_names[j % 4])

    def tpre(te):
        ctx.wait(te, "sV", n.negb_sV)
        for j in range(min(look, nk)):
            tS(te, j)
    n.block.tensor(tpre)

    def sE_all(sc):
        for i in range(nk):
            sc.wait_ge(ctx.sems[prod_names[i % 4]], prod_val(i))
            if i >= pslots:
                sc.wait_ge(ctx.sems["sP"], p0 + (i - pslots + 1))
            ctx.inc(sc.activation(pslot(i), bank(i), AF.Exp,
                                  bias=n.negb[:, qh:qh + 1]), cons_names[i % 4])
        ctx.inc(sc.copy(n.sspacer[:], n.sspacer[:]), "sS")
    n.block.scalar(sE_all)

    def expland(eng, i):
        j = i + 1
        if j < nk:
            eng.wait_ge(ctx.sems[cons_names[j % 4]], cons_val(j))
        else:
            eng.wait_ge(ctx.sems["sS"], sS0 + 1)

    def emit_tiles():
        for i, (kc, dm) in enumerate(kts):
            def vAcc(ve, i=i, dm=dm):
                expland(ve, i)
                sl = pslot(i)
                if dm is not None:
                    ve.tensor_mul(sl, sl, n.sb_dm[:, 512 * dm:512 * dm + 512])
                if i == 0:
                    acc_i = ve.tensor_copy(n.accden[:], sl)
                else:
                    acc_i = ve.tensor_tensor(n.accden[:], n.accden[:], sl,
                                             ALU.add)
                ctx.inc(acc_i, "sP")
            n.block.vector(vAcc)

            def tMain(te, i=i, kc=kc, dm=dm):
                if dm is not None:
                    te.wait_ge(ctx.sems["sP"], p0 + i + 1)
                else:
                    expland(te, i)
                first, last = i == 0, i == nk - 1
                sl = pslot(i)
                if first:
                    n.PAV.prod_wait(te)
                mm = te.matmul(n.psAV[:, 0:512], n.Vtok[:, kc:kc + 128], sl,
                               start=first, stop=last)
                if last:
                    n.PAV.prod_done(mm)
                if i + look < nk:
                    tS(te, i + look)
            n.block.tensor(tMain)

    def emit_drain():
        def tD(te):
            te.wait_ge(ctx.sems["sP"], p0 + nk)
            n.PD.prod_wait(te)
            n.PD.prod_done(te.matmul(n.psD[0:1, 0:512], n.onesf[:],
                                     n.accden[:], start=True, stop=True))
        n.block.tensor(tD)

        def vE(ve):
            n.PD.cons_wait(ve)
            n.PD.cons_done(ve.tensor_copy(n.rowz[:], n.psD[0:1, 0:512]))
            ve.reciprocal(n.rowz[:], n.rowz[:])
            ctx.inc(ve.tensor_copy(n.rrepA[:], n.rrepA[:]), "sV")
        n.block.vector(vE)

        def tR(te):
            ctx.wait(te, "sV")
            n.PD.prod_wait(te)
            n.PD.prod_done(te.matmul(n.psD[:, 0:512], n.sb_ones_1p[:],
                                     n.rowz[:], start=True, stop=True))
        n.block.tensor(tR)

        def vF(ve):
            n.PD.cons_wait(ve)
            n.PD.cons_done(ve.tensor_copy(n.rrepA[:], n.psD[:, 0:512]))
            n.PAV.cons_wait(ve)
            n.PAV.cons_done(
                ve.tensor_mul(n.sqb[:], n.psAV[:, 0:512], n.rrepA[:]))
            ctx.inc(ve.tensor_copy(n.qtmp[:, 0:512], n.qtmp[:, 0:512]), "sV")
        n.block.vector(vF)

        if not n.fused:
            def so(sq):
                ctx.wait(sq, "sV")
                gcs = _gcol(s)
                ctx.inc(sq.dma_start(
                    out=n.dram["attnout"][128 * qh:128 * qh + 128,
                                          gcs:gcs + 512],
                    in_=n.sqb[:]), "sO", 16)
            n.block.scalar(so)
        else:
            def tT(te):
                ctx.wait(te, "sV")
                n.PT1.prod_wait(te)
                for i in range(4):
                    mm = te.transpose(n.psT1[:, 128 * i:128 * i + 128],
                                      n.sqb[:, 128 * i:128 * i + 128],
                                      n.sb_id[:])
                n.PT1.prod_done(mm)
            n.block.tensor(tT)

            def vT(ve):
                n.PT1.cons_wait(ve)
                ctx.wait(ve, "sO")
                n.PT1.cons_done(ve.tensor_copy(n.tstage[:], n.psT1[:, 0:512]))
                ctx.inc(ve.tensor_copy(n.rrepA[:], n.rrepA[:]), "sV")
            n.block.vector(vT)

            def soT(sq):
                ctx.wait(sq, "sV")
                gcs = _gcol(s)
                for i in range(4):
                    ctx.inc(sq.dma_start(
                        out=n.dram["cc2_in"][gcs + 128 * i:gcs + 128 * i + 128,
                                             128 * qh:128 * qh + 128],
                        in_=n.tstage[:, 128 * i:128 * i + 128]), "sO", 16)
                    if "attdbg" in n.dram:
                        ctx.inc(sq.dma_start(
                            out=n.dram["attdbg"][gcs + 128 * i:gcs + 128 * i + 128,
                                                 128 * qh:128 * qh + 128],
                            in_=n.tstage[:, 128 * i:128 * i + 128]), "sO", 16)
            n.block.scalar(soT)

    return emit_tiles, emit_drain


def emit_l2_taps(n, taps, tap_out):
    ctx = n.ctx

    def f(sq):
        for s_ in ["sV", "sS", "sO", "sW", "sG", "sGt0", "sGt1"]:
            ctx.wait(sq, s_)
        for tname, shape, dt in taps:
            src = resolve_tap(n, tname)
            ctx.inc(sq.dma_start(out=tap_out[tname], in_=src), "sO", 16)
        ctx.wait(sq, "sO")
    n.block.scalar(f)



def build_launch3(taps=()):
    nc = bass.Bass("TRN2", target_bir_lowering=False)
    dram = {}
    din, dout, dint = _mk_dram_helpers(nc, dram)

    din("x1c", [D, TC], BF16)
    din("xn1c", [D, TC], BF16)
    din("attTc", [D, TC], BF16)
    din("w12_blk", [64, 128, D], F32)
    din("b12_pk", [128, 64], F32)
    din("w3_blk", [NDT, 128, D], F32)
    din("b3_pk", [128, 32], F32)
    din("wo_blk", [NDT, 128, D], F32)
    din("bo_pk", [128, 32], F32)
    din("id_bf", [128, 128], BF16)
    dout("y", [TC, D], F32)

    tap_out = {}
    for tname, shape, dt in taps:
        tap_out[tname] = dout("tap_" + tname, list(shape), dt)

    with contextlib.ExitStack() as stack:
        n = types.SimpleNamespace()
        n.nc = nc
        n.dram = dram
        sb = lambda name, shape, dt: stack.enter_context(nc.sbuf_tensor(name, list(shape), dt))
        ps = lambda name, shape, dt: stack.enter_context(nc.psum_tensor(name, list(shape), dt))

        n.bufA = sb("bufA", [128, NDT * TC], BF16)
        n.bufB = sb("bufB", [128, NDT * TC], BF16)
        n.A = lambda d_: n.bufA[:, d_ * TC:(d_ + 1) * TC]
        n.B = lambda d_: n.bufB[:, d_ * TC:(d_ + 1) * TC]
        n.wbuf = sb("wbuf", [128, 4 * D], BF16)
        n.silu = sb("silu", [128, TC], BF16)
        n.rstage = sb("rstage", [128, 2 * TC], BF16)
        n.ostage = sb("ostage", [128, 2 * D], F32)
        n.b12b = sb("b12b", [128, 64], F32)
        n.b3b = sb("b3b", [128, NDT], F32)
        n.bob = sb("bob", [128, NDT], F32)
        n.sb_id = sb("sb_id", [128, 128], BF16)
        n.sspacer = n.b12b[0:1, 0:64]

        n.psE = ps("psE", [128, 1024], F32)
        n.psF = ps("psF", [128, 1024], F32)
        n.psT1 = ps("psT1", [128, 1024], BF16)
        n.psT2 = ps("psT2", [128, 1024], BF16)

        ctx = n.ctx = Ctx(nc, stack)
        for s_ in ["sW", "sG", "sO", "sS", "sV", "sT", "sWl", "sVx",
                   "sWl0", "sWl1", "sWx0", "sWx1", "sGg0", "sGg1"]:
            ctx.sem(s_)
        n.PE = PsRes(ctx, n.psE)
        n.PF = PsRes(ctx, n.psF)
        n.PT1 = PsRes(ctx, n.psT1)
        n.PT2 = PsRes(ctx, n.psT2)

        n.fused = False
        with nc.Block() as block:
            n.block = block
            emit_l3_consts(n)
            emit_l3(n, taps, tap_out)
    return nc, dram


def emit_l3_consts(n):
    ctx, d = n.ctx, n.dram

    def consts(sy):
        if "id_bf" in d and not n.fused:
            ctx.inc(sy.dma_start(out=n.sb_id[:], in_=d["id_bf"]), "sW", 16)
        ctx.inc(sy.dma_start(out=n.b12b[:], in_=d["b12_pk"]), "sW", 16)
        ctx.inc(sy.dma_start(out=n.b3b[:], in_=d["b3_pk"]), "sW", 16)
        ctx.inc(sy.dma_start(out=n.bob[:], in_=d["bo_pk"]), "sW", 16)
    n.block.sync(consts)


def emit_l3(n, taps, tap_out, l2done=None):
    ctx, d = n.ctx, n.dram
    paranoid = getattr(n, "paranoid", set()) or set()

    def loadxn(sy):
        if n.fused:
            for nm in (n.PS1.tname, n.PS2.tname, n.PE.tname, n.PAV.tname):
                ctx.wait(sy, nm, l2done[nm])
        for dt_ in range(NDT):
            ctx.inc(sy.dma_start(out=n.A(dt_),
                                 in_=d["xn1c"][128 * dt_:128 * dt_ + 128, :]), "sW", 16)
    n.block.sync(loadxn)
    n.c_sW = ctx.val("sW")
    if "l3a" in paranoid:
        emit_barrier(n, "l3a")

    wl0 = getattr(n, "wl0_pre", None) if n.fused else None
    if wl0 is None:
        wl0 = [ctx.val("sWl0"), ctx.val("sWl1")]
    f10 = ctx.val(n.PF.tname)
    wslot = lambda mc: n.wbuf[:, 2 * D * (mc % 2):2 * D * (mc % 2) + 2 * D]

    def g1_part(mcs):
        def g1(gq):
            for mc in mcs:
                if mc == 0 and n.fused:
                    ctx.wait(gq, "sP", l2done["sP"])
                if mc >= 2:
                    gq.wait_ge(ctx.sems[n.PF.tname], f10 + mc - 1)
                ctx.inc(gq.dma_start(out=wslot(mc)[:, 0:D],
                                     in_=d["w12_blk"][mc]),
                        f"sWl{mc % 2}", 16)
                ctx.inc(gq.dma_start(out=wslot(mc)[:, D:2 * D],
                                     in_=d["w12_blk"][32 + mc]),
                        f"sWl{mc % 2}", 16)
        return g1
    if n.fused:
        n.block.gpsimd(g1_part(range(2, NDT)))
    else:
        n.block.gpsimd(g1_part(range(NDT)))

    for mc in range(NDT):
        def t1(te, mc=mc):
            if mc == 0:
                ctx.wait(te, "sW", n.c_sW)
                if n.fused:
                    ctx.wait(te, n.PAV.cname, l2done[n.PAV.cname])
                    ctx.wait(te, n.PD.cname, l2done[n.PD.cname])
            te.wait_ge(ctx.sems[f"sWl{mc % 2}"], wl0[mc % 2] + 32 * (mc // 2 + 1))
            n.PE.prod_wait(te)
            n.PF.prod_wait(te)
            ws = wslot(mc)
            for dt_ in range(NDT):
                first, last = dt_ == 0, dt_ == NDT - 1
                w1 = ws[:, 128 * dt_:128 * dt_ + 128]
                w2 = ws[:, D + 128 * dt_:D + 128 * dt_ + 128]
                mm1a = te.matmul(n.psE[:, 0:512], w1, n.A(dt_)[:, 0:512],
                                 start=first, stop=last)
                mm1b = te.matmul(n.psE[:, 512:1024], w1, n.A(dt_)[:, 512:1024],
                                 start=first, stop=last)
                mm2a = te.matmul(n.psF[:, 0:512], w2, n.A(dt_)[:, 0:512],
                                 start=first, stop=last)
                mm2b = te.matmul(n.psF[:, 512:1024], w2, n.A(dt_)[:, 512:1024],
                                 start=first, stop=last)
            n.PE.prod_done(mm1b)
            n.PF.prod_done(mm2b)
        n.block.tensor(t1)

        def s1(sc, mc=mc):
            n.PE.cons_wait(sc)
            sc.activation(n.silu[:], n.psE[:], AF.Silu, bias=n.b12b[:, mc:mc + 1])
            n.PE.cons_done(sc.copy(n.sspacer[:], n.sspacer[:]))
        n.block.scalar(s1)

        def v1(ve, mc=mc):
            ctx.wait(ve, n.PE.cname)
            n.PF.cons_wait(ve)
            n.PF.cons_done(
                ve.scalar_tensor_tensor(n.B(mc), n.psF[:], n.b12b[:, 32 + mc:33 + mc],
                                        n.silu[:], ALU.add, ALU.mult))
        n.block.vector(v1)
    if "l3ffn" in paranoid:
        emit_barrier(n, "l3ffn")

    e20 = ctx.val(n.PE.tname)
    f20 = ctx.val(n.PF.tname)
    wl3 = [ctx.val("sWl0"), ctx.val("sWl1")]
    w3slot = lambda oc: n.wbuf[:, 2 * D * (oc % 2):2 * D * (oc % 2) + D]

    def g2(gq):
        for oc in range(NDT):
            if oc < 2:
                gq.wait_ge(ctx.sems[n.PE.tname], e20)
                gq.wait_ge(ctx.sems[n.PF.tname], f20)
            elif oc % 2 == 0:
                gq.wait_ge(ctx.sems[n.PE.tname], e20 + oc // 2)
            else:
                gq.wait_ge(ctx.sems[n.PF.tname], f20 + (oc - 1) // 2)
            ctx.inc(gq.dma_start(out=w3slot(oc), in_=d["w3_blk"][oc]),
                    f"sWl{oc % 2}", 16)
    n.block.gpsimd(g2)

    for oc in range(NDT):
        res = n.PE if oc % 2 == 0 else n.PF

        def t2(te, oc=oc, res=res):
            te.wait_ge(ctx.sems[f"sWl{oc % 2}"], wl3[oc % 2] + 16 * (oc // 2 + 1))
            res.prod_wait(te)
            ws = w3slot(oc)
            for mc in range(NDT):
                first, last = mc == 0, mc == NDT - 1
                mma = te.matmul(res.ap[:, 0:512], ws[:, 128 * mc:128 * mc + 128],
                                n.B(mc)[:, 0:512], start=first, stop=last)
                mmb = te.matmul(res.ap[:, 512:1024], ws[:, 128 * mc:128 * mc + 128],
                                n.B(mc)[:, 512:1024], start=first, stop=last)
            res.prod_done(mmb)
        n.block.tensor(t2)

        def v2(ve, oc=oc, res=res):
            res.cons_wait(ve)
            res.cons_done(ve.tensor_scalar(n.A(oc), res.ap[:], n.b3b[:, oc:oc + 1],
                                           None, ALU.add))
        n.block.vector(v2)
    acc_sV = ctx.val(n.PE.cname) + ctx.val(n.PF.cname)
    if "l3w3" in paranoid:
        emit_barrier(n, "l3w3")

    t2e = ctx.val(n.PE.tname)
    t2f = ctx.val(n.PF.tname)

    if not n.fused:
        def syA(sy_):
            sy_.wait_ge(ctx.sems[n.PE.tname], t2e)
            sy_.wait_ge(ctx.sems[n.PF.tname], t2f)
            for dt_ in range(NDT):
                ctx.inc(sy_.dma_start(out=n.B(dt_),
                                      in_=d["attTc"][128 * dt_:128 * dt_ + 128, :]),
                        "sW", 16)
        n.block.sync(syA)
        attT_sW = ctx.val("sW")
    else:
        def gg(gq):
            ctx.wait(gq, "sC2")
            gq.wait_ge(ctx.sems[n.PE.tname], t2e)
            gq.wait_ge(ctx.sems[n.PF.tname], t2f)
            for rr in range(NCORE):
                if rr >= 2:
                    ctx.wait(gq, "sTgg", rr - 1)
                ctx.inc(gq.indirect_dma_start(
                    n.gslot(rr)[:, 0:4096], None,
                    d["cc2_out"],
                    bass.IndirectOffsetOnAxis(n.gidx[:, rr:rr + 1], 0)),
                    f"sGg{rr % 2}", 16)
        n.block.gpsimd(gg)
        gg0 = [ctx.val("sGg0") - 16 * (NCORE // 2), ctx.val("sGg1") - 16 * (NCORE // 2)]

        for rr in range(NCORE):
            res = n.PT1 if rr % 2 == 0 else n.PT2
            for j in range(8):
                def tg(te, rr=rr, j=j, res=res):
                    if j == 0:
                        te.wait_ge(ctx.sems[f"sGg{rr % 2}"],
                                   gg0[rr % 2] + 16 * (rr // 2 + 1))
                    res.prod_wait(te)
                    for k in range(4):
                        mm = te.transpose(
                            res.ap[:, 128 * k:128 * k + 128],
                            n.gslot(rr)[:, 512 * j + 128 * k:512 * j + 128 * k + 128],
                            n.sb_id[:])
                    res.prod_done(mm)
                n.block.tensor(tg)

                def vg(ve, rr=rr, j=j, res=res):
                    res.cons_wait(ve)
                    for k in range(4):
                        dt_ = 4 * rr + k
                        last = ve.tensor_copy(
                            n.bufB[:, dt_ * TC + j:dt_ * TC + j + 1017:8],
                            res.ap[:, 128 * k:128 * k + 128])
                    res.cons_done(last)
                    if j == 7:
                        ctx.inc(ve.tensor_copy(n.rstage[:, 0:512],
                                               n.rstage[:, 0:512]), "sTgg")
                n.block.vector(vg)
        attT_sW = ctx.val("sW")
    if "l3gather" in paranoid:
        emit_barrier(n, "l3gather")

    e30 = ctx.val(n.PE.tname)
    f30 = ctx.val(n.PF.tname)
    wl4 = [ctx.val("sWl0"), ctx.val("sWl1")]

    def g3(gq):
        for oc in range(NDT):
            if oc < 2:
                gq.wait_ge(ctx.sems[n.PE.tname], e30)
                gq.wait_ge(ctx.sems[n.PF.tname], f30)
            elif oc % 2 == 0:
                gq.wait_ge(ctx.sems[n.PE.tname], e30 + oc // 2)
            else:
                gq.wait_ge(ctx.sems[n.PF.tname], f30 + (oc - 1) // 2)
            ctx.inc(gq.dma_start(out=w3slot(oc), in_=d["wo_blk"][oc]),
                    f"sWl{oc % 2}", 16)
    n.block.gpsimd(g3)

    for oc in range(NDT):
        res = n.PE if oc % 2 == 0 else n.PF

        def t3(te, oc=oc, res=res):
            if oc == 0:
                if n.fused:
                    ctx.wait(te, "sTgg")
                else:
                    te.wait_ge(ctx.sems["sW"], attT_sW)
            te.wait_ge(ctx.sems[f"sWl{oc % 2}"], wl4[oc % 2] + 16 * (oc // 2 + 1))
            res.prod_wait(te)
            ws = w3slot(oc)
            for mc in range(NDT):
                first, last = mc == 0, mc == NDT - 1
                mma = te.matmul(res.ap[:, 0:512], ws[:, 128 * mc:128 * mc + 128],
                                n.B(mc)[:, 0:512], start=first, stop=last)
                mmb = te.matmul(res.ap[:, 512:1024], ws[:, 128 * mc:128 * mc + 128],
                                n.B(mc)[:, 512:1024], start=first, stop=last)
            res.prod_done(mmb)
        n.block.tensor(t3)

        def v3(ve, oc=oc, res=res):
            res.cons_wait(ve)
            ve.tensor_scalar(n.rstage[:, 0:TC], res.ap[:], n.bob[:, oc:oc + 1],
                             None, ALU.add)
            res.cons_done(ve.tensor_add(n.A(oc), n.A(oc), n.rstage[:, 0:TC]))
        n.block.vector(v3)
    wo_done_sV = ctx.val(n.PE.cname)
    if "l3wo" in paranoid:
        emit_barrier(n, "l3wo")

    w5 = [ctx.val("sWx0"), ctx.val("sWx1")]
    vx0 = ctx.val("sVx")

    def syx(sy_):
        ctx.wait(sy_, n.PE.cname)
        ctx.wait(sy_, n.PF.cname)
        for dt_ in range(NDT):
            if dt_ >= 2:
                ctx.wait(sy_, "sVx", vx0 + dt_ - 1)
            ctx.inc(sy_.dma_start(out=n.rstage[:, TC * (dt_ % 2):TC * (dt_ % 2) + TC],
                                  in_=d["x1c"][128 * dt_:128 * dt_ + 128, :]),
                    f"sWx{dt_ % 2}", 16)
    n.block.sync(syx)

    def vx(ve):
        for dt_ in range(NDT):
            ve.wait_ge(ctx.sems[f"sWx{dt_ % 2}"], w5[dt_ % 2] + 16 * (dt_ // 2 + 1))
            sl = n.rstage[:, TC * (dt_ % 2):TC * (dt_ % 2) + TC]
            ctx.inc(ve.tensor_add(n.A(dt_), n.A(dt_), sl), "sVx")
    n.block.vector(vx)
    x2_sVx = ctx.val("sVx")

    for tcb in range(NTT):
        for grp in range(4):
            res = n.PT1 if grp % 2 == 0 else n.PT2

            def tt(te, tcb=tcb, grp=grp, res=res):
                if tcb == 0 and grp == 0:
                    ctx.wait(te, "sVx", x2_sVx)
                res.prod_wait(te)
                for k in range(8):
                    dt_ = grp * 8 + k
                    mm = te.transpose(res.ap[:, 128 * k:128 * k + 128],
                                      n.A(dt_)[:, 128 * tcb:128 * tcb + 128],
                                      n.sb_id[:])
                res.prod_done(mm)
            n.block.tensor(tt)

            def tv(ve, tcb=tcb, grp=grp, res=res):
                res.cons_wait(ve)
                if grp == 0:
                    ctx.wait(ve, "sO")
                for k in range(8):
                    dt_ = grp * 8 + k
                    last = ve.tensor_copy(
                        n.ostage[:, D * (tcb % 2) + 128 * (grp * 8 + k):
                                 D * (tcb % 2) + 128 * (grp * 8 + k) + 128],
                        res.ap[:, 128 * k:128 * k + 128])
                res.cons_done(last)
                if grp == 3:
                    ctx.inc(ve.tensor_copy(n.rstage[:, 0:512],
                                           n.rstage[:, 0:512]), "sV")
            n.block.vector(tv)

        def so(sq, tcb=tcb):
            ctx.wait(sq, "sV")
            ctx.inc(sq.dma_start(out=d["y"][128 * tcb:128 * tcb + 128, :],
                                 in_=n.ostage[:, D * (tcb % 2):D * (tcb % 2) + D]),
                    "sO", 16)
        if n.fused:
            n.block.gpsimd(so)
        else:
            n.block.scalar(so)

    def fin(sq):
        for s_ in ["sV", "sS", "sO", "sW", "sG", "sWl", "sWl0", "sWl1",
                   "sWx0", "sWx1"] + (["sGg0", "sGg1", "sWt0", "sWt1",
                                       "sWd0", "sWd1", "sG0", "sG1",
                                       "sGt0", "sGt1"] if n.fused else []):
            ctx.wait(sq, s_)
        for tname, shape, dt in taps:
            src_ap = resolve_tap(n, tname)
            ctx.inc(sq.dma_start(out=tap_out[tname], in_=src_ap), "sO", 16)
        ctx.wait(sq, "sO")
    n.block.scalar(fin)



def _prep_launch2_inputs(inputs, xn1g):
    wqkv = np.asarray(inputs["L1_attn_w_qkv"]).astype(np.float32)
    bqkv = np.asarray(inputs["L1_attn_b_qkv"]).astype(np.float32)
    wT = np.ascontiguousarray(wqkv.T)

    g = np.arange(T)
    r, loc = g // 1024, g % 1024
    strip = np.where(loc < 512, r, 15 - r)
    pos = 512 * strip + (loc % 512)
    inv = 1.0 / (10000.0 ** (np.arange(0, 128, 2) / 128.0))
    ang = pos[None, :] * np.repeat(inv, 2)[:, None]
    ctabK = np.cos(ang).astype(np.float32)
    stabK = np.sin(ang).astype(np.float32)
    sgn = np.where(np.arange(128) % 2 == 0, -1.0, 1.0).astype(np.float32)
    stabK = stabK * sgn[:, None]
    sc = 1.0 / np.sqrt(128.0)
    J = np.zeros((128, 128), dtype=np.float32)
    for f in range(64):
        J[2 * f + 1, 2 * f] = 1.0
        J[2 * f, 2 * f + 1] = 1.0
    dmask = np.zeros((4, 128, 512), dtype=np.float32)
    for u in range(4):
        dmask[u] = (128 * u + np.arange(128)[:, None]) <= np.arange(512)[None, :]

    base = {
        "xn1g": xn1g,
        "ctabK": ctabK.astype(BF), "stabK": stabK.astype(BF),
        "ctabQ": (ctabK * sc).astype(BF), "stabQ": (stabK * sc).astype(BF),
        "Jmat": J.astype(BF), "dmask": dmask.astype(BF),
        "id_bf": np.eye(128).astype(BF),
        "ones_p1_bf": np.ones((128, 1), dtype=BF),
        "ones_1p_f32": np.ones((1, 128), dtype=np.float32),
    }
    in_maps = []
    for c in range(NCORE):
        m = dict(base)
        qcols = slice(512 * c, 512 * c + 512)
        m["wq_blk"] = pack_lhsT(np.ascontiguousarray(wT[:, qcols]))
        m["wk_blk"] = pack_lhsT(np.ascontiguousarray(wT[:, 4096 + 128 * c:4096 + 128 * c + 128]))
        m["wv_blk"] = pack_lhsT(np.ascontiguousarray(wT[:, 5120 + 128 * c:5120 + 128 * c + 128]))
        m["bq"] = np.ascontiguousarray(bqkv[qcols].reshape(512, 1))
        m["bk"] = np.ascontiguousarray(bqkv[4096 + 128 * c:4096 + 128 * c + 128].reshape(128, 1))
        m["bv"] = np.ascontiguousarray(bqkv[5120 + 128 * c:5120 + 128 * c + 128].reshape(128, 1))
        in_maps.append(m)
    return in_maps


def _prep_launch3_inputs(inputs, x1s, xn1s, attTs):
    w12T = np.ascontiguousarray(np.asarray(inputs["L1_ffn_w12"]).astype(np.float32).T)
    w3T = np.ascontiguousarray(np.asarray(inputs["L1_ffn_w3"]).astype(np.float32).T)
    woT = np.ascontiguousarray(np.asarray(inputs["L1_attn_w_o"]).astype(np.float32).T)
    base = {
        "w12_blk": pack_lhsT(w12T),
        "b12_pk": np.ascontiguousarray(
            np.asarray(inputs["L1_ffn_b12"]).astype(np.float32).reshape(64, 128).T),
        "w3_blk": pack_lhsT(w3T),
        "b3_pk": np.ascontiguousarray(
            np.asarray(inputs["L1_ffn_b3"]).astype(np.float32).reshape(32, 128).T),
        "wo_blk": pack_lhsT(woT),
        "bo_pk": np.ascontiguousarray(
            np.asarray(inputs["L1_attn_b_o"]).astype(np.float32).reshape(32, 128).T),
        "id_bf": np.eye(128).astype(BF),
    }
    in_maps = []
    for c in range(NCORE):
        m = dict(base)
        m["x1c"] = x1s[c]
        m["xn1c"] = xn1s[c]
        m["attTc"] = attTs[c]
        in_maps.append(m)
    return in_maps


_BUILD_CACHE = {}


def _get(name, fn):
    if name not in _BUILD_CACHE:
        _BUILD_CACHE[name] = fn()
    return _BUILD_CACHE[name]





def emit_barrier(n, tag):
    ctx = n.ctx
    names = [s_ for s_ in ctx.sems if ctx.cnt[s_] > 0]

    def mk(eng):
        for s_ in names:
            ctx.wait(eng, s_)
    n.block.sync(mk)
    n.block.gpsimd(mk)
    n.block.vector(mk)
    n.block.scalar(mk)
    n.block.tensor(mk)


def build_fused(taps=(), dbg_l1=False,
                paranoid=("conv", "x1", "postAG1w", "postKV", "postATT")):
    nc = bass.Bass("TRN2", target_bir_lowering=False)
    dram = {}
    din, dout, dint = _mk_dram_helpers(nc, dram)

    din("x0c", [TC, D], F32)
    for nm, shp in [("lam", [128, 1]), ("vc_rT", [64, 64]), ("vc_iT", [64, 64]),
                    ("bpv", [64, 1]), ("ct_r", [64, 1]), ("ct_i", [64, 1]),
                    ("p_r", [64, 1]), ("p_i", [64, 1]), ("q_r", [64, 1]),
                    ("q_i", [64, 1]), ("dpar", [1, 1]), ("logstep", [1, 1]),
                    ("b_base_pk", [128, 32]), ("wrou_pk", [128, 256]),
                    ("aab_pk", [128, 4096]),
                    ("ball", [128, D]), ("t_r_row", [1, D]), ("t_i_row", [1, D]),
                    ("cl_r", [128, 32]), ("cl_i", [128, 32]),
                    ("ones_1p_f32", [1, 128])]:
        din(nm, shp, F32)
    din("cA", [D, 128], BF16)
    din("sA", [D, 128], BF16)
    din("wbase_blk", [NDT, 128, D], F32)
    for nm, shp in [("csM", [D, 2 * D]), ("trimask", [128, 128]),
                    ("id_bf", [128, 128]), ("ones_p1_bf", [128, 1]),
                    ("E8", [8, 128])]:
        din(nm, shp, BF16)
    din("wq_blk", [4, 128, D], F32)
    din("wk_blk", [1, 128, D], F32)
    din("wv_blk", [1, 128, D], F32)
    din("bq", [512, 1], F32)
    din("bk", [128, 1], F32)
    din("bv", [128, 1], F32)
    din("ctabK", [128, T], BF16)
    din("stabK", [128, T], BF16)
    din("ctabQ", [128, T], BF16)
    din("stabQ", [128, T], BF16)
    din("Jmat", [128, 128], BF16)
    din("dmask", [4, 128, 512], BF16)
    din("gidx", [128, 8], mybir.dt.int32)
    din("w12_blk", [64, 128, D], F32)
    din("b12_pk", [128, 64], F32)
    din("w3_blk", [NDT, 128, D], F32)
    din("b3_pk", [128, 32], F32)
    din("wo_blk", [NDT, 128, D], F32)
    din("bo_pk", [128, 32], F32)

    dout("y", [TC, D], F32)

    dint("kbounce", [8, D], F32)
    dint("xs0", [D, TC], BF16)
    if dbg_l1:
        xs1 = dout("xs1", [D, TC], BF16)
        dout("attdbg", [T, 512], BF16)
    else:
        xs1 = dint("xs1", [D, TC], BF16)
    cc1_in = dint("cc1_in", [D, TC], BF16)
    cc1_out = dint("cc1_out", [NCORE * D, TC], BF16, addr_space="Shared")
    cc2_in = dint("cc2_in", [T, 512], BF16)
    cc2_out = dint("cc2_out", [NCORE * T, 512], BF16, addr_space="Shared")
    dram["x1_out"] = xs1
    dram["xn1_out"] = cc1_in
    dram["xn1g"] = cc1_out
    dram["x1c"] = xs1
    dram["xn1c"] = cc1_in

    tap_out = {}
    for tname, shape, dt in taps:
        tap_out[tname] = dout("tap_" + tname, list(shape), dt)

    with contextlib.ExitStack() as stack:
        n = _alloc_l1(nc, stack, dram, fused=True)
        n.fused = True
        sb = lambda name, shape, dt: stack.enter_context(nc.sbuf_tensor(name, list(shape), dt))

        n.Kbuf = n.bufA[:, 0:T]
        n.Vtok = n.bufA[:, T:2 * T]
        n.wqb = n.bufA[:, 2 * T:2 * T + 4 * D]
        n.xbig = n.bufB[:, 0:16384]
        n.wkb = n.bufB[:, 16384:16384 + D]
        n.wvb = n.bufB[:, 16384 + D:16384 + 2 * D]
        n.qro = n.rstage[:, 0:2048]
        n.pbuf = n.sqst[:, 0:1536]
        n.pxtra = n.mrep[:, 512:1024]
        n.sqb = n.sqst[:, 1536:2048]
        n.silu = n.rstage[:, 0:TC]
        n.gslot = lambda rr: (n.wbuf[:, D:2 * D] if rr % 2 == 0
                              else n.wbuf[:, 3 * D:4 * D])
        n.tstg = n.bufB[:, 24576:24576 + 2048]
        n.kst = n.cscr[:, 0:512]
        n.vst = n.mrep[:, 0:512]
        n.qtmp = n.cscr[:, 512:1024]
        n.rowz = n.cscr[0:1, 1024:1536]
        n.sc2 = sb("sc2", [1, 16], F32)
        n.sc2b = sb("sc2b", [1, 32], F32)
        n.negb = sb("negb", [128, 16], F32)
        n.accden = n.cscr[:, 2048:2560]
        n.rrepA = n.krr[:, :]
        n.tstage = n.rrep[:, 512:1024]
        n.bqb = sb("bqb", [128, 4], F32)
        n.bkb = sb("bkb", [128, 1], F32)
        n.bvb = sb("bvb", [128, 1], F32)
        n.sb_J = sb("sb_J", [128, 128], BF16)
        n.gidx = sb("gidx_sb", [128, 8], mybir.dt.int32)
        n.sb_dm = n.Tmat[:, 0:2048]
        n.b12b = sb("b12b", [128, 64], F32)
        n.b3b = sb("b3b", [128, NDT], F32)
        n.bob = sb("bob", [128, NDT], F32)
        n.ostage = n.wbuf[:, 0:2 * D]

        ctx = n.ctx = Ctx(nc, stack)
        for s_ in ["sW", "sG", "sO", "sS", "sV", "sT", "sWt", "sVq", "sWd",
                   "sWl", "sWf", "sTg", "sVp", "sTp", "sVx", "sOx", "sWr",
                   "sVr", "sWk", "sVk", "sC1", "sC2", "sGg", "sTgg", "sP",
                   "sX", "sKV", "sWt0", "sWt1", "sWd0", "sWd1", "sG0", "sG1",
                   "sWl0", "sWl1", "sWx0", "sWx1", "sGt0", "sGt1",
                   "sGg0", "sGg1"]:
            ctx.sem(s_)
        n.PA = PsRes(ctx, n.psA)
        n.PB = PsRes(ctx, n.psB)
        n.PE = PsRes(ctx, n.psE)
        n.PF = PsRes(ctx, n.psF)
        n.PT1 = PsRes(ctx, n.psT1)
        n.PT2 = PsRes(ctx, n.psT2)
        n.psQ = n.psE
        n.PQ = n.PE
        n.psS1 = n.psA
        n.PS1 = n.PA
        n.psS2 = n.psB
        n.PS2 = n.PB
        n.psAV = n.psF[:, 0:512]
        n.psD = n.psF[:, 512:1024]
        n.PAV = PsRes(ctx, n.psAV)
        n.PD = PsRes(ctx, n.psD)
        n.psVT = n.psT1[:, 0:512]
        n.PVT = n.PT1

        with nc.Block() as block:
            n.block = block
            emit_const_loads(n)
            emit_l2_consts_sm(n, with_dm=False)
            emit_l3_consts(n)
            bset = paranoid if isinstance(paranoid, (set, list, tuple)) else (
                {"ssm", "tb", "pro", "ln0", "ap0", "ffn0", "conv", "x1", "postL1",
                 "postAG1w", "postKV", "postATT", "postAG2", "l3a", "l3ffn",
                 "l3w3", "l3gather", "l3wo"} if paranoid else set())
            n.bset = bset
            P = lambda tag: (emit_barrier(n, tag) if tag in bset else None)
            emit_ssm_prep(n)
            P("ssm")
            emit_tbuild(n)
            P("tb")
            emit_prologue(n)
            P("pro")
            emit_lnstats(n, n.A)
            P("ln0")
            emit_apply_ln(n, n.A, n.B)
            P("ap0")
            emit_ffn0(n)
            P("ffn0")
            emit_conv(n)
            P("conv")
            emit_x1(n)
            P("x1")
            emit_l2_dm(n)
            P("postL1")
            xn1_sOx = ctx.val("sOx")

            def ag1(gq):
                ctx.wait(gq, "sOx", xn1_sOx)
                ctx.inc(gq.collective_compute(
                    "AllGather", ALU.bypass,
                    replica_groups=[list(range(NCORE))],
                    ins=[dram["cc1_in"].opt()], outs=[dram["cc1_out"].opt()]), "sC1")
            block.gpsimd(ag1)
            emit_l2_wload(n)
            P("postAG1w")
            emit_l2_kv(n)
            P("postKV")
            emit_l2_attn(n)
            P("postATT")
            sP_att = ctx.val("sP")
            n.wl0_pre = [ctx.val("sWl0"), ctx.val("sWl1")]

            def g1pre(gq):
                ctx.wait(gq, "sP", sP_att)
                for mc in range(2):
                    ctx.inc(gq.dma_start(
                        out=n.wbuf[:, 2 * D * mc:2 * D * mc + D],
                        in_=dram["w12_blk"][mc]), f"sWl{mc % 2}", 16)
                    ctx.inc(gq.dma_start(
                        out=n.wbuf[:, 2 * D * mc + D:2 * D * mc + 2 * D],
                        in_=dram["w12_blk"][32 + mc]), f"sWl{mc % 2}", 16)
            block.gpsimd(g1pre)

            att_sO = ctx.val("sO")

            def ag2(gq):
                ctx.wait(gq, "sO", att_sO)
                ctx.inc(gq.collective_compute(
                    "AllGather", ALU.bypass,
                    replica_groups=[list(range(NCORE))],
                    ins=[dram["cc2_in"].opt()], outs=[dram["cc2_out"].opt()]), "sC2")
            block.gpsimd(ag2)
            n.paranoid = bset
            if "postAG2" in bset:
                emit_barrier(n, "postAG2")
            l2done = {n.PS1.tname: ctx.val(n.PS1.tname),
                      n.PS2.tname: ctx.val(n.PS2.tname),
                      n.PE.tname: ctx.val(n.PE.tname),
                      n.PAV.tname: ctx.val(n.PAV.tname),
                      n.PAV.cname: ctx.val(n.PAV.cname),
                      n.PD.cname: ctx.val(n.PD.cname),
                      n.PT1.cname: ctx.val(n.PT1.cname),
                      "sP": ctx.val("sP"),
                      "sO": ctx.val("sO")}
            emit_l3(n, taps, tap_out, l2done=l2done)
    return nc, dram


def _prep_fused_inputs(inputs):
    in1 = _prep_launch1_inputs(inputs)
    in2 = _prep_launch2_inputs(inputs, None)
    in3 = _prep_launch3_inputs(inputs, [None] * NCORE, [None] * NCORE,
                               [None] * NCORE)
    in_maps = []
    for c in range(NCORE):
        m = dict(in1[c])
        m.update({k: v for k, v in in2[c].items() if v is not None and k != "xn1g"})
        m.update({k: v for k, v in in3[c].items() if v is not None})
        p = np.arange(128, dtype=np.int64)
        idx = (8192 * np.arange(8)[None, :] + 1024 * c + 8 * p[:, None])
        m["gidx"] = idx.astype(np.int32)
        in_maps.append(m)
    return in_maps


def kernel_unfused(**inputs):
    nc1, _ = _get("l1", lambda: build_launch1())
    in1 = _prep_launch1_inputs(inputs)
    r1 = run_bass_kernel_spmd(nc1, in1, core_ids=list(range(NCORE)))
    x1s = [r1.results[c]["x1_out"] for c in range(NCORE)]
    xn1s = [r1.results[c]["xn1_out"] for c in range(NCORE)]

    xn1g = np.ascontiguousarray(np.concatenate(xn1s, axis=0))
    nc2, _ = _get("l2", lambda: build_launch2())
    in2 = _prep_launch2_inputs(inputs, xn1g)
    r2 = run_bass_kernel_spmd(nc2, in2, core_ids=list(range(NCORE)))
    att = [r2.results[c]["attnout"] for c in range(NCORE)]

    attTs = [np.ascontiguousarray(np.concatenate(
        [att[r][:, 1024 * c:1024 * c + 1024] for r in range(NCORE)], axis=0))
        for c in range(NCORE)]
    nc3, _ = _get("l3", lambda: build_launch3())
    in3 = _prep_launch3_inputs(inputs, x1s, xn1s, attTs)
    r3 = run_bass_kernel_spmd(nc3, in3, core_ids=list(range(NCORE)))

    out = np.empty((T, D), dtype=np.float32)
    for c in range(NCORE):
        out[_strip_tokens(c), :] = r3.results[c]["y"]
    return out.reshape(1, T, D)


def kernel(**inputs):
    nc, _ = _get("fused", lambda: build_fused())
    in_maps = _prep_fused_inputs(inputs)
    res = run_bass_kernel_spmd(nc, in_maps, core_ids=list(range(NCORE)))
    out = np.empty((T, D), dtype=np.float32)
    for c in range(NCORE):
        out[_strip_tokens(c), :] = res.results[c]["y"]
    return out.reshape(1, T, D)


# version 15
# speedup vs baseline: 1.0081x; 1.0081x over previous
import contextlib
import types

import numpy as np
import ml_dtypes
import concourse.bass as bass
import concourse.mybir as mybir
from concourse.bass_utils import run_bass_kernel_spmd

F32 = mybir.dt.float32
BF16 = mybir.dt.bfloat16
AF = mybir.ActivationFunctionType
ALU = mybir.AluOpType
AX = mybir.AxisListType
PM2 = mybir.MatmulPerfMode.DoubleRow

BF = ml_dtypes.bfloat16

D = 4096
T = 8192
TC = 1024
NCORE = 8
NDT = D // 128
NTT = TC // 128
EPS = 1e-5



def pack_lhsT(wT):
    Din, O = wT.shape
    dt_, oc_ = Din // 128, O // 128
    return np.ascontiguousarray(
        wT.reshape(dt_, 128, oc_, 128).transpose(2, 1, 0, 3).reshape(oc_, 128, Din))


def _host_tables():
    t = {}
    L = D
    s = np.arange(L)
    p_of, j_of = s // 32, s % 32
    l_of_r = np.empty(L, dtype=np.int64)
    l_of_r[128 * j_of + p_of] = s

    Omega = np.exp(-2j * np.pi * (np.arange(L) / L))
    tt = (1.0 - Omega) / (1.0 + Omega)
    cc = 2.0 / (1.0 + Omega)

    t["t_r_row"] = tt.real.reshape(1, L).astype(np.float32).copy()
    t["t_i_row"] = tt.imag.reshape(1, L).astype(np.float32).copy()
    lpj = 32 * np.arange(128)[:, None] + np.arange(32)[None, :]
    t["cl_r"] = cc.real[lpj].astype(np.float32).copy()
    t["cl_i"] = cc.imag[lpj].astype(np.float32).copy()

    a = np.arange(128)
    ang_a = 2 * np.pi * np.outer(l_of_r, a) / L
    t["cA"] = (np.cos(ang_a) / L).astype(BF).copy()
    t["sA"] = (np.sin(ang_a) / L).astype(BF).copy()

    m = np.arange(L)
    ang_m = 2 * np.pi * np.outer(l_of_r, m) / L
    cM = np.cos(ang_m).astype(BF)
    sM = np.sin(ang_m).astype(BF)
    cs = np.empty((32, 128, 4, 2048), dtype=BF)
    for a in range(4):
        cs[:, :, a, 0:1024] = cM.reshape(32, 128, 4096)[:, :, 1024 * a:1024 * a + 1024]
        cs[:, :, a, 1024:2048] = sM.reshape(32, 128, 4096)[:, :, 1024 * a:1024 * a + 1024]
    t["csM"] = np.ascontiguousarray(cs.reshape(4096, 8192))

    t["trimask"] = (np.arange(128)[None, :] >= np.arange(128)[:, None]).astype(BF).copy()
    t["id_bf"] = np.eye(128).astype(BF).copy()
    t["ones_p1_bf"] = np.ones((128, 1), dtype=BF)
    t["ones_1p_f32"] = np.ones((1, 128), dtype=np.float32)
    E8 = np.zeros((8, 128), dtype=np.float32)
    for e in range(8):
        E8[e, 16 * e:16 * e + 16] = 2.0
    t["E8"] = E8.astype(BF).copy()
    return t


def _strip_tokens(c):
    return np.concatenate([np.arange(512 * c, 512 * c + 512),
                           np.arange(512 * (15 - c), 512 * (15 - c) + 512)])



class Ctx:

    def __init__(self, nc, stack):
        self.nc = nc
        self.stack = stack
        self.cnt = {}
        self.sems = {}

    def sem(self, name):
        h = self.stack.enter_context(self.nc.semaphore(name))
        self.cnt[name] = 0
        self.sems[name] = h
        return h

    def inc(self, instr, name, n=1):
        self.cnt[name] += n
        instr.then_inc(self.sems[name], n)
        return instr

    def wait(self, eng, name, val=None):
        v = self.cnt[name] if val is None else val
        if v > 0:
            eng.wait_ge(self.sems[name], v)

    def val(self, name):
        return self.cnt[name]


class PsRes:
    _n = 0

    def __init__(self, ctx, ap):
        self.ctx = ctx
        self.ap = ap
        i = PsRes._n
        PsRes._n += 1
        self.tname = f"psT{i}"
        self.cname = f"psC{i}"
        ctx.sem(self.tname)
        ctx.sem(self.cname)

    def prod_wait(self, eng):
        self.ctx.wait(eng, self.cname)

    def prod_done(self, instr):
        return self.ctx.inc(instr, self.tname)

    def cons_wait(self, eng):
        self.ctx.wait(eng, self.tname)

    def cons_done(self, instr):
        return self.ctx.inc(instr, self.cname)


class SerialEng:

    def __init__(self, eng, spacer_ap):
        self._eng = eng
        self._sp = spacer_ap

    def __getattr__(self, name):
        f = getattr(self._eng, name)

        def emit(*a, **k):
            instr = f(*a, **k)
            self._eng.tensor_copy(self._sp, self._sp)
            return instr
        return emit



def build_launch1(taps=(), etaps=False):
    nc = bass.Bass("TRN2", target_bir_lowering=False)
    dram = {}
    din, dout, dint = _mk_dram_helpers(nc, dram)

    din("x0c", [TC, D], F32)
    for nm, shp in [("lam", [128, 1]), ("vc_rT", [64, 64]), ("vc_iT", [64, 64]),
                    ("bpv", [64, 1]), ("ct_r", [64, 1]), ("ct_i", [64, 1]),
                    ("p_r", [64, 1]), ("p_i", [64, 1]), ("q_r", [64, 1]),
                    ("q_i", [64, 1]), ("dpar", [1, 1]), ("logstep", [1, 1]),
                    ("b_base_pk", [128, 32]), ("wrou_pk", [128, 256]),
                    ("aab_pk", [128, 4096]),
                    ("ball", [128, D]), ("t_r_row", [1, D]), ("t_i_row", [1, D]),
                    ("cl_r", [128, 32]), ("cl_i", [128, 32]),
                    ("ones_1p_f32", [1, 128])]:
        din(nm, shp, F32)
    din("cA", [D, 128], BF16)
    din("sA", [D, 128], BF16)
    din("wbase_blk", [NDT, 128, D], F32)
    for nm, shp in [("csM", [D, 2 * D]), ("trimask", [128, 128]),
                    ("id_bf", [128, 128]), ("ones_p1_bf", [128, 1]),
                    ("E8", [8, 128])]:
        din(nm, shp, BF16)

    dout("x1_out", [D, TC], BF16)
    dout("xn1_out", [D, TC], BF16)
    for _dn in ["dbg_m_ln0", "dbg_r_ln0", "dbg_s_ln0"]:
        dout(_dn, [1, TC], F32)
    dint("kbounce", [8, D], F32)
    dint("xs0", [D, TC], BF16)

    tap_out = {}
    for tname, shape, dt in taps:
        tap_out[tname] = dout("tap_" + tname, list(shape), dt)

    with contextlib.ExitStack() as stack:
        n = _alloc_l1(nc, stack, dram)
        ctx = n.ctx = Ctx(nc, stack)
        for s_ in ["sW", "sG", "sO", "sS", "sV", "sT", "sWt", "sVq", "sWd",
                   "sWl", "sWf", "sTg", "sVp", "sTp", "sVx", "sOx", "sWr",
                   "sVr", "sWk", "sVk", "sZ", "sWt0", "sWt1", "sWd0", "sWd1",
                   "sG0", "sG1", "sWl0", "sWl1", "sWx0", "sWx1"]:
            ctx.sem(s_)
        n.PA = PsRes(ctx, n.psA)
        n.PB = PsRes(ctx, n.psB)
        n.PE = PsRes(ctx, n.psE)
        n.PF = PsRes(ctx, n.psF)
        n.PT1 = PsRes(ctx, n.psT1)
        n.PT2 = PsRes(ctx, n.psT2)

        with nc.Block() as block:
            n.block = block
            emit_const_loads(n)
            emit_ssm_prep(n)
            if etaps:
                et = {}
                for nm, shape, dt in [("tvalP", [128, 4096], BF16),
                                      ("tvalQ", [128, 4096], BF16),
                                      ("ar_i", [128, 32], F32),
                                      ("cscrF", [128, 512], F32),
                                      ("sc11", [1, 8], F32)]:
                    et[nm] = dout("etap_" + nm, shape, dt)

                def esy(sy):
                    ctx.wait(sy, "sV", n.ssm_done_sV)
                    ctx.inc(sy.dma_start(out=et["tvalP"], in_=n.bufA[:, 0:4096]), "sZ", 16)
                    ctx.inc(sy.dma_start(out=et["tvalQ"], in_=n.bufA[:, 4096:8192]), "sZ", 16)
                    ctx.inc(sy.dma_start(out=et["ar_i"], in_=n.ar_i[:]), "sZ", 16)
                    ctx.inc(sy.dma_start(out=et["cscrF"], in_=n.cscr[:, 512:1024]), "sZ", 16)
                    ctx.inc(sy.dma_start(out=et["sc11"], in_=n.sc11[0:1, 0:8]), "sZ", 16)
                    ctx.wait(sy, "sZ")
                n.block.sync(esy)
            emit_tbuild(n)
            emit_prologue(n)
            emit_lnstats(n, n.A, dbg="ln0")
            emit_apply_ln(n, n.A, n.B)
            emit_ffn0(n)
            emit_conv(n)
            emit_x1(n)
            emit_taps(n, taps, tap_out)

    return nc, dram


def _mk_dram_helpers(nc, dram):
    def din(name, shape, dt):
        dram[name] = nc.dram_tensor(name, list(shape), dt, kind="ExternalInput").ap()
        return dram[name]

    def dout(name, shape, dt):
        dram[name] = nc.dram_tensor(name, list(shape), dt, kind="ExternalOutput").ap()
        return dram[name]

    def dint(name, shape, dt, **kw):
        dram[name] = nc.dram_tensor(name, list(shape), dt, kind="Internal", **kw).ap()
        return dram[name]
    return din, dout, dint


def _alloc_l1(nc, stack, dram, fused=False):
    n = types.SimpleNamespace()
    n.nc = nc
    n.dram = dram
    sb = lambda name, shape, dt: stack.enter_context(nc.sbuf_tensor(name, list(shape), dt))
    ps = lambda name, shape, dt: stack.enter_context(nc.psum_tensor(name, list(shape), dt))

    n.bufA = sb("bufA", [128, NDT * TC], BF16)
    n.bufB = sb("bufB", [128, NDT * TC], BF16)
    n.A = lambda d: n.bufA[:, d * TC:(d + 1) * TC]
    n.B = lambda d: n.bufB[:, d * TC:(d + 1) * TC]
    n.tvalP = lambda j: n.bufA[:, 128 * j:128 * j + 128]
    n.tvalQ = lambda j: n.bufA[:, 4096 + 128 * j:4096 + 128 * j + 128]
    n.trig = lambda s_: n.bufA[:, 8192 + 128 * s_:8192 + 128 * s_ + 128]
    n.stage = lambda s_: n.bufB[:, D * s_:D * s_ + D]

    if fused:
        n.wbuf = sb("wbuf", [128, 4 * D], BF16)
        n.Tmat = n.wbuf[:, 2 * D:3 * D]
    else:
        n.Tmat = sb("Tmat", [128, D], BF16)
        n.wbuf = sb("wbuf", [128, 2 * D], BF16)
    n.dftbuf = n.wbuf
    n.wrb = sb("wrb", [128, 8 * NDT], BF16)
    n.aab = sb("aab", [128, D], BF16)
    n.blb = sb("blb", [128, D], BF16)
    if fused:
        n.cscr = sb("cscr", [128, 6 * 512], F32)
        n.r_sum = n.cscr[0:1, 0:1024]
        n.r_ssq = n.cscr[0:1, 1024:2048]
        n.r_var = n.cscr[0:1, 2048:3072]
        n.r_rstd = n.r_var
        n.r_mean = n.r_sum
        n.rowb = n.cscr[0:1, 1536:2560]
        n.krr = sb("krr", [128, 512], F32)
        n.kchunk = n.krr[0:8, :]
        n.sspacer = n.cscr[0:1, 1536:2048]
    else:
        n.cscr = sb("cscr", [128, 3 * 512], F32)
        n.kchunk = sb("kchunk", [8, 512], F32)
        n.sspacer = n.cscr[0:1, 1024:1536]
    n.sscr = n.cscr[:, 1024:1536]
    n.vcT = n.cscr[0:64, 0:128]
    n.karr = n.cscr[:, 512:768]
    n.svec = sb("svec", [64, 16], F32)
    n.svec2 = sb("svec2", [64, 16], F32)
    n.vstack = sb("vstack", [128, 8], F32)
    n.clr = sb("clr", [128, 32], F32)
    n.cli = sb("cli", [128, 32], F32)
    n.ar_r = sb("ar_r", [128, 32], F32)
    n.ar_i = sb("ar_i", [128, 32], F32)
    n.ar_in = sb("ar_in", [128, 32], F32)
    n.lamn = sb("lamn", [128, 1], F32)
    if not fused:
        n.rowb = sb("rowb", [1, 1024], F32)
    n.sc11 = sb("sc11", [1, 8], F32)
    n.dvec = sb("dvec", [128, 1], F32)

    n.sqst = sb("sqst", [128, 2 * TC], BF16)
    n.rstage = sb("rstage", [128, 2 * TC], BF16)
    n.ctall = n.sqst[0:8, 0:TC]
    n.cexp = n.rstage[:, 0:TC]
    n.hbuf = n.rstage[:, TC:2 * TC]
    n.hsc = n.hbuf
    if not fused:
        n.r_sum = sb("r_sum", [1, TC], F32)
        n.r_ssq = sb("r_ssq", [1, TC], F32)
        n.r_rstd = sb("r_rstd", [1, TC], F32)
        n.r_mean = n.r_sum
        n.r_var = n.r_ssq
    n.mrep = sb("mrep", [128, TC], BF16)
    n.rrep = sb("rrep", [128, TC], BF16)
    n.bbuf = sb("bbuf", [128, NDT], F32)
    n.lgst = sb("lgst", [128, 64], F32)
    n.ebuf = sb("ebuf", [128, 64], F32)
    n.m1b = sb("m1b", [128, 8], F32)
    n.nm1b = sb("nm1b", [128, 8], F32)
    n.m2b = sb("m2b", [128, 8], F32)
    n.rcb = sb("rcb", [128, 8], F32)
    n.cmbt = sb("cmbt", [128, 64], BF16)

    n.sb_id = sb("sb_id", [128, 128], BF16)
    n.sb_ones_p1 = sb("sb_ones_p1", [128, 1], BF16)
    n.sb_ones_1p = sb("sb_ones_1p", [1, 128], F32)
    n.sb_tri = sb("sb_tri", [128, 128], BF16)
    n.sb_E8 = sb("sb_E8", [8, 128], BF16)

    n.psA = ps("psA", [128, 512], F32)
    n.psB = ps("psB", [128, 512], F32)
    n.psE = ps("psE", [128, 1024], F32)
    n.psF = ps("psF", [128, 1024], F32)
    n.psT1 = ps("psT1", [128, 1024], BF16)
    n.psT2 = ps("psT2", [128, 1024], BF16)
    return n



def emit_const_loads(n):
    ctx, d = n.ctx, n.dram

    def f(sy):
        for dst, src in [(n.sb_id, d["id_bf"]), (n.sb_ones_p1, d["ones_p1_bf"]),
                         (n.sb_ones_1p, d["ones_1p_f32"]), (n.sb_tri, d["trimask"]),
                         (n.sb_E8, d["E8"]), (n.lamn, d["lam"]),
                         (n.clr, d["cl_r"]), (n.cli, d["cl_i"])]:
            ctx.inc(sy.dma_start(out=dst[:], in_=src), "sW", 16)
        for i, src in enumerate(["bpv", "ct_r", "ct_i", "p_r", "p_i", "q_r", "q_i"]):
            ctx.inc(sy.dma_start(out=n.svec[:, i:i + 1], in_=d[src]), "sW", 16)
        ctx.inc(sy.dma_start(out=n.sc11[:, 0:1], in_=d["logstep"]), "sW", 16)
        ctx.inc(sy.dma_start(out=n.sc11[:, 1:2], in_=d["dpar"]), "sW", 16)
        ctx.inc(sy.dma_start(out=n.vcT[:, 0:64], in_=d["vc_rT"]), "sW", 16)
        ctx.inc(sy.dma_start(out=n.vcT[:, 64:128], in_=d["vc_iT"]), "sW", 16)
        ctx.inc(sy.dma_start(out=n.bbuf[:, 0:32], in_=d["b_base_pk"]), "sW", 16)
    n.block.sync(f)

    def g(gq):
        ctx.inc(gq.dma_start(out=n.wrb[:], in_=d["wrou_pk"]), "sG", 16)
        ctx.inc(gq.dma_start(out=n.aab[:], in_=d["aab_pk"]), "sG", 16)
        ctx.inc(gq.dma_start(out=n.blb[:], in_=d["ball"]), "sG", 16)
    n.block.gpsimd(g)
    n.const_sW = ctx.val("sW")
    n.const_sG = ctx.val("sG")


def emit_ssm_prep(n):
    ctx, d = n.ctx, n.dram

    def sc(s):
        ctx.wait(s, "sW", n.const_sW)
        s.activation(n.sc11[0:1, 2:3], n.sc11[0:1, 0:1], AF.Exp, scale=-1.0)
        ctx.inc(s.copy(n.sspacer[:], n.sspacer[:]), "sS")
    n.block.scalar(sc)

    def v1(v):
        ctx.wait(v, "sS")
        v.tensor_scalar_mul(n.sc11[0:1, 3:4], n.sc11[0:1, 2:3], 2.0)
        ctx.inc(v.tensor_scalar_mul(n.lamn[:], n.lamn[:], -1.0), "sV")
    n.block.vector(v1)
    g_ready = ctx.val("sV")

    def t1(t):
        ctx.wait(t, "sV", g_ready)
        n.PA.prod_wait(t)
        t.matmul(n.psA[0:64, 0:1], n.vcT[:, 0:64], n.svec[:, 0:1], start=True, stop=True)
        n.PA.prod_done(
            t.matmul(n.psA[0:64, 1:2], n.vcT[:, 64:128], n.svec[:, 0:1],
                     start=True, stop=True))
    n.block.tensor(t1)

    def v2(v):
        rv = v
        n.PA.cons_wait(v)
        v = SerialEng(v, n.mrep[:, 0:512])
        v.tensor_copy(n.svec[:, 7:8], n.psA[0:64, 0:1])
        n.PA.cons_done(v.tensor_copy(n.svec[:, 8:9], n.psA[0:64, 1:2]))
        C = lambda i: n.svec[:, i:i + 1]
        W = lambda i: n.svec2[:, i:i + 1]
        BR, BI, CR, CI = C(7), C(8), C(1), C(2)
        PR, PI, QR, QI = C(3), C(4), C(5), C(6)
        pairs = [(CR, CI, BR, BI, 0), (CR, CI, PR, PI, 2),
                 (QR, QI, BR, BI, 4), (QR, QI, PR, PI, 6)]
        for (xr, xi, yr, yi, o) in pairs:
            v.tensor_mul(W(o), xr, yr)
            v.tensor_mul(W(8), xi, yi)
            v.tensor_add(W(o), W(o), W(8))
            v.tensor_mul(W(o + 1), xr, yi)
            v.tensor_mul(W(8), xi, yr)
            v.tensor_sub(W(o + 1), W(o + 1), W(8))
        for cidx in range(4):
            vr, vi = W(2 * cidx), W(2 * cidx + 1)
            v.tensor_copy(n.vstack[0:64, 2 * cidx:2 * cidx + 1], vr)
            v.tensor_copy(n.vstack[64:128, 2 * cidx:2 * cidx + 1], vi)
            v.tensor_copy(n.vstack[0:64, 2 * cidx + 1:2 * cidx + 2], vi)
            v.tensor_scalar_mul(W(9), vr, -1.0)
            v.tensor_copy(n.vstack[64:128, 2 * cidx + 1:2 * cidx + 2], W(9))
        ctx.inc(rv.tensor_copy(n.mrep[:, 0:512], n.mrep[:, 0:512]), "sV")
    n.block.vector(v2)
    vstack_done = ctx.val("sV")

    CS = lambda i: n.cscr[:, 512 * i:512 * i + 512]
    RB = lambda ch, h: n.rowb[0:1, 512 * h:512 * h + 512]
    pe_t0 = ctx.val(n.PE.tname)
    wk0 = ctx.val("sWk")
    for ch in range(8):
        cs = slice(512 * ch, 512 * ch + 512)

        def sy2(sy, ch=ch, cs=cs):
            if ch >= 1:
                sy.wait_ge(ctx.sems[n.PE.tname], pe_t0 + ch)
            ctx.inc(sy.dma_start(out=RB(ch, 0), in_=d["t_r_row"][0:1, cs]), "sWr", 16)
            ctx.inc(sy.dma_start(out=RB(ch, 1), in_=d["t_i_row"][0:1, cs]), "sWr", 16)
        n.block.sync(sy2)

        def vr(v, ch=ch):
            ctx.wait(v, "sWr")
            if ch == 0:
                ctx.wait(v, "sV", vstack_done)
            sl = n.rowb[0:1, 0:1024]
            ctx.inc(v.tensor_scalar(sl, sl, n.sc11[0:1, 3:4], None, ALU.mult), "sVr")
        n.block.vector(vr)

        def t2(t, ch=ch):
            n.PE.prod_wait(t)
            ctx.wait(t, "sVr")
            t.matmul(n.psE[0:64, 0:512], n.sb_ones_1p[:, 0:64], RB(ch, 0),
                     start=True, stop=True)
            n.PE.prod_done(
                t.matmul(n.psE[64:128, 0:512], n.sb_ones_1p[:, 0:64], RB(ch, 1),
                         start=True, stop=True))
        n.block.tensor(t2)

        def v3(v, ch=ch):
            n.PE.cons_wait(v)
            if ch > 0:
                ctx.wait(v, n.PF.tname)
            n.PE.cons_done(
                v.tensor_scalar(CS(0), n.psE[:, 0:512], n.lamn[:], None, ALU.add))
            v.tensor_copy(CS(2)[0:64, :], CS(0)[64:128, :])
            v.tensor_mul(CS(2)[0:64, :], CS(2)[0:64, :], CS(2)[0:64, :])
            v.tensor_mul(CS(1)[0:64, :], CS(0)[0:64, :], CS(0)[0:64, :])
            v.tensor_add(CS(2)[0:64, :], CS(2)[0:64, :], CS(1)[0:64, :])
            v.reciprocal(CS(2)[0:64, :], CS(2)[0:64, :])
            v.tensor_mul(CS(1)[0:64, :], CS(0)[0:64, :], CS(2)[0:64, :])
            v.tensor_copy(CS(2)[64:128, :], CS(2)[0:64, :])
            ctx.inc(v.tensor_mul(CS(1)[64:128, :], CS(0)[64:128, :], CS(2)[64:128, :]),
                    "sVq")
        n.block.vector(v3)
        ii_done = ctx.val("sVq")

        def t3(t, ii_done=ii_done):
            ctx.wait(t, "sVq", ii_done)
            n.PF.prod_wait(t)
            n.PF.prod_done(
                t.matmul(n.psF[0:8, 0:512], n.vstack[:], CS(1), start=True, stop=True))
        n.block.tensor(t3)

        def v4(v, ch=ch):
            n.PF.cons_wait(v)
            if ch >= 1:
                v.wait_ge(ctx.sems["sWk"], wk0 + 16 * ch)
            n.PF.cons_done(v.tensor_copy(n.kchunk[:], n.psF[0:8, 0:512]))
            ctx.inc(v.tensor_copy(n.mrep[:, 0:512], n.mrep[:, 0:512]), "sVk")
        n.block.vector(v4)

        def sy3(sy, ch=ch, cs=cs):
            ctx.wait(sy, "sVk")
            ctx.inc(sy.dma_start(out=d["kbounce"][:, cs], in_=n.kchunk[:]),
                    "sWk", 16)
        n.block.sync(sy3)

    def sy4(sy):
        ctx.wait(sy, "sWk")
        for comp in range(8):
            ctx.inc(sy.dma_start(
                out=n.karr[:, 32 * comp:32 * comp + 32],
                in_=d["kbounce"][comp:comp + 1, :].rearrange("o (p j) -> (o p) j", p=128)),
                "sW", 16)
    n.block.sync(sy4)
    karr_in = ctx.val("sW")

    def v5(v):
        rv = v
        ctx.wait(v, "sW", karr_in)
        v = SerialEng(v, n.mrep[:, 0:512])
        K = lambda c: n.karr[:, 32 * c:32 * c + 32]
        k00r, k00i, k01r, k01i, k10r, k10i, k11r, k11i = [K(c) for c in range(8)]
        S = lambda i: n.sscr[:, 64 * i:64 * i + 32]
        v.tensor_scalar_add(S(0), k11r, 1.0)
        v.tensor_mul(S(1), S(0), S(0))
        v.tensor_mul(S(2), k11i, k11i)
        v.tensor_add(S(1), S(1), S(2))
        v.reciprocal(S(1), S(1))
        v.tensor_mul(S(2), k01r, k10r)
        v.tensor_mul(S(3), k01i, k10i)
        v.tensor_sub(S(2), S(2), S(3))
        v.tensor_mul(S(3), k01r, k10i)
        v.tensor_mul(S(4), k01i, k10r)
        v.tensor_add(S(3), S(3), S(4))
        v.tensor_mul(S(4), S(2), S(0))
        v.tensor_mul(S(5), S(3), k11i)
        v.tensor_add(S(4), S(4), S(5))
        v.tensor_mul(S(4), S(4), S(1))
        v.tensor_mul(S(5), S(3), S(0))
        v.tensor_mul(S(6), S(2), k11i)
        v.tensor_sub(S(5), S(5), S(6))
        v.tensor_mul(S(5), S(5), S(1))
        v.tensor_sub(S(4), k00r, S(4))
        v.tensor_sub(S(5), k00i, S(5))
        v.tensor_mul(n.ar_r[:], n.clr[:], S(4))
        v.tensor_mul(S(6), n.cli[:], S(5))
        v.tensor_sub(n.ar_r[:], n.ar_r[:], S(6))
        v.tensor_mul(n.ar_i[:], n.clr[:], S(5))
        v.tensor_mul(S(6), n.cli[:], S(4))
        v.tensor_add(n.ar_i[:], n.ar_i[:], S(6))
        v.tensor_scalar_mul(n.ar_in[:], n.ar_i[:], -1.0)
        ctx.inc(rv.tensor_copy(n.mrep[:, 0:512], n.mrep[:, 0:512]), "sV")
    n.block.vector(v5)
    ar_done = ctx.val("sV")

    wt0 = [ctx.val("sWt0"), ctx.val("sWt1")]

    def sy3(sy):
        for j in range(32):
            s0 = 2 * (j % 2)
            if j >= 2:
                ctx.wait(sy, "sVq", ii_done + (j - 1))
            ctx.inc(sy.dma_start(out=n.trig(s0), in_=d["cA"][128 * j:128 * j + 128, :]),
                    f"sWt{j % 2}", 16)
            ctx.inc(sy.dma_start(out=n.trig(s0 + 1), in_=d["sA"][128 * j:128 * j + 128, :]),
                    f"sWt{j % 2}", 16)
    n.block.sync(sy3)

    def v6(v):
        ctx.wait(v, "sV", ar_done)
        F = lambda i: n.cscr[:, 640 * i:640 * i + 128]
        for j in range(32):
            s0 = 2 * (j % 2)
            v.wait_ge(ctx.sems[f"sWt{j % 2}"], wt0[j % 2] + 32 * (j // 2 + 1))
            arj, aij, anj = n.ar_r[:, j:j + 1], n.ar_i[:, j:j + 1], n.ar_in[:, j:j + 1]
            v.tensor_scalar(F(0), n.trig(s0), arj, None, ALU.mult)
            v.scalar_tensor_tensor(n.tvalP(j), n.trig(s0 + 1), aij, F(0),
                                   ALU.mult, ALU.add)
            v.tensor_scalar(F(1), n.trig(s0 + 1), arj, None, ALU.mult)
            ctx.inc(v.scalar_tensor_tensor(n.tvalQ(j), n.trig(s0), anj, F(1),
                                           ALU.mult, ALU.add), "sVq")
        ctx.inc(v.tensor_copy(n.mrep[:, 0:512], n.mrep[:, 0:512]), "sV")
    n.block.vector(v6)
    n.ssm_done_sV = ctx.val("sV")


def emit_tbuild(n):
    ctx, d = n.ctx, n.dram
    wd0 = [ctx.val("sWd0"), ctx.val("sWd1")]
    td0 = ctx.val("sTg")

    def sy(sy_):
        for a in range(4):
            for j in range(32):
                it = a * 32 + j
                slot = it % 2
                if it >= 2:
                    ctx.wait(sy_, "sTg", td0 + it - it // 32)
                ctx.inc(sy_.dma_start(
                    out=n.dftbuf[:, 2048 * slot:2048 * slot + 2048],
                    in_=d["csM"][128 * j:128 * j + 128,
                                 2048 * a:2048 * a + 2048]),
                    f"sWd{slot}", 16)
    n.block.sync(sy)

    for a in range(4):
        def t(te, a=a):
            if a == 0:
                ctx.wait(te, "sV", n.ssm_done_sV)
            n.PA.prod_wait(te)
            n.PB.prod_wait(te)
            for j in range(32):
                it = a * 32 + j
                slot = it % 2
                sl = n.dftbuf[:, 2048 * slot:2048 * slot + 2048]
                te.wait_ge(ctx.sems[f"sWd{slot}"], wd0[slot] + 16 * (it // 2 + 1))
                te.matmul(n.psA[:, 0:512], n.tvalP(j), sl[:, 0:512],
                          start=(j == 0), stop=False)
                te.matmul(n.psB[:, 0:512], n.tvalP(j), sl[:, 512:1024],
                          start=(j == 0), stop=False)
                mma = te.matmul(n.psA[:, 0:512], n.tvalQ(j), sl[:, 1024:1536],
                                start=False, stop=(j == 31))
                mmb = te.matmul(n.psB[:, 0:512], n.tvalQ(j), sl[:, 1536:2048],
                                start=False, stop=(j == 31))
                if j < 31:
                    ctx.inc(mmb, "sTg")
            n.PA.prod_done(mma)
            n.PB.prod_done(mmb)
        n.block.tensor(t)

        def v(ve, a=a):
            n.PA.cons_wait(ve)
            n.PA.cons_done(
                ve.tensor_copy(n.Tmat[:, 1024 * a:1024 * a + 512],
                               n.psA[:, 0:512]))
            n.PB.cons_wait(ve)
            n.PB.cons_done(
                ve.tensor_copy(n.Tmat[:, 1024 * a + 512:1024 * a + 1024],
                               n.psB[:, 0:512]))
        n.block.vector(v)

    def v2(ve):
        ctx.inc(ve.tensor_mul(n.Tmat[:, 0:128], n.Tmat[:, 0:128], n.sb_tri[:]), "sV")
    n.block.vector(v2)
    n.tbuild_done_sV = ctx.val("sV")
    n.tbuild_pbc = ctx.val(n.PB.cname)


def emit_prologue(n):
    ctx, d = n.ctx, n.dram
    g0 = [ctx.val("sG0"), ctx.val("sG1")]
    t1a = ctx.val(n.PT1.tname)
    t2a = ctx.val(n.PT2.tname)

    def g(gq):
        for tc in range(NTT):
            if tc >= 2:
                gq.wait_ge(ctx.sems[n.PT1.tname], t1a + 2 * (tc - 1))
                gq.wait_ge(ctx.sems[n.PT2.tname], t2a + 2 * (tc - 1))
            ctx.inc(gq.dma_start(out=n.stage(tc % 2),
                                 in_=d["x0c"][128 * tc:128 * tc + 128, :]),
                    f"sG{tc % 2}", 16)
    n.block.gpsimd(g)

    for tc in range(NTT):
        for grp in range(4):
            res = n.PT1 if grp % 2 == 0 else n.PT2

            def t(te, tc=tc, grp=grp, res=res):
                if tc == 0 and grp == 0:
                    ctx.wait(te, "sV", n.tbuild_done_sV)
                if grp == 0:
                    te.wait_ge(ctx.sems[f"sG{tc % 2}"],
                               g0[tc % 2] + 16 * (tc // 2 + 1))
                res.prod_wait(te)
                for k in range(8):
                    dt_ = grp * 8 + k
                    mm = te.transpose(
                        res.ap[:, 128 * k:128 * k + 128],
                        n.stage(tc % 2)[:, 128 * dt_:128 * dt_ + 128], n.sb_id[:])
                res.prod_done(mm)
            n.block.tensor(t)

            def v(ve, tc=tc, grp=grp, res=res):
                res.cons_wait(ve)
                for k in range(8):
                    dt_ = grp * 8 + k
                    last = ve.tensor_copy(n.A(dt_)[:, 128 * tc:128 * tc + 128],
                                          res.ap[:, 128 * k:128 * k + 128])
                res.cons_done(last)
            n.block.vector(v)

    def vfin(ve):
        ctx.inc(ve.tensor_copy(n.mrep[:, 0:512], n.mrep[:, 0:512]), "sV")
    n.block.vector(vfin)
    x0T_done = ctx.val("sV")

    def so(sq):
        ctx.wait(sq, "sV", x0T_done)
        for dt_ in range(NDT):
            ctx.inc(sq.dma_start(out=d["xs0"][128 * dt_:128 * dt_ + 128, :],
                                 in_=n.A(dt_)), "sO", 16)
    n.block.scalar(so)
    n.xs0_done_sO = ctx.val("sO")


def emit_lnstats(n, X, dbg=None):
    ctx = n.ctx
    ts0 = ctx.val("sTg")

    for dt_ in range(NDT):
        def v(ve, dt_=dt_):
            if dt_ >= 2:
                ctx.wait(ve, "sTg", ts0 + dt_ - 1)
            ctx.inc(ve.tensor_mul(n.sqst[:, TC * (dt_ % 2):TC * (dt_ % 2) + TC],
                                  X(dt_), X(dt_)), "sVq")
        n.block.vector(v)

        def t(te, dt_=dt_):
            if dt_ == 0:
                n.PE.prod_wait(te)
                n.PF.prod_wait(te)
            ctx.wait(te, "sVq")
            first, last = dt_ == 0, dt_ == NDT - 1
            sq = n.sqst[:, TC * (dt_ % 2):TC * (dt_ % 2) + TC]
            te.matmul(n.psE[0:1, 0:512], n.sb_ones_p1[:], X(dt_)[:, 0:512],
                      start=first, stop=last)
            mmE = te.matmul(n.psE[0:1, 512:1024], n.sb_ones_p1[:], X(dt_)[:, 512:1024],
                            start=first, stop=last)
            mm3 = te.matmul(n.psF[0:1, 0:512], n.sb_ones_p1[:], sq[:, 0:512],
                            start=first, stop=last)
            mmF = te.matmul(n.psF[0:1, 512:1024], n.sb_ones_p1[:], sq[:, 512:1024],
                            start=first, stop=last)
            if last:
                n.PE.prod_done(mmE)
                n.PF.prod_done(mmF)
            else:
                ctx.inc(mm3, "sTg")
        n.block.tensor(t)

    def v2(ve):
        n.PE.cons_wait(ve)
        n.PF.cons_wait(ve)
        ve.tensor_copy(n.r_sum[:], n.psE[0:1, :])
        n.PE.cons_done(ve.tensor_copy(n.r_ssq[:], n.psF[0:1, :]))
        n.PF.cons_done(ve.tensor_scalar_mul(n.r_mean[:], n.r_sum[:], 1.0 / D))
        ve.tensor_mul(n.r_rstd[:], n.r_mean[:], n.r_mean[:])
        ve.scalar_tensor_tensor(n.r_var[:], n.r_ssq[:], 1.0 / D, n.r_rstd[:],
                                ALU.mult, ALU.subtract)
        ctx.inc(ve.tensor_scalar_add(n.r_var[:], n.r_var[:], EPS), "sV")
    n.block.vector(v2)

    def sc(s):
        ctx.wait(s, "sV")
        s.activation(n.r_rstd[:], n.r_var[:], AF.Ln)
        s.activation(n.r_rstd[:], n.r_rstd[:], AF.Exp, scale=-0.5)
        ctx.inc(s.copy(n.sspacer[:], n.sspacer[:]), "sS")
    n.block.scalar(sc)
    rstd_done = ctx.val("sS")

    def t2(te):
        ctx.wait(te, "sS", rstd_done)
        n.PE.prod_wait(te)
        te.matmul(n.psE[:, 0:512], n.sb_ones_1p[:], n.r_mean[0:1, 0:512],
                  start=True, stop=True)
        n.PE.prod_done(
            te.matmul(n.psE[:, 512:1024], n.sb_ones_1p[:], n.r_mean[0:1, 512:1024],
                      start=True, stop=True))
        n.PF.prod_wait(te)
        te.matmul(n.psF[:, 0:512], n.sb_ones_1p[:], n.r_rstd[0:1, 0:512],
                  start=True, stop=True)
        n.PF.prod_done(
            te.matmul(n.psF[:, 512:1024], n.sb_ones_1p[:], n.r_rstd[0:1, 512:1024],
                      start=True, stop=True))
    n.block.tensor(t2)

    def v3(ve):
        n.PE.cons_wait(ve)
        n.PE.cons_done(ve.tensor_copy(n.mrep[:], n.psE[:]))
        n.PF.cons_wait(ve)
        n.PF.cons_done(ve.tensor_copy(n.rrep[:], n.psF[:]))
        ctx.inc(ve.tensor_copy(n.mrep[:, 0:512], n.mrep[:, 0:512]), "sV")
    n.block.vector(v3)

    if dbg is not None and ("dbg_m_" + dbg) in n.dram:
        def dv(sq):
            ctx.wait(sq, "sV")
            ctx.inc(sq.dma_start(out=n.dram["dbg_m_" + dbg], in_=n.r_mean[:]), "sO", 16)
            ctx.inc(sq.dma_start(out=n.dram["dbg_r_" + dbg], in_=n.r_rstd[:]), "sO", 16)
            ctx.inc(sq.dma_start(out=n.dram["dbg_s_" + dbg], in_=n.r_ssq[:]), "sO", 16)
        n.block.scalar(dv)


def emit_apply_ln(n, src, dst, wait_sOx=None):
    ctx = n.ctx

    def v(ve):
        if wait_sOx is not None:
            ctx.wait(ve, "sOx", wait_sOx)
        for dt_ in range(NDT):
            sl = n.sqst[:, TC * (dt_ % 2):TC * (dt_ % 2) + TC]
            ve.tensor_tensor(sl, src(dt_), n.mrep[:], ALU.subtract)
            ctx.inc(ve.tensor_tensor(dst(dt_), sl, n.rrep[:], ALU.mult), "sV")
    n.block.vector(v)
    n.xn_done_sV = ctx.val("sV")


def emit_ffn0(n):
    ctx, d = n.ctx, n.dram

    def t1(te):
        ctx.wait(te, "sV", n.xn_done_sV)
        ctx.wait(te, "sG", n.const_sG)
        n.PA.prod_wait(te)
        for tt in range(NTT):
            for dt_ in range(NDT):
                mm = te.matmul(n.psA[:, 8 * tt:8 * tt + 8],
                               n.B(dt_)[:, 128 * tt:128 * tt + 128],
                               n.wrb[:, 8 * dt_:8 * dt_ + 8],
                               start=(dt_ == 0), stop=(dt_ == NDT - 1))
        n.PA.prod_done(mm)
        n.PB.prod_wait(te)
        n.PB.prod_done(te.matmul(n.psB[:, 0:1], n.sb_ones_1p[:], n.sc11[0:1, 1:2],
                                 start=True, stop=True))
    n.block.tensor(t1)

    def v1(ve):
        n.PB.cons_wait(ve)
        n.PB.cons_done(ve.tensor_copy(n.dvec[:], n.psB[:, 0:1]))
        n.PA.cons_wait(ve)
        n.PA.cons_done(ve.tensor_copy(n.lgst[:], n.psA[:, 0:64]))
        vs = SerialEng(ve, n.mrep[:, 0:512])
        for tt in range(NTT):
            lg = n.lgst[:, 8 * tt:8 * tt + 8]
            vs.reduce_max(n.m1b[:, tt:tt + 1], lg, axis=AX.X)
        vs.tensor_scalar_mul(n.nm1b[:], n.m1b[:], -1.0)
        ctx.inc(ve.tensor_copy(n.mrep[:, 0:512], n.mrep[:, 0:512]), "sV")
    n.block.vector(v1)
    r1 = ctx.val("sV")

    def s1(s):
        ctx.wait(s, "sV", r1)
        for tt in range(NTT):
            s.activation(n.ebuf[:, 8 * tt:8 * tt + 8],
                         n.lgst[:, 8 * tt:8 * tt + 8], AF.Exp,
                         bias=n.nm1b[:, tt:tt + 1])
        ctx.inc(s.copy(n.sspacer[:], n.sspacer[:]), "sS")
    n.block.scalar(s1)
    e_done = ctx.val("sS")

    def v2(ve):
        rv = ve
        ctx.wait(ve, "sS", e_done)
        ve = SerialEng(ve, n.mrep[:, 0:512])
        SS = lambda i: n.sscr[:, 64 * i:64 * i + 8]
        for tt in range(NTT):
            lg = n.lgst[:, 8 * tt:8 * tt + 8]
            eb = n.ebuf[:, 8 * tt:8 * tt + 8]
            ve.tensor_scalar(SS(0), lg, n.m1b[:, tt:tt + 1], None, ALU.is_ge)
            ve.scalar_tensor_tensor(SS(1), SS(0), -1e30, lg, ALU.mult, ALU.add)
            ve.reduce_max(n.m2b[:, tt:tt + 1], SS(1), axis=AX.X)
            ve.tensor_scalar(SS(2), lg, n.m2b[:, tt:tt + 1], None, ALU.is_ge)
            ve.tensor_mul(eb, eb, SS(2))
            ve.reduce_sum(n.rcb[:, tt:tt + 1], eb, axis=AX.X)
        ve.reciprocal(n.rcb[:], n.rcb[:])
        for tt in range(NTT):
            eb = n.ebuf[:, 8 * tt:8 * tt + 8]
            ve.tensor_scalar(n.cmbt[:, 8 * tt:8 * tt + 8], eb,
                             n.rcb[:, tt:tt + 1], None, ALU.mult)
        ctx.inc(rv.tensor_copy(n.mrep[:, 0:512], n.mrep[:, 0:512]), "sV")
    n.block.vector(v2)
    cmb_done = ctx.val("sV")

    def t2(te):
        ctx.wait(te, "sV", cmb_done)
        n.PT1.prod_wait(te)
        for tt in range(NTT):
            mm = te.transpose(n.psT1[0:8, 128 * tt:128 * tt + 128],
                              n.cmbt[:, 8 * tt:8 * tt + 8], n.sb_id[:])
        n.PT1.prod_done(mm)
    n.block.tensor(t2)

    def v3(ve):
        n.PT1.cons_wait(ve)
        n.PT1.cons_done(ve.tensor_copy(n.ctall[:], n.psT1[0:8, :]))
        ctx.inc(ve.tensor_copy(n.mrep[:, 0:512], n.mrep[:, 0:512]), "sV")
    n.block.vector(v3)
    ct_done = ctx.val("sV")

    def t3(te):
        ctx.wait(te, "sV", ct_done)
        n.PE.prod_wait(te)
        te.matmul(n.psE[:, 0:512], n.sb_E8[:], n.ctall[:, 0:512], start=True, stop=True)
        n.PE.prod_done(te.matmul(n.psE[:, 512:1024], n.sb_E8[:], n.ctall[:, 512:1024],
                                 start=True, stop=True))
        n.PF.prod_wait(te)
        for dt_ in range(NDT):
            first, last = dt_ == 0, dt_ == NDT - 1
            te.matmul(n.psF[:, 0:512], n.aab[:, 128 * dt_:128 * dt_ + 128],
                      n.B(dt_)[:, 0:512], start=first, stop=last)
            mm = te.matmul(n.psF[:, 512:1024], n.aab[:, 128 * dt_:128 * dt_ + 128],
                           n.B(dt_)[:, 512:1024], start=first, stop=last)
        n.PF.prod_done(mm)
    n.block.tensor(t3)

    def v4(ve):
        n.PE.cons_wait(ve)
        n.PE.cons_done(ve.tensor_copy(n.cexp[:], n.psE[:]))
        n.PF.cons_wait(ve)
        n.PF.cons_done(ve.tensor_copy(n.hbuf[:], n.psF[:]))
        ctx.inc(ve.tensor_mul(n.hbuf[:], n.hbuf[:], n.cexp[:]), "sV")
    n.block.vector(v4)
    hsc_done = ctx.val("sV")

    wl0 = [ctx.val("sWl0"), ctx.val("sWl1")]
    wf0 = ctx.val("sWf")

    def g1(gq):
        for oc in range(NDT):
            if oc == 0:
                ctx.wait(gq, n.PB.cname, n.tbuild_pbc)
            if oc >= 2:
                ctx.wait(gq, "sWf", wf0 + oc - 1)
            ctx.inc(gq.dma_start(out=n.wbuf[:, D * (oc % 2):D * (oc % 2) + D],
                                 in_=d["wbase_blk"][oc]), f"sWl{oc % 2}", 16)
    n.block.gpsimd(g1)

    for oc in range(NDT):
        res = n.PE if oc % 2 == 0 else n.PF

        def t4(te, oc=oc, res=res):
            if oc == 0:
                ctx.wait(te, "sV", hsc_done)
            res.prod_wait(te)
            te.wait_ge(ctx.sems[f"sWl{oc % 2}"], wl0[oc % 2] + 16 * (oc // 2 + 1))
            wb = lambda dt_: n.wbuf[:, D * (oc % 2) + 128 * dt_:D * (oc % 2) + 128 * dt_ + 128]
            for dt_ in range(NDT):
                te.matmul(res.ap[:, 0:512], wb(dt_), n.B(dt_)[:, 0:512],
                          start=(dt_ == 0), stop=False)
                mw = te.matmul(res.ap[:, 512:1024], wb(dt_), n.B(dt_)[:, 512:1024],
                               start=(dt_ == 0), stop=False)
            ctx.inc(mw, "sWf")
            te.matmul(res.ap[:, 0:512], n.blb[:, 128 * oc:128 * oc + 128],
                      n.hsc[:, 0:512], start=False, stop=True)
            res.prod_done(
                te.matmul(res.ap[:, 512:1024], n.blb[:, 128 * oc:128 * oc + 128],
                          n.hsc[:, 512:1024], start=False, stop=True))
        n.block.tensor(t4)

        def v5(ve, oc=oc, res=res):
            res.cons_wait(ve)
            res.cons_done(ve.tensor_scalar(n.A(oc), res.ap[:], n.bbuf[:, oc:oc + 1],
                                           None, ALU.add))
        n.block.vector(v5)

    def v5f(ve):
        ctx.inc(ve.tensor_copy(n.mrep[:, 0:512], n.mrep[:, 0:512]), "sV")
    n.block.vector(v5f)
    n.ffn0_done_sV = ctx.val("sV")


def emit_conv(n):
    ctx = n.ctx

    for I in range(NDT):
        res = n.PE if I % 2 == 0 else n.PF

        def t(te, I=I, res=res):
            if I == 0:
                ctx.wait(te, "sV", n.ffn0_done_sV)
            res.prod_wait(te)
            for J in range(I + 1):
                dd = I - J
                first, last = J == 0, J == I
                te.matmul(res.ap[:, 0:512], n.Tmat[:, 128 * dd:128 * dd + 128],
                          n.B(J)[:, 0:512], start=first, stop=last)
                mm = te.matmul(res.ap[:, 512:1024], n.Tmat[:, 128 * dd:128 * dd + 128],
                               n.B(J)[:, 512:1024], start=first, stop=last)
            res.prod_done(mm)
        n.block.tensor(t)

        def v(ve, I=I, res=res):
            res.cons_wait(ve)
            sl = n.rstage[:, TC * (I % 2):TC * (I % 2) + TC]
            res.cons_done(ve.scalar_tensor_tensor(sl, n.B(I), n.dvec[:], res.ap[:],
                                                  ALU.mult, ALU.add))
            ctx.inc(ve.tensor_tensor(n.A(I), n.A(I), sl, ALU.add), "sV")
        n.block.vector(v)
    n.conv_done_sV = ctx.val("sV")


def emit_x1(n):
    ctx, d = n.ctx, n.dram
    w0 = [ctx.val("sWx0"), ctx.val("sWx1")]
    vx0 = ctx.val("sVx")
    ox0 = ctx.val("sOx")

    def sy(sy_):
        ctx.wait(sy_, "sO", n.xs0_done_sO)
        ctx.wait(sy_, "sV", n.conv_done_sV)
        for dt_ in range(NDT):
            if dt_ >= 2:
                ctx.wait(sy_, "sVx", vx0 + dt_ - 1)
            ctx.inc(sy_.dma_start(out=n.rstage[:, TC * (dt_ % 2):TC * (dt_ % 2) + TC],
                                  in_=d["xs0"][128 * dt_:128 * dt_ + 128, :]),
                    f"sWx{dt_ % 2}", 16)
    n.block.sync(sy)

    def v(ve):
        for dt_ in range(NDT):
            ve.wait_ge(ctx.sems[f"sWx{dt_ % 2}"], w0[dt_ % 2] + 16 * (dt_ // 2 + 1))
            sl = n.rstage[:, TC * (dt_ % 2):TC * (dt_ % 2) + TC]
            ctx.inc(ve.tensor_tensor(n.B(dt_), sl, n.A(dt_), ALU.add), "sVx")
    n.block.vector(v)

    def so(sq):
        for dt_ in range(NDT):
            ctx.wait(sq, "sVx", vx0 + dt_ + 1)
            ctx.inc(sq.dma_start(out=d["x1_out"][128 * dt_:128 * dt_ + 128, :],
                                 in_=n.B(dt_)), "sOx", 16)
    n.block.scalar(so)
    x1out_sOx = ctx.val("sOx")

    emit_lnstats(n, n.B)
    emit_apply_ln(n, n.B, n.B, wait_sOx=x1out_sOx)

    def so2(sq):
        ctx.wait(sq, "sV", n.xn_done_sV)
        for dt_ in range(NDT):
            ctx.inc(sq.dma_start(out=d["xn1_out"][128 * dt_:128 * dt_ + 128, :],
                                 in_=n.B(dt_)), "sOx", 16)
        ctx.wait(sq, "sOx")
    n.block.scalar(so2)


def emit_taps(n, taps, tap_out):
    if not taps:
        def fin(sq):
            for s_ in ["sV", "sT", "sS", "sO", "sOx", "sW", "sG", "sWt0",
                       "sWt1", "sWd0", "sWd1", "sG0", "sG1", "sWl0", "sWl1",
                       "sWx0", "sWx1"]:
                n.ctx.wait(sq, s_)
        n.block.scalar(fin)
        return
    ctx = n.ctx

    def f(sq):
        for s_ in ["sV", "sT", "sS", "sO", "sOx", "sW", "sG", "sVx", "sVq",
                   "sWt0", "sWt1", "sWd0", "sWd1", "sG0", "sG1", "sWl0",
                   "sWl1", "sWx0", "sWx1"]:
            ctx.wait(sq, s_)
        for tname, shape, dt in taps:
            src = resolve_tap(n, tname)
            ctx.inc(sq.dma_start(out=tap_out[tname], in_=src), "sO", 16)
        ctx.wait(sq, "sO")
    n.block.scalar(f)


def resolve_tap(n, tname):
    if tname.startswith("bufA"):
        d = int(tname.split("_")[1])
        return n.A(d)
    if tname.startswith("bufB"):
        d = int(tname.split("_")[1])
        return n.B(d)
    if tname in n.dram:
        return n.dram[tname]
    return getattr(n, tname)[:]



def _prep_launch1_inputs(inputs):
    t = _host_tables()
    x0 = np.asarray(inputs["L0_attn_u"]).reshape(T, D)

    lam_ri = np.asarray(inputs["L0_attn_Lambda_ri"])
    lam = np.concatenate([lam_ri[:, 0], lam_ri[:, 1]]).reshape(128, 1).astype(np.float32)
    vc = np.asarray(inputs["L0_attn_Vc_ri"])
    base = {
        "lam": lam,
        "vc_rT": np.ascontiguousarray(vc[:, :, 0].T.astype(np.float32)),
        "vc_iT": np.ascontiguousarray(vc[:, :, 1].T.astype(np.float32)),
        "bpv": np.asarray(inputs["L0_attn_Bp"]).reshape(64, 1).astype(np.float32),
        "ct_r": np.asarray(inputs["L0_attn_Ct_ri"]).reshape(64, 2)[:, 0:1].copy(),
        "ct_i": np.asarray(inputs["L0_attn_Ct_ri"]).reshape(64, 2)[:, 1:2].copy(),
        "p_r": np.asarray(inputs["L0_attn_p_ri"])[:, 0:1].copy(),
        "p_i": np.asarray(inputs["L0_attn_p_ri"])[:, 1:2].copy(),
        "q_r": np.asarray(inputs["L0_attn_q_ri"])[:, 0:1].copy(),
        "q_i": np.asarray(inputs["L0_attn_q_ri"])[:, 1:2].copy(),
        "dpar": np.asarray(inputs["L0_attn_D"]).reshape(1, 1).astype(np.float32),
        "logstep": np.asarray(inputs["L0_attn_log_step"]).reshape(1, 1).astype(np.float32),
        "b_base": np.asarray(inputs["L0_ffn_b_base"]).reshape(D, 1).astype(np.float32),
        "wrouT": np.ascontiguousarray(np.asarray(inputs["L0_ffn_W_router"]).T.astype(np.float32)),
        "aallT": np.ascontiguousarray(
            np.asarray(inputs["L0_ffn_A_stack"]).reshape(128, D).T.astype(np.float32)),
        "ball": np.ascontiguousarray(
            np.asarray(inputs["L0_ffn_B_stack"]).transpose(0, 2, 1).reshape(128, D).astype(np.float32)),
        "wbase_blk": pack_lhsT(np.ascontiguousarray(
            np.asarray(inputs["L0_ffn_W_base"]).T.astype(np.float32))),
        "t_r_row": t["t_r_row"], "t_i_row": t["t_i_row"],
        "cl_r": t["cl_r"], "cl_i": t["cl_i"], "cA": t["cA"], "sA": t["sA"],
        "csM": t["csM"], "trimask": t["trimask"],
        "id_bf": t["id_bf"], "ones_p1_bf": t["ones_p1_bf"],
        "ones_1p_f32": t["ones_1p_f32"], "E8": t["E8"],
    }
    base["b_base_pk"] = np.ascontiguousarray(base.pop("b_base").reshape(32, 128).T)
    base["wrou_pk"] = np.ascontiguousarray(
        base.pop("wrouT").reshape(32, 128, 8).transpose(1, 0, 2).reshape(128, 256))
    base["aab_pk"] = np.ascontiguousarray(
        base.pop("aallT").reshape(32, 128, 128).transpose(1, 0, 2).reshape(128, 4096))
    in_maps = []
    for c in range(NCORE):
        m = dict(base)
        m["x0c"] = np.ascontiguousarray(x0[_strip_tokens(c), :])
        in_maps.append(m)
    return in_maps






def _gcol(s):
    r = s if s <= 7 else 15 - s
    return 1024 * r + (0 if s <= 7 else 512)


def _ktile_list(s):
    out = []
    for sk in range(s + 1):
        base = _gcol(sk)
        for u in range(4):
            out.append((base + 128 * u, u if sk == s else None))
    return out


def build_launch2(taps=()):
    nc = bass.Bass("TRN2", target_bir_lowering=False)
    dram = {}
    din, dout, dint = _mk_dram_helpers(nc, dram)

    din("xn1g", [8 * D, TC], BF16)
    din("wq_blk", [4, 128, D], F32)
    din("wk_blk", [1, 128, D], F32)
    din("wv_blk", [1, 128, D], F32)
    din("bq", [512, 1], F32)
    din("bk", [128, 1], F32)
    din("bv", [128, 1], F32)
    din("ctabK", [128, T], BF16)
    din("stabK", [128, T], BF16)
    din("ctabQ", [128, T], BF16)
    din("stabQ", [128, T], BF16)
    din("Jmat", [128, 128], BF16)
    din("dmask", [4, 128, 512], BF16)
    din("id_bf", [128, 128], BF16)
    din("ones_p1_bf", [128, 1], BF16)
    din("ones_1p_f32", [1, 128], F32)
    dout("attnout", [512, T], BF16)

    tap_out = {}
    for tname, shape, dt in taps:
        tap_out[tname] = dout("tap_" + tname, list(shape), dt)

    with contextlib.ExitStack() as stack:
        n = types.SimpleNamespace()
        n.nc = nc
        n.dram = dram
        sb = lambda name, shape, dt: stack.enter_context(nc.sbuf_tensor(name, list(shape), dt))
        ps = lambda name, shape, dt: stack.enter_context(nc.psum_tensor(name, list(shape), dt))

        n.Kbuf = sb("Kbuf", [128, T], BF16)
        n.Vtok = sb("Vtok", [128, T], BF16)
        n.wqb = sb("wqb", [128, 4 * D], BF16)
        n.wkb = sb("wkb", [128, D], BF16)
        n.wvb = sb("wvb", [128, D], BF16)
        n.xbig = sb("xbig", [128, NDT * 512], BF16)
        n.tstg = sb("tstg", [128, 4 * 512], BF16)
        n.kst = sb("kst", [128, 512], F32)
        n.vst = sb("vst", [128, 512], BF16)
        n.qro = sb("qro", [128, 4 * 512], BF16)
        n.qtmp = sb("qtmp", [128, 512], F32)
        n.pbuf = sb("pbuf", [128, 4 * 512], BF16)
        n.pxtra = n.pbuf[:, 1536:2048]
        n.sqb = sb("sqb", [128, 512], BF16)
        n.sc2b = sb("sc2b", [1, 32], F32)
        n.rowz = sb("rowz", [1, 512], F32)
        n.sc2 = sb("sc2", [1, 16], F32)
        n.negb = sb("negb", [128, 16], F32)
        n.accden = sb("accden", [128, 512], F32)
        n.rrepA = sb("rrepA", [128, 512], F32)
        n.bqb = sb("bqb", [128, 4], F32)
        n.bkb = sb("bkb", [128, 1], F32)
        n.bvb = sb("bvb", [128, 1], F32)
        n.sb_J = sb("sb_J", [128, 128], BF16)
        n.sb_dm = sb("sb_dm", [128, 4 * 512], BF16)
        n.sb_id = sb("sb_id", [128, 128], BF16)
        n.sb_ones_p1 = sb("sb_ones_p1", [128, 1], BF16)
        n.sb_ones_1p = sb("sb_ones_1p", [1, 128], F32)

        n.psQ = ps("psQ", [128, 1024], F32)
        n.psS1 = ps("psS1", [128, 512], F32)
        n.psS2 = ps("psS2", [128, 512], F32)
        psFD = ps("psFD", [128, 1024], F32)
        n.psAV = psFD[:, 0:512]
        n.psD = psFD[:, 512:1024]
        n.psVT = ps("psVT", [128, 512], BF16)

        ctx = n.ctx = Ctx(nc, stack)
        for s_ in ["sW", "sG", "sO", "sS", "sV", "sT", "sX", "sKV", "sP",
                   "sGt0", "sGt1"]:
            ctx.sem(s_)
        n.PQ = PsRes(ctx, n.psQ)
        n.PS1 = PsRes(ctx, n.psS1)
        n.PS2 = PsRes(ctx, n.psS2)
        n.PAV = PsRes(ctx, n.psAV)
        n.PD = PsRes(ctx, n.psD)
        n.PVT = PsRes(ctx, n.psVT)
        n.fused = False

        with nc.Block() as block:
            n.block = block
            emit_l2_consts_sm(n)
            emit_l2_wload(n)
            emit_l2_kv(n)
            emit_l2_attn(n)
            emit_l2_taps(n, taps, tap_out)
    return nc, dram


def emit_l2_consts_sm(n, with_dm=True):
    ctx, d = n.ctx, n.dram

    def f(sy):
        consts = [(n.sb_J, d["Jmat"]),
                  (n.bkb, d["bk"]), (n.bvb, d["bv"])]
        if not n.fused:
            consts += [(n.sb_id, d["id_bf"]), (n.sb_ones_p1, d["ones_p1_bf"]),
                       (n.sb_ones_1p, d["ones_1p_f32"])]
        for dst, src in consts:
            ctx.inc(sy.dma_start(out=dst[:], in_=src), "sW", 16)
        if n.fused:
            ctx.inc(sy.dma_start(out=n.gidx[:], in_=d["gidx"]), "sW", 16)
        for i in range(4):
            if with_dm:
                ctx.inc(sy.dma_start(out=n.sb_dm[:, 512 * i:512 * i + 512],
                                     in_=d["dmask"][i]), "sW", 16)
            ctx.inc(sy.dma_start(out=n.bqb[:, i:i + 1],
                                 in_=d["bq"][128 * i:128 * i + 128, 0:1]), "sW", 16)
    n.block.sync(f)


def emit_l2_dm(n):
    ctx, d = n.ctx, n.dram

    def f(sy):
        ctx.wait(sy, "sV", n.conv_done_sV)
        for i in range(4):
            ctx.inc(sy.dma_start(out=n.sb_dm[:, 512 * i:512 * i + 512],
                                 in_=d["dmask"][i]), "sW", 16)
    n.block.sync(f)


def emit_l2_wload(n):
    ctx, d = n.ctx, n.dram

    def g(gq):
        if n.fused:
            ctx.wait(gq, "sVx")
            ctx.wait(gq, "sOx")
        ctx.inc(gq.dma_start(out=n.wkb[:], in_=d["wk_blk"][0]), "sG", 16)
        ctx.inc(gq.dma_start(out=n.wvb[:], in_=d["wv_blk"][0]), "sG", 16)
        for i in range(4):
            ctx.inc(gq.dma_start(out=n.wqb[:, D * i:D * i + D],
                                 in_=d["wq_blk"][i]), "sG", 16)
    n.block.gpsimd(g)
    n.c_sW = ctx.val("sW")
    n.c_sG = ctx.val("sG")


def emit_l2_kv(n):
    ctx, d = n.ctx, n.dram
    w0 = ctx.val("sW")
    gt0 = [ctx.val("sGt0"), ctx.val("sGt1")]
    tslot = lambda i: n.tstg[:, 512 * (i % 2):512 * (i % 2) + 512]
    tslot2 = lambda i: n.tstg[:, 1024 + 512 * (i % 2):1024 + 512 * (i % 2) + 512]
    xt = lambda dt_: n.xbig[:, 512 * dt_:512 * dt_ + 512]

    for tc in range(16):
        r, h2 = tc // 2, tc % 2

        def syx(sy_, tc=tc, r=r, h2=h2):
            if tc == 0:
                ctx.wait(sy_, "sW", n.c_sW)
                if n.fused:
                    ctx.wait(sy_, "sC1")
                    ctx.wait(sy_, "sOx")
                    ctx.wait(sy_, "sVx")
            else:
                ctx.wait(sy_, n.PS1.tname)
                ctx.wait(sy_, n.PS2.tname)
            for dt_ in range(NDT):
                ctx.inc(sy_.dma_start(
                    out=xt(dt_),
                    in_=d["xn1g"][D * r + 128 * dt_:D * r + 128 * dt_ + 128,
                                  512 * h2:512 * h2 + 512]), "sW", 16)
        n.block.sync(syx)

        def syt(sy_, tc=tc):
            if tc >= 2:
                ctx.wait(sy_, "sV")
            cs = slice(512 * tc, 512 * tc + 512)
            ctx.inc(sy_.dma_start(out=tslot(tc), in_=d["ctabK"][:, cs]),
                    f"sGt{tc % 2}", 16)
            ctx.inc(sy_.dma_start(out=tslot2(tc), in_=d["stabK"][:, cs]),
                    f"sGt{tc % 2}", 16)
        n.block.gpsimd(syt)

        def t(te, tc=tc):
            if tc == 0:
                ctx.wait(te, "sG", n.c_sG)
                if n.fused:
                    ctx.wait(te, n.PF.cname)
            te.wait_ge(ctx.sems["sW"], w0 + 16 * NDT * (tc + 1))
            n.PS1.prod_wait(te)
            n.PS2.prod_wait(te)
            for dt_ in range(NDT):
                first, last = dt_ == 0, dt_ == NDT - 1
                wk = n.wkb[:, 128 * dt_:128 * dt_ + 128]
                wv = n.wvb[:, 128 * dt_:128 * dt_ + 128]
                mmk = te.matmul(n.psS1[:, 0:512], wk, xt(dt_), start=first, stop=last)
                mmv = te.matmul(n.psS2[:, 0:512], wv, xt(dt_), start=first, stop=last)
            n.PS1.prod_done(mmk)
            n.PS2.prod_done(mmv)
        n.block.tensor(t)

        def v1(ve, tc=tc):
            n.PS1.cons_wait(ve)
            if tc > 0:
                ctx.wait(ve, n.PVT.tname)
                ctx.wait(ve, n.PAV.tname)
            n.PS1.cons_done(
                ve.tensor_scalar(n.kst[:], n.psS1[:, 0:512], n.bkb[:], None, ALU.add))
            n.PS2.cons_wait(ve)
            n.PS2.cons_done(
                ve.tensor_scalar(n.vst[:], n.psS2[:, 0:512], n.bvb[:], None, ALU.add))
            ctx.inc(ve.tensor_copy(n.pbuf[:, 0:512], n.kst[:]), "sV")
        n.block.vector(v1)

        def t2(te, tc=tc):
            ctx.wait(te, "sV")
            n.PAV.prod_wait(te)
            n.PAV.prod_done(
                te.matmul(n.psAV[:, 0:512], n.sb_J[:], n.pbuf[:, 0:512],
                          start=True, stop=True))
        n.block.tensor(t2)

        def v2(ve, tc=tc):
            n.PAV.cons_wait(ve)
            ve.wait_ge(ctx.sems[f"sGt{tc % 2}"], gt0[tc % 2] + 32 * (tc // 2 + 1))
            if tc > 0:
                ctx.wait(ve, n.PD.tname)
            cs = slice(512 * tc, 512 * tc + 512)
            ve.tensor_mul(n.qtmp[:], n.kst[:], tslot(tc))
            n.PAV.cons_done(
                ve.tensor_mul(n.kst[:], n.psAV[:, 0:512], tslot2(tc)))
            ctx.inc(ve.tensor_add(n.Kbuf[:, cs], n.qtmp[:], n.kst[:]), "sV")
            ctx.inc(ve.tensor_mul(n.sqb[:], n.Kbuf[:, cs], n.Kbuf[:, cs]), "sV")
        n.block.vector(v2)

        def t3(te, tc=tc):
            ctx.wait(te, "sV")
            n.PD.prod_wait(te)
            n.PD.prod_done(
                te.matmul(n.psD[0:1, 0:512], n.sb_ones_p1[:], n.sqb[:],
                          start=True, stop=True))
            n.PVT.prod_wait(te)
            for i in range(4):
                mm = te.transpose(n.psVT[:, 128 * i:128 * i + 128],
                                  n.vst[:, 128 * i:128 * i + 128], n.sb_id[:])
            n.PVT.prod_done(mm)
        n.block.tensor(t3)

        def v3(ve, tc=tc):
            n.PD.cons_wait(ve)
            n.PD.cons_done(ve.tensor_copy(n.rowz[:], n.psD[0:1, 0:512]))
            ve.reduce_max(n.sc2b[0:1, tc:tc + 1], n.rowz[:], axis=AX.X)
            n.PVT.cons_wait(ve)
            for i in range(4):
                kt = 4 * tc + i
                last = ve.tensor_copy(n.Vtok[:, 128 * kt:128 * kt + 128],
                                      n.psVT[:, 128 * i:128 * i + 128])
            n.PVT.cons_done(last)
        n.block.vector(v3)

    def v4(ve):
        ve.tensor_copy(n.rrepA[:], n.rrepA[:])
        ve.reduce_max(n.sc2[0:1, 0:1], n.sc2b[0:1, 0:16], axis=AX.X)
        ctx.inc(ve.tensor_copy(n.rrepA[:], n.rrepA[:]), "sV")
    n.block.vector(v4)

    def s1(s):
        ctx.wait(s, "sV")
        s.activation(n.sc2[0:1, 2:4], n.sc2[0:1, 0:2], AF.Sqrt)
        ctx.inc(s.copy(n.sspacer[:], n.sspacer[:]), "sS")
    n.block.scalar(s1)
    n.kmax_sS = ctx.val("sS")


def emit_l2_attn(n):
    ctx = n.ctx
    n.onesf = n.negb[:, 8:9]

    def vinit(ve):
        ctx.inc(ve.memset(n.onesf[:], 1.0), "sV")
    n.block.vector(vinit)
    for s in range(16):
        emit_l2_strip(n, s)


def emit_l2_strip(n, s):
    ctx, d = n.ctx, n.dram
    gc = _gcol(s)
    kts = _ktile_list(s)
    rblk = s if s <= 7 else 15 - s
    coff = 0 if s <= 7 else 512

    w0 = ctx.val("sW")
    xt = lambda dt_: n.xbig[:, 512 * dt_:512 * dt_ + 512]

    def syq(sy_):
        ctx.wait(sy_, n.PS1.tname)
        ctx.wait(sy_, n.PS2.tname)
        ctx.wait(sy_, n.PQ.tname)
        for dt_ in range(NDT):
            ctx.inc(sy_.dma_start(
                out=xt(dt_),
                in_=d["xn1g"][D * rblk + 128 * dt_:D * rblk + 128 * dt_ + 128,
                              coff:coff + 512]), "sW", 16)
    n.block.sync(syq)

    def syt(sy_):
        ctx.wait(sy_, "sV")
        cs = slice(gc, gc + 512)
        ctx.inc(sy_.dma_start(out=n.tstg[:, 0:512], in_=d["ctabQ"][:, cs]), "sG", 16)
        ctx.inc(sy_.dma_start(out=n.tstg[:, 512:1024], in_=d["stabQ"][:, cs]), "sG", 16)
    n.block.gpsimd(syt)

    for pas in range(2):
        def tq(te, pas=pas):
            if pas == 0:
                te.wait_ge(ctx.sems["sW"], w0 + 16 * NDT)
            n.PQ.prod_wait(te)
            for dt_ in range(NDT):
                first, last = dt_ == 0, dt_ == NDT - 1
                for qi in range(2):
                    qh = 2 * pas + qi
                    mm = te.matmul(n.psQ[:, 512 * qi:512 * qi + 512],
                                   n.wqb[:, D * qh + 128 * dt_:D * qh + 128 * dt_ + 128],
                                   xt(dt_), start=first, stop=last)
            n.PQ.prod_done(mm)
        n.block.tensor(tq)

        def vq(ve, pas=pas):
            n.PQ.cons_wait(ve)
            if pas == 0:
                ctx.wait(ve, "sG")
                ctx.wait(ve, "sO")
            for qi in range(2):
                qh = 2 * pas + qi
                ve.tensor_scalar(n.qtmp[:], n.psQ[:, 512 * qi:512 * qi + 512],
                                 n.bqb[:, qh:qh + 1], None, ALU.add)
                last = ve.tensor_copy(n.qro[:, 512 * qh:512 * qh + 512], n.qtmp[:])
            n.PQ.cons_done(last)
            ctx.inc(ve.tensor_copy(n.rrepA[:], n.rrepA[:]), "sV")
        n.block.vector(vq)

    for qh in range(4):
        def tj(te, qh=qh):
            ctx.wait(te, "sV")
            n.PS1.prod_wait(te)
            n.PS1.prod_done(
                te.matmul(n.psS1[:, 0:512], n.sb_J[:],
                          n.qro[:, 512 * qh:512 * qh + 512], start=True, stop=True))
        n.block.tensor(tj)

        def vj(ve, qh=qh):
            n.PS1.cons_wait(ve)
            sl = n.qro[:, 512 * qh:512 * qh + 512]
            ve.tensor_mul(n.qtmp[:], sl, n.tstg[:, 0:512])
            n.PS1.cons_done(ve.tensor_mul(n.kst[:], n.psS1[:, 0:512],
                                          n.tstg[:, 512:1024]))
            ve.tensor_add(sl, n.qtmp[:], n.kst[:])
            ctx.inc(ve.tensor_mul(n.sqb[:], sl, sl), "sV")
        n.block.vector(vj)

        def tb(te, qh=qh):
            ctx.wait(te, "sV")
            n.PD.prod_wait(te)
            n.PD.prod_done(
                te.matmul(n.psD[0:1, 0:512], n.sb_ones_p1[:], n.sqb[:],
                          start=True, stop=True))
        n.block.tensor(tb)

        def vb(ve, qh=qh):
            n.PD.cons_wait(ve)
            n.PD.cons_done(ve.tensor_copy(n.rowz[:], n.psD[0:1, 0:512]))
            ve.reduce_max(n.sc2[0:1, 8 + qh:9 + qh], n.rowz[:], axis=AX.X)
            ctx.inc(ve.tensor_copy(n.rrepA[:], n.rrepA[:]), "sV")
        n.block.vector(vb)

    def sball(sc):
        ctx.wait(sc, "sV")
        if s == 0:
            ctx.wait(sc, "sS", n.kmax_sS)
        sc.activation(n.sc2[0:1, 12:16], n.sc2[0:1, 8:12], AF.Sqrt)
        ctx.inc(sc.copy(n.sspacer[:], n.sspacer[:]), "sS")
    n.block.scalar(sball)

    def vball(ve):
        ctx.wait(ve, "sS")
        ve.tensor_scalar(n.sc2b[0:1, 24:28], n.sc2[0:1, 12:16],
                         n.sc2[0:1, 2:3], -1.0, ALU.mult, ALU.mult)
        ctx.inc(ve.tensor_copy(n.rrepA[:], n.rrepA[:]), "sV")
    n.block.vector(vball)

    def tneg(te):
        ctx.wait(te, "sV")
        n.PD.prod_wait(te)
        n.PD.prod_done(
            te.matmul(n.psD[:, 0:4], n.sb_ones_1p[:], n.sc2b[0:1, 24:28],
                      start=True, stop=True))
    n.block.tensor(tneg)

    def vneg(ve):
        n.PD.cons_wait(ve)
        n.PD.cons_done(ve.tensor_copy(n.negb[:, 0:4], n.psD[:, 0:4]))
        ctx.inc(ve.tensor_copy(n.rrepA[:], n.rrepA[:]), "sV")
    n.block.vector(vneg)
    n.negb_sV = ctx.val("sV")

    prev_drain = None
    for qh in range(4):
        tiles, drain = emit_l2_kloop(n, s, qh, kts)
        if prev_drain is not None:
            prev_drain()
        tiles()
        prev_drain = drain
    prev_drain()


def emit_l2_kloop(n, s, qh, kts):
    ctx, d = n.ctx, n.dram
    nk = len(kts)
    pslots = 4
    look = 3
    p0 = ctx.val("sP")
    ca0 = ctx.val(n.PS1.cname)
    cb0 = ctx.val(n.PS2.cname)
    ta0 = ctx.val(n.PS1.tname)
    tb0 = ctx.val(n.PS2.tname)
    pe_t0 = ctx.val(n.PE.tname)
    pe_c0 = ctx.val(n.PE.cname)
    sS0 = ctx.val("sS")
    slots = [n.pbuf[:, 0:512], n.pbuf[:, 512:1024], n.pbuf[:, 1024:1536],
             n.pxtra[:, 0:512]]
    pslot = lambda i: slots[i % pslots]
    banks = [n.psS1[:, 0:512], n.psS2[:, 0:512],
             n.psE[:, 0:512], n.psE[:, 512:1024]]
    bank = lambda i: banks[i % 4]
    prod_names = (n.PS1.tname, n.PS2.tname, n.PE.tname, n.PE.tname)
    cons_names = (n.PS1.cname, n.PS2.cname, n.PE.cname, n.PE.cname)
    cons_base = (ca0, cb0, pe_c0, pe_c0)

    def prod_val(i):
        b, pos = i % 4, i // 4
        return (ta0 + pos + 1, tb0 + pos + 1,
                pe_t0 + 2 * pos + 1, pe_t0 + 2 * pos + 2)[b]

    def cons_val(i):
        b, pos = i % 4, i // 4
        return (ca0 + pos + 1, cb0 + pos + 1,
                pe_c0 + 2 * pos + 1, pe_c0 + 2 * pos + 2)[b]

    def tS(te, j):
        kc = kts[j][0]
        if j < 4:
            te.wait_ge(ctx.sems[cons_names[j % 4]], cons_base[j % 4])
        else:
            te.wait_ge(ctx.sems[cons_names[j % 4]], cons_val(j - 4))
        ctx.inc(te.matmul(bank(j), n.Kbuf[:, kc:kc + 128],
                          n.qro[:, 512 * qh:512 * qh + 512],
                          start=True, stop=True), prod_names[j % 4])

    def tpre(te):
        ctx.wait(te, "sV", n.negb_sV)
        for j in range(min(look, nk)):
            tS(te, j)
    n.block.tensor(tpre)

    def sE_all(sc):
        for i in range(nk):
            sc.wait_ge(ctx.sems[prod_names[i % 4]], prod_val(i))
            if i >= pslots:
                sc.wait_ge(ctx.sems["sP"], p0 + (i - pslots + 1))
            ctx.inc(sc.activation(pslot(i), bank(i), AF.Exp,
                                  bias=n.negb[:, qh:qh + 1]), cons_names[i % 4])
        ctx.inc(sc.copy(n.sspacer[:], n.sspacer[:]), "sS")
    n.block.scalar(sE_all)

    def expland(eng, i):
        j = i + 1
        if j < nk:
            eng.wait_ge(ctx.sems[cons_names[j % 4]], cons_val(j))
        else:
            eng.wait_ge(ctx.sems["sS"], sS0 + 1)

    def emit_tiles():
        for i, (kc, dm) in enumerate(kts):
            def vAcc(ve, i=i, dm=dm):
                expland(ve, i)
                sl = pslot(i)
                if dm is not None:
                    ve.tensor_mul(sl, sl, n.sb_dm[:, 512 * dm:512 * dm + 512])
                if i == 0:
                    acc_i = ve.tensor_copy(n.accden[:], sl)
                else:
                    acc_i = ve.tensor_tensor(n.accden[:], n.accden[:], sl,
                                             ALU.add)
                ctx.inc(acc_i, "sP")
            n.block.vector(vAcc)

            def tMain(te, i=i, kc=kc, dm=dm):
                if dm is not None:
                    te.wait_ge(ctx.sems["sP"], p0 + i + 1)
                else:
                    expland(te, i)
                first, last = i == 0, i == nk - 1
                sl = pslot(i)
                if first:
                    n.PAV.prod_wait(te)
                mm = te.matmul(n.psAV[:, 0:512], n.Vtok[:, kc:kc + 128], sl,
                               start=first, stop=last)
                if last:
                    n.PAV.prod_done(mm)
                if i + look < nk:
                    tS(te, i + look)
            n.block.tensor(tMain)

    def emit_drain():
        def tD(te):
            te.wait_ge(ctx.sems["sP"], p0 + nk)
            n.PD.prod_wait(te)
            n.PD.prod_done(te.matmul(n.psD[0:1, 0:512], n.onesf[:],
                                     n.accden[:], start=True, stop=True))
        n.block.tensor(tD)

        def vE(ve):
            n.PD.cons_wait(ve)
            n.PD.cons_done(ve.tensor_copy(n.rowz[:], n.psD[0:1, 0:512]))
            ve.reciprocal(n.rowz[:], n.rowz[:])
            ctx.inc(ve.tensor_copy(n.rrepA[:], n.rrepA[:]), "sV")
        n.block.vector(vE)

        def tR(te):
            ctx.wait(te, "sV")
            n.PD.prod_wait(te)
            n.PD.prod_done(te.matmul(n.psD[:, 0:512], n.sb_ones_1p[:],
                                     n.rowz[:], start=True, stop=True))
        n.block.tensor(tR)

        def vF(ve):
            n.PD.cons_wait(ve)
            n.PD.cons_done(ve.tensor_copy(n.rrepA[:], n.psD[:, 0:512]))
            n.PAV.cons_wait(ve)
            n.PAV.cons_done(
                ve.tensor_mul(n.sqb[:], n.psAV[:, 0:512], n.rrepA[:]))
            ctx.inc(ve.tensor_copy(n.qtmp[:, 0:512], n.qtmp[:, 0:512]), "sV")
        n.block.vector(vF)

        if not n.fused:
            def so(sq):
                ctx.wait(sq, "sV")
                gcs = _gcol(s)
                ctx.inc(sq.dma_start(
                    out=n.dram["attnout"][128 * qh:128 * qh + 128,
                                          gcs:gcs + 512],
                    in_=n.sqb[:]), "sO", 16)
            n.block.scalar(so)
        else:
            def tT(te):
                ctx.wait(te, "sV")
                n.PT1.prod_wait(te)
                for i in range(4):
                    mm = te.transpose(n.psT1[:, 128 * i:128 * i + 128],
                                      n.sqb[:, 128 * i:128 * i + 128],
                                      n.sb_id[:])
                n.PT1.prod_done(mm)
            n.block.tensor(tT)

            def vT(ve):
                n.PT1.cons_wait(ve)
                ctx.wait(ve, "sO")
                n.PT1.cons_done(ve.tensor_copy(n.tstage[:], n.psT1[:, 0:512]))
                ctx.inc(ve.tensor_copy(n.rrepA[:], n.rrepA[:]), "sV")
            n.block.vector(vT)

            def soT(sq):
                ctx.wait(sq, "sV")
                gcs = _gcol(s)
                for i in range(4):
                    ctx.inc(sq.dma_start(
                        out=n.dram["cc2_in"][gcs + 128 * i:gcs + 128 * i + 128,
                                             128 * qh:128 * qh + 128],
                        in_=n.tstage[:, 128 * i:128 * i + 128]), "sO", 16)
                    if "attdbg" in n.dram:
                        ctx.inc(sq.dma_start(
                            out=n.dram["attdbg"][gcs + 128 * i:gcs + 128 * i + 128,
                                                 128 * qh:128 * qh + 128],
                            in_=n.tstage[:, 128 * i:128 * i + 128]), "sO", 16)
            n.block.scalar(soT)

    return emit_tiles, emit_drain


def emit_l2_taps(n, taps, tap_out):
    ctx = n.ctx

    def f(sq):
        for s_ in ["sV", "sS", "sO", "sW", "sG", "sGt0", "sGt1"]:
            ctx.wait(sq, s_)
        for tname, shape, dt in taps:
            src = resolve_tap(n, tname)
            ctx.inc(sq.dma_start(out=tap_out[tname], in_=src), "sO", 16)
        ctx.wait(sq, "sO")
    n.block.scalar(f)



def build_launch3(taps=()):
    nc = bass.Bass("TRN2", target_bir_lowering=False)
    dram = {}
    din, dout, dint = _mk_dram_helpers(nc, dram)

    din("x1c", [D, TC], BF16)
    din("xn1c", [D, TC], BF16)
    din("attTc", [D, TC], BF16)
    din("w12_blk", [64, 128, D], F32)
    din("b12_pk", [128, 64], F32)
    din("w3_blk", [NDT, 128, D], F32)
    din("b3_pk", [128, 32], F32)
    din("wo_blk", [NDT, 128, D], F32)
    din("bo_pk", [128, 32], F32)
    din("id_bf", [128, 128], BF16)
    dout("y", [TC, D], F32)

    tap_out = {}
    for tname, shape, dt in taps:
        tap_out[tname] = dout("tap_" + tname, list(shape), dt)

    with contextlib.ExitStack() as stack:
        n = types.SimpleNamespace()
        n.nc = nc
        n.dram = dram
        sb = lambda name, shape, dt: stack.enter_context(nc.sbuf_tensor(name, list(shape), dt))
        ps = lambda name, shape, dt: stack.enter_context(nc.psum_tensor(name, list(shape), dt))

        n.bufA = sb("bufA", [128, NDT * TC], BF16)
        n.bufB = sb("bufB", [128, NDT * TC], BF16)
        n.A = lambda d_: n.bufA[:, d_ * TC:(d_ + 1) * TC]
        n.B = lambda d_: n.bufB[:, d_ * TC:(d_ + 1) * TC]
        n.wbuf = sb("wbuf", [128, 4 * D], BF16)
        n.silu = sb("silu", [128, TC], BF16)
        n.rstage = sb("rstage", [128, 2 * TC], BF16)
        n.ostage = sb("ostage", [128, 2 * D], F32)
        n.b12b = sb("b12b", [128, 64], F32)
        n.b3b = sb("b3b", [128, NDT], F32)
        n.bob = sb("bob", [128, NDT], F32)
        n.sb_id = sb("sb_id", [128, 128], BF16)
        n.sspacer = n.b12b[0:1, 0:64]

        n.psE = ps("psE", [128, 1024], F32)
        n.psF = ps("psF", [128, 1024], F32)
        n.psT1 = ps("psT1", [128, 1024], BF16)
        n.psT2 = ps("psT2", [128, 1024], BF16)

        ctx = n.ctx = Ctx(nc, stack)
        for s_ in ["sW", "sG", "sO", "sS", "sV", "sT", "sWl", "sVx",
                   "sWl0", "sWl1", "sWx0", "sWx1", "sGg0", "sGg1"]:
            ctx.sem(s_)
        n.PE = PsRes(ctx, n.psE)
        n.PF = PsRes(ctx, n.psF)
        n.PT1 = PsRes(ctx, n.psT1)
        n.PT2 = PsRes(ctx, n.psT2)

        n.fused = False
        with nc.Block() as block:
            n.block = block
            emit_l3_consts(n)
            emit_l3(n, taps, tap_out)
    return nc, dram


def emit_l3_consts(n):
    ctx, d = n.ctx, n.dram

    def consts(sy):
        if "id_bf" in d and not n.fused:
            ctx.inc(sy.dma_start(out=n.sb_id[:], in_=d["id_bf"]), "sW", 16)
        ctx.inc(sy.dma_start(out=n.b12b[:], in_=d["b12_pk"]), "sW", 16)
        ctx.inc(sy.dma_start(out=n.b3b[:], in_=d["b3_pk"]), "sW", 16)
        ctx.inc(sy.dma_start(out=n.bob[:], in_=d["bo_pk"]), "sW", 16)
    n.block.sync(consts)


def emit_l3(n, taps, tap_out, l2done=None):
    ctx, d = n.ctx, n.dram
    paranoid = getattr(n, "paranoid", set()) or set()

    def loadxn(sy):
        if n.fused:
            for nm in (n.PS1.tname, n.PS2.tname, n.PE.tname, n.PAV.tname):
                ctx.wait(sy, nm, l2done[nm])
        for dt_ in range(NDT):
            ctx.inc(sy.dma_start(out=n.A(dt_),
                                 in_=d["xn1c"][128 * dt_:128 * dt_ + 128, :]), "sW", 16)
    n.block.sync(loadxn)
    n.c_sW = ctx.val("sW")
    if "l3a" in paranoid:
        emit_barrier(n, "l3a")

    wl0 = getattr(n, "wl0_pre", None) if n.fused else None
    if wl0 is None:
        wl0 = [ctx.val("sWl0"), ctx.val("sWl1")]
    f10 = ctx.val(n.PF.tname)
    wslot = lambda mc: n.wbuf[:, 2 * D * (mc % 2):2 * D * (mc % 2) + 2 * D]

    def g1_part(mcs):
        def g1(gq):
            for mc in mcs:
                if mc == 0 and n.fused:
                    ctx.wait(gq, "sP", l2done["sP"])
                if mc >= 2:
                    gq.wait_ge(ctx.sems[n.PF.tname], f10 + mc - 1)
                ctx.inc(gq.dma_start(out=wslot(mc)[:, 0:D],
                                     in_=d["w12_blk"][mc]),
                        f"sWl{mc % 2}", 16)
                ctx.inc(gq.dma_start(out=wslot(mc)[:, D:2 * D],
                                     in_=d["w12_blk"][32 + mc]),
                        f"sWl{mc % 2}", 16)
        return g1
    if n.fused:
        n.block.gpsimd(g1_part(range(2, NDT)))
    else:
        n.block.gpsimd(g1_part(range(NDT)))

    for mc in range(NDT):
        def t1(te, mc=mc):
            if mc == 0:
                ctx.wait(te, "sW", n.c_sW)
                if n.fused:
                    ctx.wait(te, n.PAV.cname, l2done[n.PAV.cname])
                    ctx.wait(te, n.PD.cname, l2done[n.PD.cname])
            te.wait_ge(ctx.sems[f"sWl{mc % 2}"], wl0[mc % 2] + 32 * (mc // 2 + 1))
            n.PE.prod_wait(te)
            n.PF.prod_wait(te)
            ws = wslot(mc)
            for dt_ in range(NDT):
                first, last = dt_ == 0, dt_ == NDT - 1
                w1 = ws[:, 128 * dt_:128 * dt_ + 128]
                w2 = ws[:, D + 128 * dt_:D + 128 * dt_ + 128]
                mm1a = te.matmul(n.psE[:, 0:512], w1, n.A(dt_)[:, 0:512],
                                 start=first, stop=last)
                mm1b = te.matmul(n.psE[:, 512:1024], w1, n.A(dt_)[:, 512:1024],
                                 start=first, stop=last)
                mm2a = te.matmul(n.psF[:, 0:512], w2, n.A(dt_)[:, 0:512],
                                 start=first, stop=last)
                mm2b = te.matmul(n.psF[:, 512:1024], w2, n.A(dt_)[:, 512:1024],
                                 start=first, stop=last)
            n.PE.prod_done(mm1b)
            n.PF.prod_done(mm2b)
        n.block.tensor(t1)

        def s1(sc, mc=mc):
            n.PE.cons_wait(sc)
            sc.activation(n.silu[:], n.psE[:], AF.Silu, bias=n.b12b[:, mc:mc + 1])
            n.PE.cons_done(sc.copy(n.sspacer[:], n.sspacer[:]))
        n.block.scalar(s1)

        def v1(ve, mc=mc):
            ctx.wait(ve, n.PE.cname)
            n.PF.cons_wait(ve)
            n.PF.cons_done(
                ve.scalar_tensor_tensor(n.B(mc), n.psF[:], n.b12b[:, 32 + mc:33 + mc],
                                        n.silu[:], ALU.add, ALU.mult))
        n.block.vector(v1)
    if "l3ffn" in paranoid:
        emit_barrier(n, "l3ffn")

    e20 = ctx.val(n.PE.tname)
    f20 = ctx.val(n.PF.tname)
    wl3 = [ctx.val("sWl0"), ctx.val("sWl1")]
    w3slot = lambda oc: n.wbuf[:, 2 * D * (oc % 2):2 * D * (oc % 2) + D]

    def g2(gq):
        for oc in range(NDT):
            if oc < 2:
                gq.wait_ge(ctx.sems[n.PE.tname], e20)
                gq.wait_ge(ctx.sems[n.PF.tname], f20)
            elif oc % 2 == 0:
                gq.wait_ge(ctx.sems[n.PE.tname], e20 + oc // 2)
            else:
                gq.wait_ge(ctx.sems[n.PF.tname], f20 + (oc - 1) // 2)
            ctx.inc(gq.dma_start(out=w3slot(oc), in_=d["w3_blk"][oc]),
                    f"sWl{oc % 2}", 16)
    n.block.gpsimd(g2)

    for oc in range(NDT):
        res = n.PE if oc % 2 == 0 else n.PF

        def t2(te, oc=oc, res=res):
            te.wait_ge(ctx.sems[f"sWl{oc % 2}"], wl3[oc % 2] + 16 * (oc // 2 + 1))
            res.prod_wait(te)
            ws = w3slot(oc)
            for mc in range(NDT):
                first, last = mc == 0, mc == NDT - 1
                mma = te.matmul(res.ap[:, 0:512], ws[:, 128 * mc:128 * mc + 128],
                                n.B(mc)[:, 0:512], start=first, stop=last)
                mmb = te.matmul(res.ap[:, 512:1024], ws[:, 128 * mc:128 * mc + 128],
                                n.B(mc)[:, 512:1024], start=first, stop=last)
            res.prod_done(mmb)
        n.block.tensor(t2)

        def v2(ve, oc=oc, res=res):
            res.cons_wait(ve)
            res.cons_done(ve.tensor_scalar(n.A(oc), res.ap[:], n.b3b[:, oc:oc + 1],
                                           None, ALU.add))
        n.block.vector(v2)
    acc_sV = ctx.val(n.PE.cname) + ctx.val(n.PF.cname)
    if "l3w3" in paranoid:
        emit_barrier(n, "l3w3")

    t2e = ctx.val(n.PE.tname)
    t2f = ctx.val(n.PF.tname)

    if not n.fused:
        def syA(sy_):
            sy_.wait_ge(ctx.sems[n.PE.tname], t2e)
            sy_.wait_ge(ctx.sems[n.PF.tname], t2f)
            for dt_ in range(NDT):
                ctx.inc(sy_.dma_start(out=n.B(dt_),
                                      in_=d["attTc"][128 * dt_:128 * dt_ + 128, :]),
                        "sW", 16)
        n.block.sync(syA)
        attT_sW = ctx.val("sW")
    else:
        def gg(gq):
            ctx.wait(gq, "sC2")
            gq.wait_ge(ctx.sems[n.PE.tname], t2e)
            gq.wait_ge(ctx.sems[n.PF.tname], t2f)
            for rr in range(NCORE):
                if rr >= 2:
                    ctx.wait(gq, "sTgg", rr - 1)
                ctx.inc(gq.indirect_dma_start(
                    n.gslot(rr)[:, 0:4096], None,
                    d["cc2_out"],
                    bass.IndirectOffsetOnAxis(n.gidx[:, rr:rr + 1], 0)),
                    f"sGg{rr % 2}", 16)
        n.block.gpsimd(gg)
        gg0 = [ctx.val("sGg0") - 16 * (NCORE // 2), ctx.val("sGg1") - 16 * (NCORE // 2)]

        for rr in range(NCORE):
            res = n.PT1 if rr % 2 == 0 else n.PT2
            for j in range(8):
                def tg(te, rr=rr, j=j, res=res):
                    if j == 0:
                        te.wait_ge(ctx.sems[f"sGg{rr % 2}"],
                                   gg0[rr % 2] + 16 * (rr // 2 + 1))
                    res.prod_wait(te)
                    for k in range(4):
                        mm = te.transpose(
                            res.ap[:, 128 * k:128 * k + 128],
                            n.gslot(rr)[:, 512 * j + 128 * k:512 * j + 128 * k + 128],
                            n.sb_id[:])
                    res.prod_done(mm)
                n.block.tensor(tg)

                def vg(ve, rr=rr, j=j, res=res):
                    res.cons_wait(ve)
                    for k in range(4):
                        dt_ = 4 * rr + k
                        last = ve.tensor_copy(
                            n.bufB[:, dt_ * TC + j:dt_ * TC + j + 1017:8],
                            res.ap[:, 128 * k:128 * k + 128])
                    res.cons_done(last)
                    if j == 7:
                        ctx.inc(ve.tensor_copy(n.rstage[:, 0:512],
                                               n.rstage[:, 0:512]), "sTgg")
                n.block.vector(vg)
        attT_sW = ctx.val("sW")
    if "l3gather" in paranoid:
        emit_barrier(n, "l3gather")

    e30 = ctx.val(n.PE.tname)
    f30 = ctx.val(n.PF.tname)
    wl4 = [ctx.val("sWl0"), ctx.val("sWl1")]

    def g3(gq):
        for oc in range(NDT):
            if oc < 2:
                gq.wait_ge(ctx.sems[n.PE.tname], e30)
                gq.wait_ge(ctx.sems[n.PF.tname], f30)
            elif oc % 2 == 0:
                gq.wait_ge(ctx.sems[n.PE.tname], e30 + oc // 2)
            else:
                gq.wait_ge(ctx.sems[n.PF.tname], f30 + (oc - 1) // 2)
            ctx.inc(gq.dma_start(out=w3slot(oc), in_=d["wo_blk"][oc]),
                    f"sWl{oc % 2}", 16)
    n.block.gpsimd(g3)

    for oc in range(NDT):
        res = n.PE if oc % 2 == 0 else n.PF

        def t3(te, oc=oc, res=res):
            if oc == 0:
                if n.fused:
                    ctx.wait(te, "sTgg")
                else:
                    te.wait_ge(ctx.sems["sW"], attT_sW)
            te.wait_ge(ctx.sems[f"sWl{oc % 2}"], wl4[oc % 2] + 16 * (oc // 2 + 1))
            res.prod_wait(te)
            ws = w3slot(oc)
            for mc in range(NDT):
                first, last = mc == 0, mc == NDT - 1
                mma = te.matmul(res.ap[:, 0:512], ws[:, 128 * mc:128 * mc + 128],
                                n.B(mc)[:, 0:512], start=first, stop=last)
                mmb = te.matmul(res.ap[:, 512:1024], ws[:, 128 * mc:128 * mc + 128],
                                n.B(mc)[:, 512:1024], start=first, stop=last)
            res.prod_done(mmb)
        n.block.tensor(t3)

        def v3(ve, oc=oc, res=res):
            res.cons_wait(ve)
            ve.tensor_scalar(n.rstage[:, 0:TC], res.ap[:], n.bob[:, oc:oc + 1],
                             None, ALU.add)
            res.cons_done(ve.tensor_add(n.A(oc), n.A(oc), n.rstage[:, 0:TC]))
        n.block.vector(v3)
    wo_done_sV = ctx.val(n.PE.cname)
    if "l3wo" in paranoid:
        emit_barrier(n, "l3wo")

    w5 = [ctx.val("sWx0"), ctx.val("sWx1")]
    vx0 = ctx.val("sVx")

    def syx(sy_):
        ctx.wait(sy_, n.PE.cname)
        ctx.wait(sy_, n.PF.cname)
        for dt_ in range(NDT):
            if dt_ >= 2:
                ctx.wait(sy_, "sVx", vx0 + dt_ - 1)
            ctx.inc(sy_.dma_start(out=n.rstage[:, TC * (dt_ % 2):TC * (dt_ % 2) + TC],
                                  in_=d["x1c"][128 * dt_:128 * dt_ + 128, :]),
                    f"sWx{dt_ % 2}", 16)
    n.block.sync(syx)

    def vx(ve):
        for dt_ in range(NDT):
            ve.wait_ge(ctx.sems[f"sWx{dt_ % 2}"], w5[dt_ % 2] + 16 * (dt_ // 2 + 1))
            sl = n.rstage[:, TC * (dt_ % 2):TC * (dt_ % 2) + TC]
            ctx.inc(ve.tensor_add(n.A(dt_), n.A(dt_), sl), "sVx")
    n.block.vector(vx)
    x2_sVx = ctx.val("sVx")

    for tcb in range(NTT):
        for grp in range(4):
            res = n.PT1 if grp % 2 == 0 else n.PT2

            def tt(te, tcb=tcb, grp=grp, res=res):
                if tcb == 0 and grp == 0:
                    ctx.wait(te, "sVx", x2_sVx)
                res.prod_wait(te)
                for k in range(8):
                    dt_ = grp * 8 + k
                    mm = te.transpose(res.ap[:, 128 * k:128 * k + 128],
                                      n.A(dt_)[:, 128 * tcb:128 * tcb + 128],
                                      n.sb_id[:])
                res.prod_done(mm)
            n.block.tensor(tt)

            def tv(ve, tcb=tcb, grp=grp, res=res):
                res.cons_wait(ve)
                if grp == 0:
                    ctx.wait(ve, "sO")
                for k in range(8):
                    dt_ = grp * 8 + k
                    last = ve.tensor_copy(
                        n.ostage[:, D * (tcb % 2) + 128 * (grp * 8 + k):
                                 D * (tcb % 2) + 128 * (grp * 8 + k) + 128],
                        res.ap[:, 128 * k:128 * k + 128])
                res.cons_done(last)
                if grp == 3:
                    ctx.inc(ve.tensor_copy(n.rstage[:, 0:512],
                                           n.rstage[:, 0:512]), "sV")
            n.block.vector(tv)

        def so(sq, tcb=tcb):
            ctx.wait(sq, "sV")
            ctx.inc(sq.dma_start(out=d["y"][128 * tcb:128 * tcb + 128, :],
                                 in_=n.ostage[:, D * (tcb % 2):D * (tcb % 2) + D]),
                    "sO", 16)
        if n.fused:
            n.block.gpsimd(so)
        else:
            n.block.scalar(so)

    def fin(sq):
        for s_ in ["sV", "sS", "sO", "sW", "sG", "sWl", "sWl0", "sWl1",
                   "sWx0", "sWx1"] + (["sGg0", "sGg1", "sWt0", "sWt1",
                                       "sWd0", "sWd1", "sG0", "sG1",
                                       "sGt0", "sGt1"] if n.fused else []):
            ctx.wait(sq, s_)
        for tname, shape, dt in taps:
            src_ap = resolve_tap(n, tname)
            ctx.inc(sq.dma_start(out=tap_out[tname], in_=src_ap), "sO", 16)
        ctx.wait(sq, "sO")
    n.block.scalar(fin)



def _prep_launch2_inputs(inputs, xn1g):
    wqkv = np.asarray(inputs["L1_attn_w_qkv"]).astype(np.float32)
    bqkv = np.asarray(inputs["L1_attn_b_qkv"]).astype(np.float32)
    wT = np.ascontiguousarray(wqkv.T)

    g = np.arange(T)
    r, loc = g // 1024, g % 1024
    strip = np.where(loc < 512, r, 15 - r)
    pos = 512 * strip + (loc % 512)
    inv = 1.0 / (10000.0 ** (np.arange(0, 128, 2) / 128.0))
    ang = pos[None, :] * np.repeat(inv, 2)[:, None]
    ctabK = np.cos(ang).astype(np.float32)
    stabK = np.sin(ang).astype(np.float32)
    sgn = np.where(np.arange(128) % 2 == 0, -1.0, 1.0).astype(np.float32)
    stabK = stabK * sgn[:, None]
    sc = 1.0 / np.sqrt(128.0)
    J = np.zeros((128, 128), dtype=np.float32)
    for f in range(64):
        J[2 * f + 1, 2 * f] = 1.0
        J[2 * f, 2 * f + 1] = 1.0
    dmask = np.zeros((4, 128, 512), dtype=np.float32)
    for u in range(4):
        dmask[u] = (128 * u + np.arange(128)[:, None]) <= np.arange(512)[None, :]

    base = {
        "xn1g": xn1g,
        "ctabK": ctabK.astype(BF), "stabK": stabK.astype(BF),
        "ctabQ": (ctabK * sc).astype(BF), "stabQ": (stabK * sc).astype(BF),
        "Jmat": J.astype(BF), "dmask": dmask.astype(BF),
        "id_bf": np.eye(128).astype(BF),
        "ones_p1_bf": np.ones((128, 1), dtype=BF),
        "ones_1p_f32": np.ones((1, 128), dtype=np.float32),
    }
    in_maps = []
    for c in range(NCORE):
        m = dict(base)
        qcols = slice(512 * c, 512 * c + 512)
        m["wq_blk"] = pack_lhsT(np.ascontiguousarray(wT[:, qcols]))
        m["wk_blk"] = pack_lhsT(np.ascontiguousarray(wT[:, 4096 + 128 * c:4096 + 128 * c + 128]))
        m["wv_blk"] = pack_lhsT(np.ascontiguousarray(wT[:, 5120 + 128 * c:5120 + 128 * c + 128]))
        m["bq"] = np.ascontiguousarray(bqkv[qcols].reshape(512, 1))
        m["bk"] = np.ascontiguousarray(bqkv[4096 + 128 * c:4096 + 128 * c + 128].reshape(128, 1))
        m["bv"] = np.ascontiguousarray(bqkv[5120 + 128 * c:5120 + 128 * c + 128].reshape(128, 1))
        in_maps.append(m)
    return in_maps


def _prep_launch3_inputs(inputs, x1s, xn1s, attTs):
    w12T = np.ascontiguousarray(np.asarray(inputs["L1_ffn_w12"]).astype(np.float32).T)
    w3T = np.ascontiguousarray(np.asarray(inputs["L1_ffn_w3"]).astype(np.float32).T)
    woT = np.ascontiguousarray(np.asarray(inputs["L1_attn_w_o"]).astype(np.float32).T)
    base = {
        "w12_blk": pack_lhsT(w12T),
        "b12_pk": np.ascontiguousarray(
            np.asarray(inputs["L1_ffn_b12"]).astype(np.float32).reshape(64, 128).T),
        "w3_blk": pack_lhsT(w3T),
        "b3_pk": np.ascontiguousarray(
            np.asarray(inputs["L1_ffn_b3"]).astype(np.float32).reshape(32, 128).T),
        "wo_blk": pack_lhsT(woT),
        "bo_pk": np.ascontiguousarray(
            np.asarray(inputs["L1_attn_b_o"]).astype(np.float32).reshape(32, 128).T),
        "id_bf": np.eye(128).astype(BF),
    }
    in_maps = []
    for c in range(NCORE):
        m = dict(base)
        m["x1c"] = x1s[c]
        m["xn1c"] = xn1s[c]
        m["attTc"] = attTs[c]
        in_maps.append(m)
    return in_maps


_BUILD_CACHE = {}


def _get(name, fn):
    if name not in _BUILD_CACHE:
        _BUILD_CACHE[name] = fn()
    return _BUILD_CACHE[name]





def emit_barrier(n, tag):
    ctx = n.ctx
    names = [s_ for s_ in ctx.sems if ctx.cnt[s_] > 0]

    def mk(eng):
        for s_ in names:
            ctx.wait(eng, s_)
    n.block.sync(mk)
    n.block.gpsimd(mk)
    n.block.vector(mk)
    n.block.scalar(mk)
    n.block.tensor(mk)


def build_fused(taps=(), dbg_l1=False, paranoid=()):
    nc = bass.Bass("TRN2", target_bir_lowering=False)
    dram = {}
    din, dout, dint = _mk_dram_helpers(nc, dram)

    din("x0c", [TC, D], F32)
    for nm, shp in [("lam", [128, 1]), ("vc_rT", [64, 64]), ("vc_iT", [64, 64]),
                    ("bpv", [64, 1]), ("ct_r", [64, 1]), ("ct_i", [64, 1]),
                    ("p_r", [64, 1]), ("p_i", [64, 1]), ("q_r", [64, 1]),
                    ("q_i", [64, 1]), ("dpar", [1, 1]), ("logstep", [1, 1]),
                    ("b_base_pk", [128, 32]), ("wrou_pk", [128, 256]),
                    ("aab_pk", [128, 4096]),
                    ("ball", [128, D]), ("t_r_row", [1, D]), ("t_i_row", [1, D]),
                    ("cl_r", [128, 32]), ("cl_i", [128, 32]),
                    ("ones_1p_f32", [1, 128])]:
        din(nm, shp, F32)
    din("cA", [D, 128], BF16)
    din("sA", [D, 128], BF16)
    din("wbase_blk", [NDT, 128, D], F32)
    for nm, shp in [("csM", [D, 2 * D]), ("trimask", [128, 128]),
                    ("id_bf", [128, 128]), ("ones_p1_bf", [128, 1]),
                    ("E8", [8, 128])]:
        din(nm, shp, BF16)
    din("wq_blk", [4, 128, D], F32)
    din("wk_blk", [1, 128, D], F32)
    din("wv_blk", [1, 128, D], F32)
    din("bq", [512, 1], F32)
    din("bk", [128, 1], F32)
    din("bv", [128, 1], F32)
    din("ctabK", [128, T], BF16)
    din("stabK", [128, T], BF16)
    din("ctabQ", [128, T], BF16)
    din("stabQ", [128, T], BF16)
    din("Jmat", [128, 128], BF16)
    din("dmask", [4, 128, 512], BF16)
    din("gidx", [128, 8], mybir.dt.int32)
    din("w12_blk", [64, 128, D], F32)
    din("b12_pk", [128, 64], F32)
    din("w3_blk", [NDT, 128, D], F32)
    din("b3_pk", [128, 32], F32)
    din("wo_blk", [NDT, 128, D], F32)
    din("bo_pk", [128, 32], F32)

    dout("y", [TC, D], F32)

    dint("kbounce", [8, D], F32)
    dint("xs0", [D, TC], BF16)
    if dbg_l1:
        xs1 = dout("xs1", [D, TC], BF16)
        dout("attdbg", [T, 512], BF16)
    else:
        xs1 = dint("xs1", [D, TC], BF16)
    cc1_in = dint("cc1_in", [D, TC], BF16)
    cc1_out = dint("cc1_out", [NCORE * D, TC], BF16, addr_space="Shared")
    cc2_in = dint("cc2_in", [T, 512], BF16)
    cc2_out = dint("cc2_out", [NCORE * T, 512], BF16, addr_space="Shared")
    dram["x1_out"] = xs1
    dram["xn1_out"] = cc1_in
    dram["xn1g"] = cc1_out
    dram["x1c"] = xs1
    dram["xn1c"] = cc1_in

    tap_out = {}
    for tname, shape, dt in taps:
        tap_out[tname] = dout("tap_" + tname, list(shape), dt)

    with contextlib.ExitStack() as stack:
        n = _alloc_l1(nc, stack, dram, fused=True)
        n.fused = True
        sb = lambda name, shape, dt: stack.enter_context(nc.sbuf_tensor(name, list(shape), dt))

        n.Kbuf = n.bufA[:, 0:T]
        n.Vtok = n.bufA[:, T:2 * T]
        n.wqb = n.bufA[:, 2 * T:2 * T + 4 * D]
        n.xbig = n.bufB[:, 0:16384]
        n.wkb = n.bufB[:, 16384:16384 + D]
        n.wvb = n.bufB[:, 16384 + D:16384 + 2 * D]
        n.qro = n.rstage[:, 0:2048]
        n.pbuf = n.sqst[:, 0:1536]
        n.pxtra = n.mrep[:, 512:1024]
        n.sqb = n.sqst[:, 1536:2048]
        n.silu = n.rstage[:, 0:TC]
        n.gslot = lambda rr: (n.wbuf[:, D:2 * D] if rr % 2 == 0
                              else n.wbuf[:, 3 * D:4 * D])
        n.tstg = n.bufB[:, 24576:24576 + 2048]
        n.kst = n.cscr[:, 0:512]
        n.vst = n.mrep[:, 0:512]
        n.qtmp = n.cscr[:, 512:1024]
        n.rowz = n.cscr[0:1, 1024:1536]
        n.sc2 = sb("sc2", [1, 16], F32)
        n.sc2b = sb("sc2b", [1, 32], F32)
        n.negb = sb("negb", [128, 16], F32)
        n.accden = n.cscr[:, 2048:2560]
        n.rrepA = n.krr[:, :]
        n.tstage = n.rrep[:, 512:1024]
        n.bqb = sb("bqb", [128, 4], F32)
        n.bkb = sb("bkb", [128, 1], F32)
        n.bvb = sb("bvb", [128, 1], F32)
        n.sb_J = sb("sb_J", [128, 128], BF16)
        n.gidx = sb("gidx_sb", [128, 8], mybir.dt.int32)
        n.sb_dm = n.Tmat[:, 0:2048]
        n.b12b = sb("b12b", [128, 64], F32)
        n.b3b = sb("b3b", [128, NDT], F32)
        n.bob = sb("bob", [128, NDT], F32)
        n.ostage = n.wbuf[:, 0:2 * D]

        ctx = n.ctx = Ctx(nc, stack)
        for s_ in ["sW", "sG", "sO", "sS", "sV", "sT", "sWt", "sVq", "sWd",
                   "sWl", "sWf", "sTg", "sVp", "sTp", "sVx", "sOx", "sWr",
                   "sVr", "sWk", "sVk", "sC1", "sC2", "sGg", "sTgg", "sP",
                   "sX", "sKV", "sWt0", "sWt1", "sWd0", "sWd1", "sG0", "sG1",
                   "sWl0", "sWl1", "sWx0", "sWx1", "sGt0", "sGt1",
                   "sGg0", "sGg1"]:
            ctx.sem(s_)
        n.PA = PsRes(ctx, n.psA)
        n.PB = PsRes(ctx, n.psB)
        n.PE = PsRes(ctx, n.psE)
        n.PF = PsRes(ctx, n.psF)
        n.PT1 = PsRes(ctx, n.psT1)
        n.PT2 = PsRes(ctx, n.psT2)
        n.psQ = n.psE
        n.PQ = n.PE
        n.psS1 = n.psA
        n.PS1 = n.PA
        n.psS2 = n.psB
        n.PS2 = n.PB
        n.psAV = n.psF[:, 0:512]
        n.psD = n.psF[:, 512:1024]
        n.PAV = PsRes(ctx, n.psAV)
        n.PD = PsRes(ctx, n.psD)
        n.psVT = n.psT1[:, 0:512]
        n.PVT = n.PT1

        with nc.Block() as block:
            n.block = block
            emit_const_loads(n)
            emit_l2_consts_sm(n, with_dm=False)
            emit_l3_consts(n)
            bset = paranoid if isinstance(paranoid, (set, list, tuple)) else (
                {"ssm", "tb", "pro", "ln0", "ap0", "ffn0", "conv", "x1", "postL1",
                 "postAG1w", "postKV", "postATT", "postAG2", "l3a", "l3ffn",
                 "l3w3", "l3gather", "l3wo"} if paranoid else set())
            n.bset = bset
            P = lambda tag: (emit_barrier(n, tag) if tag in bset else None)
            emit_ssm_prep(n)
            P("ssm")
            emit_tbuild(n)
            P("tb")
            emit_prologue(n)
            P("pro")
            emit_lnstats(n, n.A)
            P("ln0")
            emit_apply_ln(n, n.A, n.B)
            P("ap0")
            emit_ffn0(n)
            P("ffn0")
            emit_conv(n)
            P("conv")
            emit_x1(n)
            P("x1")
            emit_l2_dm(n)
            P("postL1")
            xn1_sOx = ctx.val("sOx")

            def ag1(gq):
                ctx.wait(gq, "sOx", xn1_sOx)
                ctx.inc(gq.collective_compute(
                    "AllGather", ALU.bypass,
                    replica_groups=[list(range(NCORE))],
                    ins=[dram["cc1_in"].opt()], outs=[dram["cc1_out"].opt()]), "sC1")
            block.gpsimd(ag1)
            emit_l2_wload(n)
            P("postAG1w")
            emit_l2_kv(n)
            P("postKV")
            emit_l2_attn(n)
            P("postATT")
            sP_att = ctx.val("sP")
            n.wl0_pre = [ctx.val("sWl0"), ctx.val("sWl1")]

            def g1pre(gq):
                ctx.wait(gq, "sP", sP_att)
                for mc in range(2):
                    ctx.inc(gq.dma_start(
                        out=n.wbuf[:, 2 * D * mc:2 * D * mc + D],
                        in_=dram["w12_blk"][mc]), f"sWl{mc % 2}", 16)
                    ctx.inc(gq.dma_start(
                        out=n.wbuf[:, 2 * D * mc + D:2 * D * mc + 2 * D],
                        in_=dram["w12_blk"][32 + mc]), f"sWl{mc % 2}", 16)
            block.gpsimd(g1pre)

            att_sO = ctx.val("sO")

            def ag2(gq):
                ctx.wait(gq, "sO", att_sO)
                ctx.inc(gq.collective_compute(
                    "AllGather", ALU.bypass,
                    replica_groups=[list(range(NCORE))],
                    ins=[dram["cc2_in"].opt()], outs=[dram["cc2_out"].opt()]), "sC2")
            block.gpsimd(ag2)
            n.paranoid = bset
            if "postAG2" in bset:
                emit_barrier(n, "postAG2")
            l2done = {n.PS1.tname: ctx.val(n.PS1.tname),
                      n.PS2.tname: ctx.val(n.PS2.tname),
                      n.PE.tname: ctx.val(n.PE.tname),
                      n.PAV.tname: ctx.val(n.PAV.tname),
                      n.PAV.cname: ctx.val(n.PAV.cname),
                      n.PD.cname: ctx.val(n.PD.cname),
                      n.PT1.cname: ctx.val(n.PT1.cname),
                      "sP": ctx.val("sP"),
                      "sO": ctx.val("sO")}
            emit_l3(n, taps, tap_out, l2done=l2done)
    return nc, dram


def _prep_fused_inputs(inputs):
    in1 = _prep_launch1_inputs(inputs)
    in2 = _prep_launch2_inputs(inputs, None)
    in3 = _prep_launch3_inputs(inputs, [None] * NCORE, [None] * NCORE,
                               [None] * NCORE)
    in_maps = []
    for c in range(NCORE):
        m = dict(in1[c])
        m.update({k: v for k, v in in2[c].items() if v is not None and k != "xn1g"})
        m.update({k: v for k, v in in3[c].items() if v is not None})
        p = np.arange(128, dtype=np.int64)
        idx = (8192 * np.arange(8)[None, :] + 1024 * c + 8 * p[:, None])
        m["gidx"] = idx.astype(np.int32)
        in_maps.append(m)
    return in_maps


def kernel_unfused(**inputs):
    nc1, _ = _get("l1", lambda: build_launch1())
    in1 = _prep_launch1_inputs(inputs)
    r1 = run_bass_kernel_spmd(nc1, in1, core_ids=list(range(NCORE)))
    x1s = [r1.results[c]["x1_out"] for c in range(NCORE)]
    xn1s = [r1.results[c]["xn1_out"] for c in range(NCORE)]

    xn1g = np.ascontiguousarray(np.concatenate(xn1s, axis=0))
    nc2, _ = _get("l2", lambda: build_launch2())
    in2 = _prep_launch2_inputs(inputs, xn1g)
    r2 = run_bass_kernel_spmd(nc2, in2, core_ids=list(range(NCORE)))
    att = [r2.results[c]["attnout"] for c in range(NCORE)]

    attTs = [np.ascontiguousarray(np.concatenate(
        [att[r][:, 1024 * c:1024 * c + 1024] for r in range(NCORE)], axis=0))
        for c in range(NCORE)]
    nc3, _ = _get("l3", lambda: build_launch3())
    in3 = _prep_launch3_inputs(inputs, x1s, xn1s, attTs)
    r3 = run_bass_kernel_spmd(nc3, in3, core_ids=list(range(NCORE)))

    out = np.empty((T, D), dtype=np.float32)
    for c in range(NCORE):
        out[_strip_tokens(c), :] = r3.results[c]["y"]
    return out.reshape(1, T, D)


def kernel(**inputs):
    nc, _ = _get("fused", lambda: build_fused())
    in_maps = _prep_fused_inputs(inputs)
    res = run_bass_kernel_spmd(nc, in_maps, core_ids=list(range(NCORE)))
    out = np.empty((T, D), dtype=np.float32)
    for c in range(NCORE):
        out[_strip_tokens(c), :] = res.results[c]["y"]
    return out.reshape(1, T, D)


# version 16 (final)
# speedup vs baseline: 1.0538x; 1.0454x over previous
import contextlib
import types

import numpy as np
import ml_dtypes
import concourse.bass as bass
import concourse.mybir as mybir
from concourse.bass_utils import run_bass_kernel_spmd

F32 = mybir.dt.float32
BF16 = mybir.dt.bfloat16
AF = mybir.ActivationFunctionType
ALU = mybir.AluOpType
AX = mybir.AxisListType
PM2 = mybir.MatmulPerfMode.DoubleRow

BF = ml_dtypes.bfloat16

D = 4096
T = 8192
TC = 1024
NCORE = 8
NDT = D // 128
NTT = TC // 128
EPS = 1e-5



def pack_lhsT(wT):
    Din, O = wT.shape
    dt_, oc_ = Din // 128, O // 128
    return np.ascontiguousarray(
        wT.reshape(dt_, 128, oc_, 128).transpose(2, 1, 0, 3).reshape(oc_, 128, Din))


def _host_tables():
    t = {}
    L = D
    s = np.arange(L)
    p_of, j_of = s // 32, s % 32
    l_of_r = np.empty(L, dtype=np.int64)
    l_of_r[128 * j_of + p_of] = s

    Omega = np.exp(-2j * np.pi * (np.arange(L) / L))
    tt = (1.0 - Omega) / (1.0 + Omega)
    cc = 2.0 / (1.0 + Omega)

    t["t_r_row"] = tt.real.reshape(1, L).astype(np.float32).copy()
    t["t_i_row"] = tt.imag.reshape(1, L).astype(np.float32).copy()
    lpj = 32 * np.arange(128)[:, None] + np.arange(32)[None, :]
    t["cl_r"] = cc.real[lpj].astype(np.float32).copy()
    t["cl_i"] = cc.imag[lpj].astype(np.float32).copy()

    a = np.arange(128)
    ang_a = 2 * np.pi * np.outer(l_of_r, a) / L
    t["cA"] = (np.cos(ang_a) / L).astype(BF).copy()
    t["sA"] = (np.sin(ang_a) / L).astype(BF).copy()

    m = np.arange(L)
    ang_m = 2 * np.pi * np.outer(l_of_r, m) / L
    cM = np.cos(ang_m).astype(BF)
    sM = np.sin(ang_m).astype(BF)
    cs = np.empty((32, 128, 4, 2048), dtype=BF)
    for a in range(4):
        cs[:, :, a, 0:1024] = cM.reshape(32, 128, 4096)[:, :, 1024 * a:1024 * a + 1024]
        cs[:, :, a, 1024:2048] = sM.reshape(32, 128, 4096)[:, :, 1024 * a:1024 * a + 1024]
    t["csM"] = np.ascontiguousarray(cs.reshape(4096, 8192))

    t["trimask"] = (np.arange(128)[None, :] >= np.arange(128)[:, None]).astype(BF).copy()
    t["id_bf"] = np.eye(128).astype(BF).copy()
    t["ones_p1_bf"] = np.ones((128, 1), dtype=BF)
    t["ones_1p_f32"] = np.ones((1, 128), dtype=np.float32)
    E8 = np.zeros((8, 128), dtype=np.float32)
    for e in range(8):
        E8[e, 16 * e:16 * e + 16] = 2.0
    t["E8"] = E8.astype(BF).copy()
    return t


def _strip_tokens(c):
    return np.concatenate([np.arange(512 * c, 512 * c + 512),
                           np.arange(512 * (15 - c), 512 * (15 - c) + 512)])



class Ctx:

    def __init__(self, nc, stack):
        self.nc = nc
        self.stack = stack
        self.cnt = {}
        self.sems = {}

    def sem(self, name):
        h = self.stack.enter_context(self.nc.semaphore(name))
        self.cnt[name] = 0
        self.sems[name] = h
        return h

    def inc(self, instr, name, n=1):
        self.cnt[name] += n
        instr.then_inc(self.sems[name], n)
        return instr

    def wait(self, eng, name, val=None):
        v = self.cnt[name] if val is None else val
        if v > 0:
            eng.wait_ge(self.sems[name], v)

    def val(self, name):
        return self.cnt[name]


class PsRes:
    _n = 0

    def __init__(self, ctx, ap):
        self.ctx = ctx
        self.ap = ap
        i = PsRes._n
        PsRes._n += 1
        self.tname = f"psT{i}"
        self.cname = f"psC{i}"
        ctx.sem(self.tname)
        ctx.sem(self.cname)

    def prod_wait(self, eng):
        self.ctx.wait(eng, self.cname)

    def prod_done(self, instr):
        return self.ctx.inc(instr, self.tname)

    def cons_wait(self, eng):
        self.ctx.wait(eng, self.tname)

    def cons_done(self, instr):
        return self.ctx.inc(instr, self.cname)


class SerialEng:

    def __init__(self, eng, spacer_ap):
        self._eng = eng
        self._sp = spacer_ap

    def __getattr__(self, name):
        f = getattr(self._eng, name)

        def emit(*a, **k):
            instr = f(*a, **k)
            self._eng.tensor_copy(self._sp, self._sp)
            return instr
        return emit



def build_launch1(taps=(), etaps=False):
    nc = bass.Bass("TRN2", target_bir_lowering=False)
    dram = {}
    din, dout, dint = _mk_dram_helpers(nc, dram)

    din("x0c", [TC, D], F32)
    for nm, shp in [("lam", [128, 1]), ("vc_rT", [64, 64]), ("vc_iT", [64, 64]),
                    ("bpv", [64, 1]), ("ct_r", [64, 1]), ("ct_i", [64, 1]),
                    ("p_r", [64, 1]), ("p_i", [64, 1]), ("q_r", [64, 1]),
                    ("q_i", [64, 1]), ("dpar", [1, 1]), ("logstep", [1, 1]),
                    ("b_base_pk", [128, 32]), ("wrou_pk", [128, 256]),
                    ("aab_pk", [128, 4096]),
                    ("ball", [128, D]), ("t_r_row", [1, D]), ("t_i_row", [1, D]),
                    ("cl_r", [128, 32]), ("cl_i", [128, 32]),
                    ("ones_1p_f32", [1, 128])]:
        din(nm, shp, F32)
    din("cA", [D, 128], BF16)
    din("sA", [D, 128], BF16)
    din("wbase_blk", [NDT, 128, D], F32)
    for nm, shp in [("csM", [D, 2 * D]), ("trimask", [128, 128]),
                    ("id_bf", [128, 128]), ("ones_p1_bf", [128, 1]),
                    ("E8", [8, 128])]:
        din(nm, shp, BF16)

    dout("x1_out", [D, TC], BF16)
    dout("xn1_out", [D, TC], BF16)
    for _dn in ["dbg_m_ln0", "dbg_r_ln0", "dbg_s_ln0"]:
        dout(_dn, [1, TC], F32)
    dint("kbounce", [8, D], F32)
    dint("xs0", [D, TC], BF16)

    tap_out = {}
    for tname, shape, dt in taps:
        tap_out[tname] = dout("tap_" + tname, list(shape), dt)

    with contextlib.ExitStack() as stack:
        n = _alloc_l1(nc, stack, dram)
        ctx = n.ctx = Ctx(nc, stack)
        for s_ in ["sW", "sG", "sO", "sS", "sV", "sT", "sWt", "sVq", "sWd",
                   "sWl", "sWf", "sTg", "sVp", "sTp", "sVx", "sOx", "sWr",
                   "sVr", "sWk", "sVk", "sZ", "sWt0", "sWt1", "sWd0", "sWd1",
                   "sG0", "sG1", "sWl0", "sWl1", "sWx0", "sWx1"]:
            ctx.sem(s_)
        n.PA = PsRes(ctx, n.psA)
        n.PB = PsRes(ctx, n.psB)
        n.PE = PsRes(ctx, n.psE)
        n.PF = PsRes(ctx, n.psF)
        n.PT1 = PsRes(ctx, n.psT1)
        n.PT2 = PsRes(ctx, n.psT2)

        with nc.Block() as block:
            n.block = block
            emit_const_loads(n)
            emit_ssm_prep(n)
            if etaps:
                et = {}
                for nm, shape, dt in [("tvalP", [128, 4096], BF16),
                                      ("tvalQ", [128, 4096], BF16),
                                      ("ar_i", [128, 32], F32),
                                      ("cscrF", [128, 512], F32),
                                      ("sc11", [1, 8], F32)]:
                    et[nm] = dout("etap_" + nm, shape, dt)

                def esy(sy):
                    ctx.wait(sy, "sV", n.ssm_done_sV)
                    ctx.inc(sy.dma_start(out=et["tvalP"], in_=n.bufA[:, 0:4096]), "sZ", 16)
                    ctx.inc(sy.dma_start(out=et["tvalQ"], in_=n.bufA[:, 4096:8192]), "sZ", 16)
                    ctx.inc(sy.dma_start(out=et["ar_i"], in_=n.ar_i[:]), "sZ", 16)
                    ctx.inc(sy.dma_start(out=et["cscrF"], in_=n.cscr[:, 512:1024]), "sZ", 16)
                    ctx.inc(sy.dma_start(out=et["sc11"], in_=n.sc11[0:1, 0:8]), "sZ", 16)
                    ctx.wait(sy, "sZ")
                n.block.sync(esy)
            emit_tbuild(n)
            emit_prologue(n)
            emit_lnstats(n, n.A, dbg="ln0")
            emit_apply_ln(n, n.A, n.B)
            emit_ffn0(n)
            emit_conv(n)
            emit_x1(n)
            emit_taps(n, taps, tap_out)

    return nc, dram


def _mk_dram_helpers(nc, dram):
    def din(name, shape, dt):
        dram[name] = nc.dram_tensor(name, list(shape), dt, kind="ExternalInput").ap()
        return dram[name]

    def dout(name, shape, dt):
        dram[name] = nc.dram_tensor(name, list(shape), dt, kind="ExternalOutput").ap()
        return dram[name]

    def dint(name, shape, dt, **kw):
        dram[name] = nc.dram_tensor(name, list(shape), dt, kind="Internal", **kw).ap()
        return dram[name]
    return din, dout, dint


def _alloc_l1(nc, stack, dram, fused=False):
    n = types.SimpleNamespace()
    n.nc = nc
    n.dram = dram
    sb = lambda name, shape, dt: stack.enter_context(nc.sbuf_tensor(name, list(shape), dt))
    ps = lambda name, shape, dt: stack.enter_context(nc.psum_tensor(name, list(shape), dt))

    n.bufA = sb("bufA", [128, NDT * TC], BF16)
    n.bufB = sb("bufB", [128, NDT * TC], BF16)
    n.A = lambda d: n.bufA[:, d * TC:(d + 1) * TC]
    n.B = lambda d: n.bufB[:, d * TC:(d + 1) * TC]
    n.tvalP = lambda j: n.bufA[:, 128 * j:128 * j + 128]
    n.tvalQ = lambda j: n.bufA[:, 4096 + 128 * j:4096 + 128 * j + 128]
    n.trig = lambda s_: n.bufA[:, 8192 + 128 * s_:8192 + 128 * s_ + 128]
    n.stage = lambda s_: n.bufB[:, D * s_:D * s_ + D]

    if fused:
        n.wbuf = sb("wbuf", [128, 4 * D], BF16)
        n.Tmat = n.wbuf[:, 2 * D:3 * D]
    else:
        n.Tmat = sb("Tmat", [128, D], BF16)
        n.wbuf = sb("wbuf", [128, 2 * D], BF16)
    n.dftbuf = n.wbuf
    n.wrb = sb("wrb", [128, 8 * NDT], BF16)
    n.aab = sb("aab", [128, D], BF16)
    n.blb = sb("blb", [128, D], BF16)
    if fused:
        n.cscr = sb("cscr", [128, 6 * 512], F32)
        n.r_sum = n.cscr[0:1, 0:1024]
        n.r_ssq = n.cscr[0:1, 1024:2048]
        n.r_var = n.cscr[0:1, 2048:3072]
        n.r_rstd = n.r_var
        n.r_mean = n.r_sum
        n.rowb = n.cscr[0:1, 1536:2560]
        n.krr = sb("krr", [128, 512], F32)
        n.kchunk = n.krr[0:8, :]
        n.sspacer = n.cscr[0:1, 1536:2048]
    else:
        n.cscr = sb("cscr", [128, 3 * 512], F32)
        n.kchunk = sb("kchunk", [8, 512], F32)
        n.sspacer = n.cscr[0:1, 1024:1536]
    n.sscr = n.cscr[:, 1024:1536]
    n.vcT = n.cscr[0:64, 0:128]
    n.karr = n.cscr[:, 512:768]
    n.svec = sb("svec", [64, 16], F32)
    n.svec2 = sb("svec2", [64, 16], F32)
    n.vstack = sb("vstack", [128, 8], F32)
    n.clr = sb("clr", [128, 32], F32)
    n.cli = sb("cli", [128, 32], F32)
    n.ar_r = sb("ar_r", [128, 32], F32)
    n.ar_i = sb("ar_i", [128, 32], F32)
    n.ar_in = sb("ar_in", [128, 32], F32)
    n.lamn = sb("lamn", [128, 1], F32)
    if not fused:
        n.rowb = sb("rowb", [1, 1024], F32)
    n.sc11 = sb("sc11", [1, 8], F32)
    n.dvec = sb("dvec", [128, 1], F32)

    n.sqst = sb("sqst", [128, 2 * TC], BF16)
    n.rstage = sb("rstage", [128, 2 * TC], BF16)
    n.ctall = n.sqst[0:8, 0:TC]
    n.cexp = n.rstage[:, 0:TC]
    n.hbuf = n.rstage[:, TC:2 * TC]
    n.hsc = n.hbuf
    if not fused:
        n.r_sum = sb("r_sum", [1, TC], F32)
        n.r_ssq = sb("r_ssq", [1, TC], F32)
        n.r_rstd = sb("r_rstd", [1, TC], F32)
        n.r_mean = n.r_sum
        n.r_var = n.r_ssq
    n.mrep = sb("mrep", [128, TC], BF16)
    n.rrep = sb("rrep", [128, TC], BF16)
    n.bbuf = sb("bbuf", [128, NDT], F32)
    n.lgst = sb("lgst", [128, 64], F32)
    n.ebuf = sb("ebuf", [128, 64], F32)
    n.m1b = sb("m1b", [128, 8], F32)
    n.nm1b = sb("nm1b", [128, 8], F32)
    n.m2b = sb("m2b", [128, 8], F32)
    n.rcb = sb("rcb", [128, 8], F32)
    n.cmbt = sb("cmbt", [128, 64], BF16)

    n.sb_id = sb("sb_id", [128, 128], BF16)
    n.sb_ones_p1 = sb("sb_ones_p1", [128, 1], BF16)
    n.sb_ones_1p = sb("sb_ones_1p", [1, 128], F32)
    n.sb_tri = sb("sb_tri", [128, 128], BF16)
    n.sb_E8 = sb("sb_E8", [8, 128], BF16)

    n.psA = ps("psA", [128, 512], F32)
    n.psB = ps("psB", [128, 512], F32)
    n.psE = ps("psE", [128, 1024], F32)
    n.psF = ps("psF", [128, 1024], F32)
    n.psT1 = ps("psT1", [128, 1024], BF16)
    n.psT2 = ps("psT2", [128, 1024], BF16)
    return n



def emit_const_loads(n):
    ctx, d = n.ctx, n.dram

    def f(sy):
        for dst, src in [(n.sb_id, d["id_bf"]), (n.sb_ones_p1, d["ones_p1_bf"]),
                         (n.sb_ones_1p, d["ones_1p_f32"]), (n.sb_tri, d["trimask"]),
                         (n.sb_E8, d["E8"]), (n.lamn, d["lam"]),
                         (n.clr, d["cl_r"]), (n.cli, d["cl_i"])]:
            ctx.inc(sy.dma_start(out=dst[:], in_=src), "sW", 16)
        for i, src in enumerate(["bpv", "ct_r", "ct_i", "p_r", "p_i", "q_r", "q_i"]):
            ctx.inc(sy.dma_start(out=n.svec[:, i:i + 1], in_=d[src]), "sW", 16)
        ctx.inc(sy.dma_start(out=n.sc11[:, 0:1], in_=d["logstep"]), "sW", 16)
        ctx.inc(sy.dma_start(out=n.sc11[:, 1:2], in_=d["dpar"]), "sW", 16)
        ctx.inc(sy.dma_start(out=n.vcT[:, 0:64], in_=d["vc_rT"]), "sW", 16)
        ctx.inc(sy.dma_start(out=n.vcT[:, 64:128], in_=d["vc_iT"]), "sW", 16)
        ctx.inc(sy.dma_start(out=n.bbuf[:, 0:32], in_=d["b_base_pk"]), "sW", 16)
    n.block.sync(f)

    def g(gq):
        ctx.inc(gq.dma_start(out=n.wrb[:], in_=d["wrou_pk"]), "sG", 16)
        ctx.inc(gq.dma_start(out=n.aab[:], in_=d["aab_pk"]), "sG", 16)
        ctx.inc(gq.dma_start(out=n.blb[:], in_=d["ball"]), "sG", 16)
    n.block.gpsimd(g)
    n.const_sW = ctx.val("sW")
    n.const_sG = ctx.val("sG")


def emit_ssm_prep(n):
    ctx, d = n.ctx, n.dram

    def sc(s):
        ctx.wait(s, "sW", n.const_sW)
        s.activation(n.sc11[0:1, 2:3], n.sc11[0:1, 0:1], AF.Exp, scale=-1.0)
        ctx.inc(s.copy(n.sspacer[:], n.sspacer[:]), "sS")
    n.block.scalar(sc)

    def v1(v):
        ctx.wait(v, "sS")
        v.tensor_scalar_mul(n.sc11[0:1, 3:4], n.sc11[0:1, 2:3], 2.0)
        ctx.inc(v.tensor_scalar_mul(n.lamn[:], n.lamn[:], -1.0), "sV")
    n.block.vector(v1)
    g_ready = ctx.val("sV")

    def t1(t):
        ctx.wait(t, "sV", g_ready)
        n.PA.prod_wait(t)
        t.matmul(n.psA[0:64, 0:1], n.vcT[:, 0:64], n.svec[:, 0:1], start=True, stop=True)
        n.PA.prod_done(
            t.matmul(n.psA[0:64, 1:2], n.vcT[:, 64:128], n.svec[:, 0:1],
                     start=True, stop=True))
    n.block.tensor(t1)

    def v2(v):
        rv = v
        n.PA.cons_wait(v)
        v = SerialEng(v, n.mrep[:, 0:512])
        v.tensor_copy(n.svec[:, 7:8], n.psA[0:64, 0:1])
        n.PA.cons_done(v.tensor_copy(n.svec[:, 8:9], n.psA[0:64, 1:2]))
        C = lambda i: n.svec[:, i:i + 1]
        W = lambda i: n.svec2[:, i:i + 1]
        BR, BI, CR, CI = C(7), C(8), C(1), C(2)
        PR, PI, QR, QI = C(3), C(4), C(5), C(6)
        pairs = [(CR, CI, BR, BI, 0), (CR, CI, PR, PI, 2),
                 (QR, QI, BR, BI, 4), (QR, QI, PR, PI, 6)]
        for (xr, xi, yr, yi, o) in pairs:
            v.tensor_mul(W(o), xr, yr)
            v.tensor_mul(W(8), xi, yi)
            v.tensor_add(W(o), W(o), W(8))
            v.tensor_mul(W(o + 1), xr, yi)
            v.tensor_mul(W(8), xi, yr)
            v.tensor_sub(W(o + 1), W(o + 1), W(8))
        for cidx in range(4):
            vr, vi = W(2 * cidx), W(2 * cidx + 1)
            v.tensor_copy(n.vstack[0:64, 2 * cidx:2 * cidx + 1], vr)
            v.tensor_copy(n.vstack[64:128, 2 * cidx:2 * cidx + 1], vi)
            v.tensor_copy(n.vstack[0:64, 2 * cidx + 1:2 * cidx + 2], vi)
            v.tensor_scalar_mul(W(9), vr, -1.0)
            v.tensor_copy(n.vstack[64:128, 2 * cidx + 1:2 * cidx + 2], W(9))
        ctx.inc(rv.tensor_copy(n.mrep[:, 0:512], n.mrep[:, 0:512]), "sV")
    n.block.vector(v2)
    vstack_done = ctx.val("sV")

    CS = lambda i: n.cscr[:, 512 * i:512 * i + 512]
    RB = lambda ch, h: n.rowb[0:1, 512 * h:512 * h + 512]
    pe_t0 = ctx.val(n.PE.tname)
    wk0 = ctx.val("sWk")
    for ch in range(8):
        cs = slice(512 * ch, 512 * ch + 512)

        def sy2(sy, ch=ch, cs=cs):
            if ch >= 1:
                sy.wait_ge(ctx.sems[n.PE.tname], pe_t0 + ch)
            ctx.inc(sy.dma_start(out=RB(ch, 0), in_=d["t_r_row"][0:1, cs]), "sWr", 16)
            ctx.inc(sy.dma_start(out=RB(ch, 1), in_=d["t_i_row"][0:1, cs]), "sWr", 16)
        n.block.sync(sy2)

        def vr(v, ch=ch):
            ctx.wait(v, "sWr")
            if ch == 0:
                ctx.wait(v, "sV", vstack_done)
            sl = n.rowb[0:1, 0:1024]
            ctx.inc(v.tensor_scalar(sl, sl, n.sc11[0:1, 3:4], None, ALU.mult), "sVr")
        n.block.vector(vr)

        def t2(t, ch=ch):
            n.PE.prod_wait(t)
            ctx.wait(t, "sVr")
            t.matmul(n.psE[0:64, 0:512], n.sb_ones_1p[:, 0:64], RB(ch, 0),
                     start=True, stop=True)
            n.PE.prod_done(
                t.matmul(n.psE[64:128, 0:512], n.sb_ones_1p[:, 0:64], RB(ch, 1),
                         start=True, stop=True))
        n.block.tensor(t2)

        def v3(v, ch=ch):
            n.PE.cons_wait(v)
            if ch > 0:
                ctx.wait(v, n.PF.tname)
            n.PE.cons_done(
                v.tensor_scalar(CS(0), n.psE[:, 0:512], n.lamn[:], None, ALU.add))
            v.tensor_copy(CS(2)[0:64, :], CS(0)[64:128, :])
            v.tensor_mul(CS(2)[0:64, :], CS(2)[0:64, :], CS(2)[0:64, :])
            v.tensor_mul(CS(1)[0:64, :], CS(0)[0:64, :], CS(0)[0:64, :])
            v.tensor_add(CS(2)[0:64, :], CS(2)[0:64, :], CS(1)[0:64, :])
            v.reciprocal(CS(2)[0:64, :], CS(2)[0:64, :])
            v.tensor_mul(CS(1)[0:64, :], CS(0)[0:64, :], CS(2)[0:64, :])
            v.tensor_copy(CS(2)[64:128, :], CS(2)[0:64, :])
            ctx.inc(v.tensor_mul(CS(1)[64:128, :], CS(0)[64:128, :], CS(2)[64:128, :]),
                    "sVq")
        n.block.vector(v3)
        ii_done = ctx.val("sVq")

        def t3(t, ii_done=ii_done):
            ctx.wait(t, "sVq", ii_done)
            n.PF.prod_wait(t)
            n.PF.prod_done(
                t.matmul(n.psF[0:8, 0:512], n.vstack[:], CS(1), start=True, stop=True))
        n.block.tensor(t3)

        def v4(v, ch=ch):
            n.PF.cons_wait(v)
            if ch >= 1:
                v.wait_ge(ctx.sems["sWk"], wk0 + 16 * ch)
            n.PF.cons_done(v.tensor_copy(n.kchunk[:], n.psF[0:8, 0:512]))
            ctx.inc(v.tensor_copy(n.mrep[:, 0:512], n.mrep[:, 0:512]), "sVk")
        n.block.vector(v4)

        def sy3(sy, ch=ch, cs=cs):
            ctx.wait(sy, "sVk")
            ctx.inc(sy.dma_start(out=d["kbounce"][:, cs], in_=n.kchunk[:]),
                    "sWk", 16)
        n.block.sync(sy3)

    def sy4(sy):
        ctx.wait(sy, "sWk")
        for comp in range(8):
            ctx.inc(sy.dma_start(
                out=n.karr[:, 32 * comp:32 * comp + 32],
                in_=d["kbounce"][comp:comp + 1, :].rearrange("o (p j) -> (o p) j", p=128)),
                "sW", 16)
    n.block.sync(sy4)
    karr_in = ctx.val("sW")

    def v5(v):
        rv = v
        ctx.wait(v, "sW", karr_in)
        v = SerialEng(v, n.mrep[:, 0:512])
        K = lambda c: n.karr[:, 32 * c:32 * c + 32]
        k00r, k00i, k01r, k01i, k10r, k10i, k11r, k11i = [K(c) for c in range(8)]
        S = lambda i: n.sscr[:, 64 * i:64 * i + 32]
        v.tensor_scalar_add(S(0), k11r, 1.0)
        v.tensor_mul(S(1), S(0), S(0))
        v.tensor_mul(S(2), k11i, k11i)
        v.tensor_add(S(1), S(1), S(2))
        v.reciprocal(S(1), S(1))
        v.tensor_mul(S(2), k01r, k10r)
        v.tensor_mul(S(3), k01i, k10i)
        v.tensor_sub(S(2), S(2), S(3))
        v.tensor_mul(S(3), k01r, k10i)
        v.tensor_mul(S(4), k01i, k10r)
        v.tensor_add(S(3), S(3), S(4))
        v.tensor_mul(S(4), S(2), S(0))
        v.tensor_mul(S(5), S(3), k11i)
        v.tensor_add(S(4), S(4), S(5))
        v.tensor_mul(S(4), S(4), S(1))
        v.tensor_mul(S(5), S(3), S(0))
        v.tensor_mul(S(6), S(2), k11i)
        v.tensor_sub(S(5), S(5), S(6))
        v.tensor_mul(S(5), S(5), S(1))
        v.tensor_sub(S(4), k00r, S(4))
        v.tensor_sub(S(5), k00i, S(5))
        v.tensor_mul(n.ar_r[:], n.clr[:], S(4))
        v.tensor_mul(S(6), n.cli[:], S(5))
        v.tensor_sub(n.ar_r[:], n.ar_r[:], S(6))
        v.tensor_mul(n.ar_i[:], n.clr[:], S(5))
        v.tensor_mul(S(6), n.cli[:], S(4))
        v.tensor_add(n.ar_i[:], n.ar_i[:], S(6))
        v.tensor_scalar_mul(n.ar_in[:], n.ar_i[:], -1.0)
        ctx.inc(rv.tensor_copy(n.mrep[:, 0:512], n.mrep[:, 0:512]), "sV")
    n.block.vector(v5)
    ar_done = ctx.val("sV")

    wt0 = [ctx.val("sWt0"), ctx.val("sWt1")]

    def sy3(sy):
        for j in range(32):
            s0 = 2 * (j % 2)
            if j >= 2:
                ctx.wait(sy, "sVq", ii_done + (j - 1))
            ctx.inc(sy.dma_start(out=n.trig(s0), in_=d["cA"][128 * j:128 * j + 128, :]),
                    f"sWt{j % 2}", 16)
            ctx.inc(sy.dma_start(out=n.trig(s0 + 1), in_=d["sA"][128 * j:128 * j + 128, :]),
                    f"sWt{j % 2}", 16)
    n.block.sync(sy3)

    def v6(v):
        ctx.wait(v, "sV", ar_done)
        F = lambda i: n.cscr[:, 640 * i:640 * i + 128]
        for j in range(32):
            s0 = 2 * (j % 2)
            v.wait_ge(ctx.sems[f"sWt{j % 2}"], wt0[j % 2] + 32 * (j // 2 + 1))
            arj, aij, anj = n.ar_r[:, j:j + 1], n.ar_i[:, j:j + 1], n.ar_in[:, j:j + 1]
            v.tensor_scalar(F(0), n.trig(s0), arj, None, ALU.mult)
            v.scalar_tensor_tensor(n.tvalP(j), n.trig(s0 + 1), aij, F(0),
                                   ALU.mult, ALU.add)
            v.tensor_scalar(F(1), n.trig(s0 + 1), arj, None, ALU.mult)
            ctx.inc(v.scalar_tensor_tensor(n.tvalQ(j), n.trig(s0), anj, F(1),
                                           ALU.mult, ALU.add), "sVq")
        ctx.inc(v.tensor_copy(n.mrep[:, 0:512], n.mrep[:, 0:512]), "sV")
    n.block.vector(v6)
    n.ssm_done_sV = ctx.val("sV")


def emit_tbuild(n):
    ctx, d = n.ctx, n.dram
    wd0 = [ctx.val("sWd0"), ctx.val("sWd1")]
    td0 = ctx.val("sTg")

    after = []
    cnt = 0
    for k in range(128):
        if k % 32 != 31:
            cnt += 1
        after.append(cnt)
    NSLOT = 4

    def dma_for(queue_its):
        def f(q):
            for it in queue_its:
                a, j = it // 32, it % 32
                slot = it % NSLOT
                if it >= NSLOT:
                    k = it - NSLOT
                    need = after[k] if k % 32 != 31 else after[k + 1]
                    ctx.wait(q, "sTg", td0 + need)
                ctx.inc(q.dma_start(
                    out=n.dftbuf[:, 2048 * slot:2048 * slot + 2048],
                    in_=d["csM"][128 * j:128 * j + 128,
                                 2048 * a:2048 * a + 2048]),
                    f"sWd{it % 2}", 16)
        return f
    n.block.sync(dma_for([it for it in range(128) if it % 2 == 0]))
    n.block.gpsimd(dma_for([it for it in range(128) if it % 2 == 1]))

    for a in range(4):
        def t(te, a=a):
            if a == 0:
                ctx.wait(te, "sV", n.ssm_done_sV)
            n.PA.prod_wait(te)
            n.PB.prod_wait(te)
            for j in range(32):
                it = a * 32 + j
                slot = it % NSLOT
                sl = n.dftbuf[:, 2048 * slot:2048 * slot + 2048]
                te.wait_ge(ctx.sems[f"sWd{it % 2}"], wd0[it % 2] + 16 * (it // 2 + 1))
                te.matmul(n.psA[:, 0:512], n.tvalP(j), sl[:, 0:512],
                          start=(j == 0), stop=False)
                te.matmul(n.psB[:, 0:512], n.tvalP(j), sl[:, 512:1024],
                          start=(j == 0), stop=False)
                mma = te.matmul(n.psA[:, 0:512], n.tvalQ(j), sl[:, 1024:1536],
                                start=False, stop=(j == 31))
                mmb = te.matmul(n.psB[:, 0:512], n.tvalQ(j), sl[:, 1536:2048],
                                start=False, stop=(j == 31))
                if j < 31:
                    ctx.inc(mmb, "sTg")
            n.PA.prod_done(mma)
            n.PB.prod_done(mmb)
        n.block.tensor(t)

        def v(ve, a=a):
            n.PA.cons_wait(ve)
            n.PA.cons_done(
                ve.tensor_copy(n.Tmat[:, 1024 * a:1024 * a + 512],
                               n.psA[:, 0:512]))
            n.PB.cons_wait(ve)
            n.PB.cons_done(
                ve.tensor_copy(n.Tmat[:, 1024 * a + 512:1024 * a + 1024],
                               n.psB[:, 0:512]))
        n.block.vector(v)

    def v2(ve):
        ctx.inc(ve.tensor_mul(n.Tmat[:, 0:128], n.Tmat[:, 0:128], n.sb_tri[:]), "sV")
    n.block.vector(v2)
    n.tbuild_done_sV = ctx.val("sV")
    n.tbuild_pbc = ctx.val(n.PB.cname)


def emit_prologue(n):
    ctx, d = n.ctx, n.dram
    g0 = [ctx.val("sG0"), ctx.val("sG1")]
    t1a = ctx.val(n.PT1.tname)
    t2a = ctx.val(n.PT2.tname)

    def g(gq):
        for tc in range(NTT):
            if tc >= 2:
                gq.wait_ge(ctx.sems[n.PT1.tname], t1a + 2 * (tc - 1))
                gq.wait_ge(ctx.sems[n.PT2.tname], t2a + 2 * (tc - 1))
            ctx.inc(gq.dma_start(out=n.stage(tc % 2),
                                 in_=d["x0c"][128 * tc:128 * tc + 128, :]),
                    f"sG{tc % 2}", 16)
    n.block.gpsimd(g)

    for tc in range(NTT):
        for grp in range(4):
            res = n.PT1 if grp % 2 == 0 else n.PT2

            def t(te, tc=tc, grp=grp, res=res):
                if tc == 0 and grp == 0:
                    ctx.wait(te, "sV", n.tbuild_done_sV)
                if grp == 0:
                    te.wait_ge(ctx.sems[f"sG{tc % 2}"],
                               g0[tc % 2] + 16 * (tc // 2 + 1))
                res.prod_wait(te)
                for k in range(8):
                    dt_ = grp * 8 + k
                    mm = te.transpose(
                        res.ap[:, 128 * k:128 * k + 128],
                        n.stage(tc % 2)[:, 128 * dt_:128 * dt_ + 128], n.sb_id[:])
                res.prod_done(mm)
            n.block.tensor(t)

            def v(ve, tc=tc, grp=grp, res=res):
                res.cons_wait(ve)
                for k in range(8):
                    dt_ = grp * 8 + k
                    last = ve.tensor_copy(n.A(dt_)[:, 128 * tc:128 * tc + 128],
                                          res.ap[:, 128 * k:128 * k + 128])
                res.cons_done(last)
            n.block.vector(v)

    def vfin(ve):
        ctx.inc(ve.tensor_copy(n.mrep[:, 0:512], n.mrep[:, 0:512]), "sV")
    n.block.vector(vfin)
    x0T_done = ctx.val("sV")

    def so(sq):
        ctx.wait(sq, "sV", x0T_done)
        for dt_ in range(NDT):
            ctx.inc(sq.dma_start(out=d["xs0"][128 * dt_:128 * dt_ + 128, :],
                                 in_=n.A(dt_)), "sO", 16)
    n.block.scalar(so)
    n.xs0_done_sO = ctx.val("sO")


def emit_lnstats(n, X, dbg=None):
    ctx = n.ctx
    ts0 = ctx.val("sTg")

    for dt_ in range(NDT):
        def v(ve, dt_=dt_):
            if dt_ >= 2:
                ctx.wait(ve, "sTg", ts0 + dt_ - 1)
            ctx.inc(ve.tensor_mul(n.sqst[:, TC * (dt_ % 2):TC * (dt_ % 2) + TC],
                                  X(dt_), X(dt_)), "sVq")
        n.block.vector(v)

        def t(te, dt_=dt_):
            if dt_ == 0:
                n.PE.prod_wait(te)
                n.PF.prod_wait(te)
            ctx.wait(te, "sVq")
            first, last = dt_ == 0, dt_ == NDT - 1
            sq = n.sqst[:, TC * (dt_ % 2):TC * (dt_ % 2) + TC]
            te.matmul(n.psE[0:1, 0:512], n.sb_ones_p1[:], X(dt_)[:, 0:512],
                      start=first, stop=last)
            mmE = te.matmul(n.psE[0:1, 512:1024], n.sb_ones_p1[:], X(dt_)[:, 512:1024],
                            start=first, stop=last)
            mm3 = te.matmul(n.psF[0:1, 0:512], n.sb_ones_p1[:], sq[:, 0:512],
                            start=first, stop=last)
            mmF = te.matmul(n.psF[0:1, 512:1024], n.sb_ones_p1[:], sq[:, 512:1024],
                            start=first, stop=last)
            if last:
                n.PE.prod_done(mmE)
                n.PF.prod_done(mmF)
            else:
                ctx.inc(mm3, "sTg")
        n.block.tensor(t)

    def v2(ve):
        n.PE.cons_wait(ve)
        n.PF.cons_wait(ve)
        ve.tensor_copy(n.r_sum[:], n.psE[0:1, :])
        n.PE.cons_done(ve.tensor_copy(n.r_ssq[:], n.psF[0:1, :]))
        n.PF.cons_done(ve.tensor_scalar_mul(n.r_mean[:], n.r_sum[:], 1.0 / D))
        ve.tensor_mul(n.r_rstd[:], n.r_mean[:], n.r_mean[:])
        ve.scalar_tensor_tensor(n.r_var[:], n.r_ssq[:], 1.0 / D, n.r_rstd[:],
                                ALU.mult, ALU.subtract)
        ctx.inc(ve.tensor_scalar_add(n.r_var[:], n.r_var[:], EPS), "sV")
    n.block.vector(v2)

    def sc(s):
        ctx.wait(s, "sV")
        s.activation(n.r_rstd[:], n.r_var[:], AF.Ln)
        s.activation(n.r_rstd[:], n.r_rstd[:], AF.Exp, scale=-0.5)
        ctx.inc(s.copy(n.sspacer[:], n.sspacer[:]), "sS")
    n.block.scalar(sc)
    rstd_done = ctx.val("sS")

    def t2(te):
        ctx.wait(te, "sS", rstd_done)
        n.PE.prod_wait(te)
        te.matmul(n.psE[:, 0:512], n.sb_ones_1p[:], n.r_mean[0:1, 0:512],
                  start=True, stop=True)
        n.PE.prod_done(
            te.matmul(n.psE[:, 512:1024], n.sb_ones_1p[:], n.r_mean[0:1, 512:1024],
                      start=True, stop=True))
        n.PF.prod_wait(te)
        te.matmul(n.psF[:, 0:512], n.sb_ones_1p[:], n.r_rstd[0:1, 0:512],
                  start=True, stop=True)
        n.PF.prod_done(
            te.matmul(n.psF[:, 512:1024], n.sb_ones_1p[:], n.r_rstd[0:1, 512:1024],
                      start=True, stop=True))
    n.block.tensor(t2)

    def v3(ve):
        n.PE.cons_wait(ve)
        n.PE.cons_done(ve.tensor_copy(n.mrep[:], n.psE[:]))
        n.PF.cons_wait(ve)
        n.PF.cons_done(ve.tensor_copy(n.rrep[:], n.psF[:]))
        ctx.inc(ve.tensor_copy(n.mrep[:, 0:512], n.mrep[:, 0:512]), "sV")
    n.block.vector(v3)

    if dbg is not None and ("dbg_m_" + dbg) in n.dram:
        def dv(sq):
            ctx.wait(sq, "sV")
            ctx.inc(sq.dma_start(out=n.dram["dbg_m_" + dbg], in_=n.r_mean[:]), "sO", 16)
            ctx.inc(sq.dma_start(out=n.dram["dbg_r_" + dbg], in_=n.r_rstd[:]), "sO", 16)
            ctx.inc(sq.dma_start(out=n.dram["dbg_s_" + dbg], in_=n.r_ssq[:]), "sO", 16)
        n.block.scalar(dv)


def emit_apply_ln(n, src, dst, wait_sOx=None):
    ctx = n.ctx

    def v(ve):
        if wait_sOx is not None:
            ctx.wait(ve, "sOx", wait_sOx)
        for dt_ in range(NDT):
            sl = n.sqst[:, TC * (dt_ % 2):TC * (dt_ % 2) + TC]
            ve.tensor_tensor(sl, src(dt_), n.mrep[:], ALU.subtract)
            ctx.inc(ve.tensor_tensor(dst(dt_), sl, n.rrep[:], ALU.mult), "sV")
    n.block.vector(v)
    n.xn_done_sV = ctx.val("sV")


def emit_ffn0(n):
    ctx, d = n.ctx, n.dram

    def t1(te):
        ctx.wait(te, "sV", n.xn_done_sV)
        ctx.wait(te, "sG", n.const_sG)
        n.PA.prod_wait(te)
        for tt in range(NTT):
            for dt_ in range(NDT):
                mm = te.matmul(n.psA[:, 8 * tt:8 * tt + 8],
                               n.B(dt_)[:, 128 * tt:128 * tt + 128],
                               n.wrb[:, 8 * dt_:8 * dt_ + 8],
                               start=(dt_ == 0), stop=(dt_ == NDT - 1))
        n.PA.prod_done(mm)
        n.PB.prod_wait(te)
        n.PB.prod_done(te.matmul(n.psB[:, 0:1], n.sb_ones_1p[:], n.sc11[0:1, 1:2],
                                 start=True, stop=True))
    n.block.tensor(t1)

    def v1(ve):
        n.PB.cons_wait(ve)
        n.PB.cons_done(ve.tensor_copy(n.dvec[:], n.psB[:, 0:1]))
        n.PA.cons_wait(ve)
        n.PA.cons_done(ve.tensor_copy(n.lgst[:], n.psA[:, 0:64]))
        vs = SerialEng(ve, n.mrep[:, 0:512])
        for tt in range(NTT):
            lg = n.lgst[:, 8 * tt:8 * tt + 8]
            vs.reduce_max(n.m1b[:, tt:tt + 1], lg, axis=AX.X)
        vs.tensor_scalar_mul(n.nm1b[:], n.m1b[:], -1.0)
        ctx.inc(ve.tensor_copy(n.mrep[:, 0:512], n.mrep[:, 0:512]), "sV")
    n.block.vector(v1)
    r1 = ctx.val("sV")

    def s1(s):
        ctx.wait(s, "sV", r1)
        for tt in range(NTT):
            s.activation(n.ebuf[:, 8 * tt:8 * tt + 8],
                         n.lgst[:, 8 * tt:8 * tt + 8], AF.Exp,
                         bias=n.nm1b[:, tt:tt + 1])
        ctx.inc(s.copy(n.sspacer[:], n.sspacer[:]), "sS")
    n.block.scalar(s1)
    e_done = ctx.val("sS")

    def v2(ve):
        rv = ve
        ctx.wait(ve, "sS", e_done)
        ve = SerialEng(ve, n.mrep[:, 0:512])
        SS = lambda i: n.sscr[:, 64 * i:64 * i + 8]
        for tt in range(NTT):
            lg = n.lgst[:, 8 * tt:8 * tt + 8]
            eb = n.ebuf[:, 8 * tt:8 * tt + 8]
            ve.tensor_scalar(SS(0), lg, n.m1b[:, tt:tt + 1], None, ALU.is_ge)
            ve.scalar_tensor_tensor(SS(1), SS(0), -1e30, lg, ALU.mult, ALU.add)
            ve.reduce_max(n.m2b[:, tt:tt + 1], SS(1), axis=AX.X)
            ve.tensor_scalar(SS(2), lg, n.m2b[:, tt:tt + 1], None, ALU.is_ge)
            ve.tensor_mul(eb, eb, SS(2))
            ve.reduce_sum(n.rcb[:, tt:tt + 1], eb, axis=AX.X)
        ve.reciprocal(n.rcb[:], n.rcb[:])
        for tt in range(NTT):
            eb = n.ebuf[:, 8 * tt:8 * tt + 8]
            ve.tensor_scalar(n.cmbt[:, 8 * tt:8 * tt + 8], eb,
                             n.rcb[:, tt:tt + 1], None, ALU.mult)
        ctx.inc(rv.tensor_copy(n.mrep[:, 0:512], n.mrep[:, 0:512]), "sV")
    n.block.vector(v2)
    cmb_done = ctx.val("sV")

    def t2(te):
        ctx.wait(te, "sV", cmb_done)
        n.PT1.prod_wait(te)
        for tt in range(NTT):
            mm = te.transpose(n.psT1[0:8, 128 * tt:128 * tt + 128],
                              n.cmbt[:, 8 * tt:8 * tt + 8], n.sb_id[:])
        n.PT1.prod_done(mm)
    n.block.tensor(t2)

    def v3(ve):
        n.PT1.cons_wait(ve)
        n.PT1.cons_done(ve.tensor_copy(n.ctall[:], n.psT1[0:8, :]))
        ctx.inc(ve.tensor_copy(n.mrep[:, 0:512], n.mrep[:, 0:512]), "sV")
    n.block.vector(v3)
    ct_done = ctx.val("sV")

    def t3(te):
        ctx.wait(te, "sV", ct_done)
        n.PE.prod_wait(te)
        te.matmul(n.psE[:, 0:512], n.sb_E8[:], n.ctall[:, 0:512], start=True, stop=True)
        n.PE.prod_done(te.matmul(n.psE[:, 512:1024], n.sb_E8[:], n.ctall[:, 512:1024],
                                 start=True, stop=True))
        n.PF.prod_wait(te)
        for dt_ in range(NDT):
            first, last = dt_ == 0, dt_ == NDT - 1
            te.matmul(n.psF[:, 0:512], n.aab[:, 128 * dt_:128 * dt_ + 128],
                      n.B(dt_)[:, 0:512], start=first, stop=last)
            mm = te.matmul(n.psF[:, 512:1024], n.aab[:, 128 * dt_:128 * dt_ + 128],
                           n.B(dt_)[:, 512:1024], start=first, stop=last)
        n.PF.prod_done(mm)
    n.block.tensor(t3)

    def v4(ve):
        n.PE.cons_wait(ve)
        n.PE.cons_done(ve.tensor_copy(n.cexp[:], n.psE[:]))
        n.PF.cons_wait(ve)
        n.PF.cons_done(ve.tensor_copy(n.hbuf[:], n.psF[:]))
        ctx.inc(ve.tensor_mul(n.hbuf[:], n.hbuf[:], n.cexp[:]), "sV")
    n.block.vector(v4)
    hsc_done = ctx.val("sV")

    wl0 = [ctx.val("sWl0"), ctx.val("sWl1")]
    wf0 = ctx.val("sWf")

    def g1(gq):
        for oc in range(NDT):
            if oc == 0:
                ctx.wait(gq, n.PB.cname, n.tbuild_pbc)
            if oc >= 2:
                ctx.wait(gq, "sWf", wf0 + oc - 1)
            ctx.inc(gq.dma_start(out=n.wbuf[:, D * (oc % 2):D * (oc % 2) + D],
                                 in_=d["wbase_blk"][oc]), f"sWl{oc % 2}", 16)
    n.block.gpsimd(g1)

    for oc in range(NDT):
        res = n.PE if oc % 2 == 0 else n.PF

        def t4(te, oc=oc, res=res):
            if oc == 0:
                ctx.wait(te, "sV", hsc_done)
            res.prod_wait(te)
            te.wait_ge(ctx.sems[f"sWl{oc % 2}"], wl0[oc % 2] + 16 * (oc // 2 + 1))
            wb = lambda dt_: n.wbuf[:, D * (oc % 2) + 128 * dt_:D * (oc % 2) + 128 * dt_ + 128]
            for dt_ in range(NDT):
                te.matmul(res.ap[:, 0:512], wb(dt_), n.B(dt_)[:, 0:512],
                          start=(dt_ == 0), stop=False)
                mw = te.matmul(res.ap[:, 512:1024], wb(dt_), n.B(dt_)[:, 512:1024],
                               start=(dt_ == 0), stop=False)
            ctx.inc(mw, "sWf")
            te.matmul(res.ap[:, 0:512], n.blb[:, 128 * oc:128 * oc + 128],
                      n.hsc[:, 0:512], start=False, stop=True)
            res.prod_done(
                te.matmul(res.ap[:, 512:1024], n.blb[:, 128 * oc:128 * oc + 128],
                          n.hsc[:, 512:1024], start=False, stop=True))
        n.block.tensor(t4)

        def v5(ve, oc=oc, res=res):
            res.cons_wait(ve)
            res.cons_done(ve.tensor_scalar(n.A(oc), res.ap[:], n.bbuf[:, oc:oc + 1],
                                           None, ALU.add))
        n.block.vector(v5)

    def v5f(ve):
        ctx.inc(ve.tensor_copy(n.mrep[:, 0:512], n.mrep[:, 0:512]), "sV")
    n.block.vector(v5f)
    n.ffn0_done_sV = ctx.val("sV")


def emit_conv(n):
    ctx = n.ctx

    for I in range(NDT):
        res = n.PE if I % 2 == 0 else n.PF

        def t(te, I=I, res=res):
            if I == 0:
                ctx.wait(te, "sV", n.ffn0_done_sV)
            res.prod_wait(te)
            for J in range(I + 1):
                dd = I - J
                first, last = J == 0, J == I
                te.matmul(res.ap[:, 0:512], n.Tmat[:, 128 * dd:128 * dd + 128],
                          n.B(J)[:, 0:512], start=first, stop=last)
                mm = te.matmul(res.ap[:, 512:1024], n.Tmat[:, 128 * dd:128 * dd + 128],
                               n.B(J)[:, 512:1024], start=first, stop=last)
            res.prod_done(mm)
        n.block.tensor(t)

        def v(ve, I=I, res=res):
            res.cons_wait(ve)
            sl = n.rstage[:, TC * (I % 2):TC * (I % 2) + TC]
            res.cons_done(ve.scalar_tensor_tensor(sl, n.B(I), n.dvec[:], res.ap[:],
                                                  ALU.mult, ALU.add))
            ctx.inc(ve.tensor_tensor(n.A(I), n.A(I), sl, ALU.add), "sV")
        n.block.vector(v)
    n.conv_done_sV = ctx.val("sV")


def emit_x1(n):
    ctx, d = n.ctx, n.dram
    w0 = [ctx.val("sWx0"), ctx.val("sWx1")]
    vx0 = ctx.val("sVx")
    ox0 = ctx.val("sOx")

    def sy(sy_):
        ctx.wait(sy_, "sO", n.xs0_done_sO)
        ctx.wait(sy_, "sV", n.conv_done_sV)
        for dt_ in range(NDT):
            if dt_ >= 2:
                ctx.wait(sy_, "sVx", vx0 + dt_ - 1)
            ctx.inc(sy_.dma_start(out=n.rstage[:, TC * (dt_ % 2):TC * (dt_ % 2) + TC],
                                  in_=d["xs0"][128 * dt_:128 * dt_ + 128, :]),
                    f"sWx{dt_ % 2}", 16)
    n.block.sync(sy)

    def v(ve):
        for dt_ in range(NDT):
            ve.wait_ge(ctx.sems[f"sWx{dt_ % 2}"], w0[dt_ % 2] + 16 * (dt_ // 2 + 1))
            sl = n.rstage[:, TC * (dt_ % 2):TC * (dt_ % 2) + TC]
            ctx.inc(ve.tensor_tensor(n.B(dt_), sl, n.A(dt_), ALU.add), "sVx")
    n.block.vector(v)

    def so(sq):
        for dt_ in range(NDT):
            ctx.wait(sq, "sVx", vx0 + dt_ + 1)
            ctx.inc(sq.dma_start(out=d["x1_out"][128 * dt_:128 * dt_ + 128, :],
                                 in_=n.B(dt_)), "sOx", 16)
    n.block.scalar(so)
    x1out_sOx = ctx.val("sOx")

    emit_lnstats(n, n.B)
    emit_apply_ln(n, n.B, n.B, wait_sOx=x1out_sOx)

    def so2(sq):
        ctx.wait(sq, "sV", n.xn_done_sV)
        for dt_ in range(NDT):
            ctx.inc(sq.dma_start(out=d["xn1_out"][128 * dt_:128 * dt_ + 128, :],
                                 in_=n.B(dt_)), "sOx", 16)
        ctx.wait(sq, "sOx")
    n.block.scalar(so2)


def emit_taps(n, taps, tap_out):
    if not taps:
        def fin(sq):
            for s_ in ["sV", "sT", "sS", "sO", "sOx", "sW", "sG", "sWt0",
                       "sWt1", "sWd0", "sWd1", "sG0", "sG1", "sWl0", "sWl1",
                       "sWx0", "sWx1"]:
                n.ctx.wait(sq, s_)
        n.block.scalar(fin)
        return
    ctx = n.ctx

    def f(sq):
        for s_ in ["sV", "sT", "sS", "sO", "sOx", "sW", "sG", "sVx", "sVq",
                   "sWt0", "sWt1", "sWd0", "sWd1", "sG0", "sG1", "sWl0",
                   "sWl1", "sWx0", "sWx1"]:
            ctx.wait(sq, s_)
        for tname, shape, dt in taps:
            src = resolve_tap(n, tname)
            ctx.inc(sq.dma_start(out=tap_out[tname], in_=src), "sO", 16)
        ctx.wait(sq, "sO")
    n.block.scalar(f)


def resolve_tap(n, tname):
    if tname.startswith("bufA"):
        d = int(tname.split("_")[1])
        return n.A(d)
    if tname.startswith("bufB"):
        d = int(tname.split("_")[1])
        return n.B(d)
    if tname in n.dram:
        return n.dram[tname]
    return getattr(n, tname)[:]



def _prep_launch1_inputs(inputs):
    t = _host_tables()
    x0 = np.asarray(inputs["L0_attn_u"]).reshape(T, D)

    lam_ri = np.asarray(inputs["L0_attn_Lambda_ri"])
    lam = np.concatenate([lam_ri[:, 0], lam_ri[:, 1]]).reshape(128, 1).astype(np.float32)
    vc = np.asarray(inputs["L0_attn_Vc_ri"])
    base = {
        "lam": lam,
        "vc_rT": np.ascontiguousarray(vc[:, :, 0].T.astype(np.float32)),
        "vc_iT": np.ascontiguousarray(vc[:, :, 1].T.astype(np.float32)),
        "bpv": np.asarray(inputs["L0_attn_Bp"]).reshape(64, 1).astype(np.float32),
        "ct_r": np.asarray(inputs["L0_attn_Ct_ri"]).reshape(64, 2)[:, 0:1].copy(),
        "ct_i": np.asarray(inputs["L0_attn_Ct_ri"]).reshape(64, 2)[:, 1:2].copy(),
        "p_r": np.asarray(inputs["L0_attn_p_ri"])[:, 0:1].copy(),
        "p_i": np.asarray(inputs["L0_attn_p_ri"])[:, 1:2].copy(),
        "q_r": np.asarray(inputs["L0_attn_q_ri"])[:, 0:1].copy(),
        "q_i": np.asarray(inputs["L0_attn_q_ri"])[:, 1:2].copy(),
        "dpar": np.asarray(inputs["L0_attn_D"]).reshape(1, 1).astype(np.float32),
        "logstep": np.asarray(inputs["L0_attn_log_step"]).reshape(1, 1).astype(np.float32),
        "b_base": np.asarray(inputs["L0_ffn_b_base"]).reshape(D, 1).astype(np.float32),
        "wrouT": np.ascontiguousarray(np.asarray(inputs["L0_ffn_W_router"]).T.astype(np.float32)),
        "aallT": np.ascontiguousarray(
            np.asarray(inputs["L0_ffn_A_stack"]).reshape(128, D).T.astype(np.float32)),
        "ball": np.ascontiguousarray(
            np.asarray(inputs["L0_ffn_B_stack"]).transpose(0, 2, 1).reshape(128, D).astype(np.float32)),
        "wbase_blk": pack_lhsT(np.ascontiguousarray(
            np.asarray(inputs["L0_ffn_W_base"]).T.astype(np.float32))),
        "t_r_row": t["t_r_row"], "t_i_row": t["t_i_row"],
        "cl_r": t["cl_r"], "cl_i": t["cl_i"], "cA": t["cA"], "sA": t["sA"],
        "csM": t["csM"], "trimask": t["trimask"],
        "id_bf": t["id_bf"], "ones_p1_bf": t["ones_p1_bf"],
        "ones_1p_f32": t["ones_1p_f32"], "E8": t["E8"],
    }
    base["b_base_pk"] = np.ascontiguousarray(base.pop("b_base").reshape(32, 128).T)
    base["wrou_pk"] = np.ascontiguousarray(
        base.pop("wrouT").reshape(32, 128, 8).transpose(1, 0, 2).reshape(128, 256))
    base["aab_pk"] = np.ascontiguousarray(
        base.pop("aallT").reshape(32, 128, 128).transpose(1, 0, 2).reshape(128, 4096))
    in_maps = []
    for c in range(NCORE):
        m = dict(base)
        m["x0c"] = np.ascontiguousarray(x0[_strip_tokens(c), :])
        in_maps.append(m)
    return in_maps






def _gcol(s):
    r = s if s <= 7 else 15 - s
    return 1024 * r + (0 if s <= 7 else 512)


def _ktile_list(s):
    out = []
    for sk in range(s + 1):
        base = _gcol(sk)
        for u in range(4):
            out.append((base + 128 * u, u if sk == s else None))
    return out


def build_launch2(taps=()):
    nc = bass.Bass("TRN2", target_bir_lowering=False)
    dram = {}
    din, dout, dint = _mk_dram_helpers(nc, dram)

    din("xn1g", [8 * D, TC], BF16)
    din("wq_blk", [4, 128, D], F32)
    din("wk_blk", [1, 128, D], F32)
    din("wv_blk", [1, 128, D], F32)
    din("bq", [512, 1], F32)
    din("bk", [128, 1], F32)
    din("bv", [128, 1], F32)
    din("ctabK", [128, T], BF16)
    din("stabK", [128, T], BF16)
    din("ctabQ", [128, T], BF16)
    din("stabQ", [128, T], BF16)
    din("Jmat", [128, 128], BF16)
    din("dmask", [4, 128, 512], BF16)
    din("id_bf", [128, 128], BF16)
    din("ones_p1_bf", [128, 1], BF16)
    din("ones_1p_f32", [1, 128], F32)
    dout("attnout", [512, T], BF16)

    tap_out = {}
    for tname, shape, dt in taps:
        tap_out[tname] = dout("tap_" + tname, list(shape), dt)

    with contextlib.ExitStack() as stack:
        n = types.SimpleNamespace()
        n.nc = nc
        n.dram = dram
        sb = lambda name, shape, dt: stack.enter_context(nc.sbuf_tensor(name, list(shape), dt))
        ps = lambda name, shape, dt: stack.enter_context(nc.psum_tensor(name, list(shape), dt))

        n.Kbuf = sb("Kbuf", [128, T], BF16)
        n.Vtok = sb("Vtok", [128, T], BF16)
        n.wqb = sb("wqb", [128, 4 * D], BF16)
        n.wkb = sb("wkb", [128, D], BF16)
        n.wvb = sb("wvb", [128, D], BF16)
        n.xbig = sb("xbig", [128, NDT * 512], BF16)
        n.tstg = sb("tstg", [128, 4 * 512], BF16)
        n.kst = sb("kst", [128, 512], F32)
        n.vst = sb("vst", [128, 512], BF16)
        n.qro = sb("qro", [128, 4 * 512], BF16)
        n.qtmp = sb("qtmp", [128, 512], F32)
        n.pbuf = sb("pbuf", [128, 4 * 512], BF16)
        n.pxtra = n.pbuf[:, 1536:2048]
        n.sqb = sb("sqb", [128, 512], BF16)
        n.sc2b = sb("sc2b", [1, 32], F32)
        n.rowz = sb("rowz", [1, 512], F32)
        n.sc2 = sb("sc2", [1, 16], F32)
        n.negb = sb("negb", [128, 16], F32)
        n.accden = sb("accden", [128, 512], F32)
        n.rrepA = sb("rrepA", [128, 512], F32)
        n.bqb = sb("bqb", [128, 4], F32)
        n.bkb = sb("bkb", [128, 1], F32)
        n.bvb = sb("bvb", [128, 1], F32)
        n.sb_J = sb("sb_J", [128, 128], BF16)
        n.sb_dm = sb("sb_dm", [128, 4 * 512], BF16)
        n.sb_id = sb("sb_id", [128, 128], BF16)
        n.sb_ones_p1 = sb("sb_ones_p1", [128, 1], BF16)
        n.sb_ones_1p = sb("sb_ones_1p", [1, 128], F32)

        n.psQ = ps("psQ", [128, 1024], F32)
        n.psS1 = ps("psS1", [128, 512], F32)
        n.psS2 = ps("psS2", [128, 512], F32)
        psFD = ps("psFD", [128, 1024], F32)
        n.psAV = psFD[:, 0:512]
        n.psD = psFD[:, 512:1024]
        n.psVT = ps("psVT", [128, 512], BF16)

        ctx = n.ctx = Ctx(nc, stack)
        for s_ in ["sW", "sG", "sO", "sS", "sV", "sT", "sX", "sKV", "sP",
                   "sGt0", "sGt1"]:
            ctx.sem(s_)
        n.PQ = PsRes(ctx, n.psQ)
        n.PS1 = PsRes(ctx, n.psS1)
        n.PS2 = PsRes(ctx, n.psS2)
        n.PAV = PsRes(ctx, n.psAV)
        n.PD = PsRes(ctx, n.psD)
        n.PVT = PsRes(ctx, n.psVT)
        n.fused = False

        with nc.Block() as block:
            n.block = block
            emit_l2_consts_sm(n)
            emit_l2_wload(n)
            emit_l2_kv(n)
            emit_l2_attn(n)
            emit_l2_taps(n, taps, tap_out)
    return nc, dram


def emit_l2_consts_sm(n, with_dm=True):
    ctx, d = n.ctx, n.dram

    def f(sy):
        consts = [(n.sb_J, d["Jmat"]),
                  (n.bkb, d["bk"]), (n.bvb, d["bv"])]
        if not n.fused:
            consts += [(n.sb_id, d["id_bf"]), (n.sb_ones_p1, d["ones_p1_bf"]),
                       (n.sb_ones_1p, d["ones_1p_f32"])]
        for dst, src in consts:
            ctx.inc(sy.dma_start(out=dst[:], in_=src), "sW", 16)
        if n.fused:
            ctx.inc(sy.dma_start(out=n.gidx[:], in_=d["gidx"]), "sW", 16)
        for i in range(4):
            if with_dm:
                ctx.inc(sy.dma_start(out=n.sb_dm[:, 512 * i:512 * i + 512],
                                     in_=d["dmask"][i]), "sW", 16)
            ctx.inc(sy.dma_start(out=n.bqb[:, i:i + 1],
                                 in_=d["bq"][128 * i:128 * i + 128, 0:1]), "sW", 16)
    n.block.sync(f)


def emit_l2_dm(n):
    ctx, d = n.ctx, n.dram

    def f(sy):
        ctx.wait(sy, "sV", n.conv_done_sV)
        for i in range(4):
            ctx.inc(sy.dma_start(out=n.sb_dm[:, 512 * i:512 * i + 512],
                                 in_=d["dmask"][i]), "sW", 16)
    n.block.sync(f)


def emit_l2_wload(n):
    ctx, d = n.ctx, n.dram

    def g(gq):
        if n.fused:
            ctx.wait(gq, "sVx")
            ctx.wait(gq, "sOx")
        ctx.inc(gq.dma_start(out=n.wkb[:], in_=d["wk_blk"][0]), "sG", 16)
        ctx.inc(gq.dma_start(out=n.wvb[:], in_=d["wv_blk"][0]), "sG", 16)
        for i in range(4):
            ctx.inc(gq.dma_start(out=n.wqb[:, D * i:D * i + D],
                                 in_=d["wq_blk"][i]), "sG", 16)
    n.block.gpsimd(g)
    n.c_sW = ctx.val("sW")
    n.c_sG = ctx.val("sG")


def emit_l2_kv(n):
    ctx, d = n.ctx, n.dram
    w0 = ctx.val("sW")
    gt0 = [ctx.val("sGt0"), ctx.val("sGt1")]
    tslot = lambda i: n.tstg[:, 512 * (i % 2):512 * (i % 2) + 512]
    tslot2 = lambda i: n.tstg[:, 1024 + 512 * (i % 2):1024 + 512 * (i % 2) + 512]
    xt = lambda dt_: n.xbig[:, 512 * dt_:512 * dt_ + 512]

    for tc in range(16):
        r, h2 = tc // 2, tc % 2

        def syx(sy_, tc=tc, r=r, h2=h2):
            if tc == 0:
                ctx.wait(sy_, "sW", n.c_sW)
                if n.fused:
                    ctx.wait(sy_, "sC1")
                    ctx.wait(sy_, "sOx")
                    ctx.wait(sy_, "sVx")
            else:
                ctx.wait(sy_, n.PS1.tname)
                ctx.wait(sy_, n.PS2.tname)
            for dt_ in range(NDT):
                ctx.inc(sy_.dma_start(
                    out=xt(dt_),
                    in_=d["xn1g"][D * r + 128 * dt_:D * r + 128 * dt_ + 128,
                                  512 * h2:512 * h2 + 512]), "sW", 16)
        n.block.sync(syx)

        def syt(sy_, tc=tc):
            if tc >= 2:
                ctx.wait(sy_, "sV")
            cs = slice(512 * tc, 512 * tc + 512)
            ctx.inc(sy_.dma_start(out=tslot(tc), in_=d["ctabK"][:, cs]),
                    f"sGt{tc % 2}", 16)
            ctx.inc(sy_.dma_start(out=tslot2(tc), in_=d["stabK"][:, cs]),
                    f"sGt{tc % 2}", 16)
        n.block.gpsimd(syt)

        def t(te, tc=tc):
            if tc == 0:
                ctx.wait(te, "sG", n.c_sG)
                if n.fused:
                    ctx.wait(te, n.PF.cname)
            te.wait_ge(ctx.sems["sW"], w0 + 16 * NDT * (tc + 1))
            n.PS1.prod_wait(te)
            n.PS2.prod_wait(te)
            for dt_ in range(NDT):
                first, last = dt_ == 0, dt_ == NDT - 1
                wk = n.wkb[:, 128 * dt_:128 * dt_ + 128]
                wv = n.wvb[:, 128 * dt_:128 * dt_ + 128]
                mmk = te.matmul(n.psS1[:, 0:512], wk, xt(dt_), start=first, stop=last)
                mmv = te.matmul(n.psS2[:, 0:512], wv, xt(dt_), start=first, stop=last)
            n.PS1.prod_done(mmk)
            n.PS2.prod_done(mmv)
        n.block.tensor(t)

        def v1(ve, tc=tc):
            n.PS1.cons_wait(ve)
            if tc > 0:
                ctx.wait(ve, n.PVT.tname)
                ctx.wait(ve, n.PAV.tname)
            n.PS1.cons_done(
                ve.tensor_scalar(n.kst[:], n.psS1[:, 0:512], n.bkb[:], None, ALU.add))
            n.PS2.cons_wait(ve)
            n.PS2.cons_done(
                ve.tensor_scalar(n.vst[:], n.psS2[:, 0:512], n.bvb[:], None, ALU.add))
            ctx.inc(ve.tensor_copy(n.pbuf[:, 0:512], n.kst[:]), "sV")
        n.block.vector(v1)

        def t2(te, tc=tc):
            ctx.wait(te, "sV")
            n.PAV.prod_wait(te)
            n.PAV.prod_done(
                te.matmul(n.psAV[:, 0:512], n.sb_J[:], n.pbuf[:, 0:512],
                          start=True, stop=True))
        n.block.tensor(t2)

        def v2(ve, tc=tc):
            n.PAV.cons_wait(ve)
            ve.wait_ge(ctx.sems[f"sGt{tc % 2}"], gt0[tc % 2] + 32 * (tc // 2 + 1))
            if tc > 0:
                ctx.wait(ve, n.PD.tname)
            cs = slice(512 * tc, 512 * tc + 512)
            ve.tensor_mul(n.qtmp[:], n.kst[:], tslot(tc))
            n.PAV.cons_done(
                ve.tensor_mul(n.kst[:], n.psAV[:, 0:512], tslot2(tc)))
            ctx.inc(ve.tensor_add(n.Kbuf[:, cs], n.qtmp[:], n.kst[:]), "sV")
            ctx.inc(ve.tensor_mul(n.sqb[:], n.Kbuf[:, cs], n.Kbuf[:, cs]), "sV")
        n.block.vector(v2)

        def t3(te, tc=tc):
            ctx.wait(te, "sV")
            n.PD.prod_wait(te)
            n.PD.prod_done(
                te.matmul(n.psD[0:1, 0:512], n.sb_ones_p1[:], n.sqb[:],
                          start=True, stop=True))
            n.PVT.prod_wait(te)
            for i in range(4):
                mm = te.transpose(n.psVT[:, 128 * i:128 * i + 128],
                                  n.vst[:, 128 * i:128 * i + 128], n.sb_id[:])
            n.PVT.prod_done(mm)
        n.block.tensor(t3)

        def v3(ve, tc=tc):
            n.PD.cons_wait(ve)
            n.PD.cons_done(ve.tensor_copy(n.rowz[:], n.psD[0:1, 0:512]))
            ve.reduce_max(n.sc2b[0:1, tc:tc + 1], n.rowz[:], axis=AX.X)
            n.PVT.cons_wait(ve)
            for i in range(4):
                kt = 4 * tc + i
                last = ve.tensor_copy(n.Vtok[:, 128 * kt:128 * kt + 128],
                                      n.psVT[:, 128 * i:128 * i + 128])
            n.PVT.cons_done(last)
        n.block.vector(v3)

    def v4(ve):
        ve.tensor_copy(n.rrepA[:], n.rrepA[:])
        ve.reduce_max(n.sc2[0:1, 0:1], n.sc2b[0:1, 0:16], axis=AX.X)
        ctx.inc(ve.tensor_copy(n.rrepA[:], n.rrepA[:]), "sV")
    n.block.vector(v4)

    def s1(s):
        ctx.wait(s, "sV")
        s.activation(n.sc2[0:1, 2:4], n.sc2[0:1, 0:2], AF.Sqrt)
        ctx.inc(s.copy(n.sspacer[:], n.sspacer[:]), "sS")
    n.block.scalar(s1)
    n.kmax_sS = ctx.val("sS")


def emit_l2_attn(n):
    ctx = n.ctx
    n.onesf = n.negb[:, 8:9]

    def vinit(ve):
        ctx.inc(ve.memset(n.onesf[:], 1.0), "sV")
    n.block.vector(vinit)
    for s in range(16):
        emit_l2_strip(n, s)


def emit_l2_strip(n, s):
    ctx, d = n.ctx, n.dram
    gc = _gcol(s)
    kts = _ktile_list(s)
    rblk = s if s <= 7 else 15 - s
    coff = 0 if s <= 7 else 512

    w0 = ctx.val("sW")
    xt = lambda dt_: n.xbig[:, 512 * dt_:512 * dt_ + 512]

    def syq(sy_):
        ctx.wait(sy_, n.PS1.tname)
        ctx.wait(sy_, n.PS2.tname)
        ctx.wait(sy_, n.PQ.tname)
        for dt_ in range(NDT):
            ctx.inc(sy_.dma_start(
                out=xt(dt_),
                in_=d["xn1g"][D * rblk + 128 * dt_:D * rblk + 128 * dt_ + 128,
                              coff:coff + 512]), "sW", 16)
    n.block.sync(syq)

    def syt(sy_):
        ctx.wait(sy_, "sV")
        cs = slice(gc, gc + 512)
        ctx.inc(sy_.dma_start(out=n.tstg[:, 0:512], in_=d["ctabQ"][:, cs]), "sG", 16)
        ctx.inc(sy_.dma_start(out=n.tstg[:, 512:1024], in_=d["stabQ"][:, cs]), "sG", 16)
    n.block.gpsimd(syt)

    for pas in range(2):
        def tq(te, pas=pas):
            if pas == 0:
                te.wait_ge(ctx.sems["sW"], w0 + 16 * NDT)
            n.PQ.prod_wait(te)
            for dt_ in range(NDT):
                first, last = dt_ == 0, dt_ == NDT - 1
                for qi in range(2):
                    qh = 2 * pas + qi
                    mm = te.matmul(n.psQ[:, 512 * qi:512 * qi + 512],
                                   n.wqb[:, D * qh + 128 * dt_:D * qh + 128 * dt_ + 128],
                                   xt(dt_), start=first, stop=last)
            n.PQ.prod_done(mm)
        n.block.tensor(tq)

        def vq(ve, pas=pas):
            n.PQ.cons_wait(ve)
            if pas == 0:
                ctx.wait(ve, "sG")
                ctx.wait(ve, "sO")
            for qi in range(2):
                qh = 2 * pas + qi
                ve.tensor_scalar(n.qtmp[:], n.psQ[:, 512 * qi:512 * qi + 512],
                                 n.bqb[:, qh:qh + 1], None, ALU.add)
                last = ve.tensor_copy(n.qro[:, 512 * qh:512 * qh + 512], n.qtmp[:])
            n.PQ.cons_done(last)
            ctx.inc(ve.tensor_copy(n.rrepA[:], n.rrepA[:]), "sV")
        n.block.vector(vq)

    for qh in range(4):
        def tj(te, qh=qh):
            ctx.wait(te, "sV")
            n.PS1.prod_wait(te)
            n.PS1.prod_done(
                te.matmul(n.psS1[:, 0:512], n.sb_J[:],
                          n.qro[:, 512 * qh:512 * qh + 512], start=True, stop=True))
        n.block.tensor(tj)

        def vj(ve, qh=qh):
            n.PS1.cons_wait(ve)
            sl = n.qro[:, 512 * qh:512 * qh + 512]
            ve.tensor_mul(n.qtmp[:], sl, n.tstg[:, 0:512])
            n.PS1.cons_done(ve.tensor_mul(n.kst[:], n.psS1[:, 0:512],
                                          n.tstg[:, 512:1024]))
            ve.tensor_add(sl, n.qtmp[:], n.kst[:])
            ctx.inc(ve.tensor_mul(n.sqb[:], sl, sl), "sV")
        n.block.vector(vj)

        def tb(te, qh=qh):
            ctx.wait(te, "sV")
            n.PD.prod_wait(te)
            n.PD.prod_done(
                te.matmul(n.psD[0:1, 0:512], n.sb_ones_p1[:], n.sqb[:],
                          start=True, stop=True))
        n.block.tensor(tb)

        def vb(ve, qh=qh):
            n.PD.cons_wait(ve)
            n.PD.cons_done(ve.tensor_copy(n.rowz[:], n.psD[0:1, 0:512]))
            ve.reduce_max(n.sc2[0:1, 8 + qh:9 + qh], n.rowz[:], axis=AX.X)
            ctx.inc(ve.tensor_copy(n.rrepA[:], n.rrepA[:]), "sV")
        n.block.vector(vb)

    def sball(sc):
        ctx.wait(sc, "sV")
        if s == 0:
            ctx.wait(sc, "sS", n.kmax_sS)
        sc.activation(n.sc2[0:1, 12:16], n.sc2[0:1, 8:12], AF.Sqrt)
        ctx.inc(sc.copy(n.sspacer[:], n.sspacer[:]), "sS")
    n.block.scalar(sball)

    def vball(ve):
        ctx.wait(ve, "sS")
        ve.tensor_scalar(n.sc2b[0:1, 24:28], n.sc2[0:1, 12:16],
                         n.sc2[0:1, 2:3], -1.0, ALU.mult, ALU.mult)
        ctx.inc(ve.tensor_copy(n.rrepA[:], n.rrepA[:]), "sV")
    n.block.vector(vball)

    def tneg(te):
        ctx.wait(te, "sV")
        n.PD.prod_wait(te)
        n.PD.prod_done(
            te.matmul(n.psD[:, 0:4], n.sb_ones_1p[:], n.sc2b[0:1, 24:28],
                      start=True, stop=True))
    n.block.tensor(tneg)

    def vneg(ve):
        n.PD.cons_wait(ve)
        n.PD.cons_done(ve.tensor_copy(n.negb[:, 0:4], n.psD[:, 0:4]))
        ctx.inc(ve.tensor_copy(n.rrepA[:], n.rrepA[:]), "sV")
    n.block.vector(vneg)
    n.negb_sV = ctx.val("sV")

    prev_drain = None
    for qh in range(4):
        tiles, drain = emit_l2_kloop(n, s, qh, kts)
        if prev_drain is not None:
            prev_drain()
        tiles()
        prev_drain = drain
    prev_drain()


def emit_l2_kloop(n, s, qh, kts):
    ctx, d = n.ctx, n.dram
    nk = len(kts)
    pslots = 4
    look = 3
    p0 = ctx.val("sP")
    ca0 = ctx.val(n.PS1.cname)
    cb0 = ctx.val(n.PS2.cname)
    ta0 = ctx.val(n.PS1.tname)
    tb0 = ctx.val(n.PS2.tname)
    pe_t0 = ctx.val(n.PE.tname)
    pe_c0 = ctx.val(n.PE.cname)
    sS0 = ctx.val("sS")
    slots = [n.pbuf[:, 0:512], n.pbuf[:, 512:1024], n.pbuf[:, 1024:1536],
             n.pxtra[:, 0:512]]
    pslot = lambda i: slots[i % pslots]
    banks = [n.psS1[:, 0:512], n.psS2[:, 0:512],
             n.psE[:, 0:512], n.psE[:, 512:1024]]
    bank = lambda i: banks[i % 4]
    prod_names = (n.PS1.tname, n.PS2.tname, n.PE.tname, n.PE.tname)
    cons_names = (n.PS1.cname, n.PS2.cname, n.PE.cname, n.PE.cname)
    cons_base = (ca0, cb0, pe_c0, pe_c0)

    def prod_val(i):
        b, pos = i % 4, i // 4
        return (ta0 + pos + 1, tb0 + pos + 1,
                pe_t0 + 2 * pos + 1, pe_t0 + 2 * pos + 2)[b]

    def cons_val(i):
        b, pos = i % 4, i // 4
        return (ca0 + pos + 1, cb0 + pos + 1,
                pe_c0 + 2 * pos + 1, pe_c0 + 2 * pos + 2)[b]

    def tS(te, j):
        kc = kts[j][0]
        if j < 4:
            te.wait_ge(ctx.sems[cons_names[j % 4]], cons_base[j % 4])
        else:
            te.wait_ge(ctx.sems[cons_names[j % 4]], cons_val(j - 4))
        ctx.inc(te.matmul(bank(j), n.Kbuf[:, kc:kc + 128],
                          n.qro[:, 512 * qh:512 * qh + 512],
                          start=True, stop=True), prod_names[j % 4])

    def tpre(te):
        ctx.wait(te, "sV", n.negb_sV)
        for j in range(min(look, nk)):
            tS(te, j)
    n.block.tensor(tpre)

    def sE_all(sc):
        for i in range(nk):
            sc.wait_ge(ctx.sems[prod_names[i % 4]], prod_val(i))
            if i >= pslots:
                sc.wait_ge(ctx.sems["sP"], p0 + (i - pslots + 1))
            ctx.inc(sc.activation(pslot(i), bank(i), AF.Exp,
                                  bias=n.negb[:, qh:qh + 1]), cons_names[i % 4])
        ctx.inc(sc.copy(n.sspacer[:], n.sspacer[:]), "sS")
    n.block.scalar(sE_all)

    def expland(eng, i):
        j = i + 1
        if j < nk:
            eng.wait_ge(ctx.sems[cons_names[j % 4]], cons_val(j))
        else:
            eng.wait_ge(ctx.sems["sS"], sS0 + 1)

    def emit_tiles():
        for i, (kc, dm) in enumerate(kts):
            def vAcc(ve, i=i, dm=dm):
                expland(ve, i)
                sl = pslot(i)
                if dm is not None:
                    ve.tensor_mul(sl, sl, n.sb_dm[:, 512 * dm:512 * dm + 512])
                if i == 0:
                    acc_i = ve.tensor_copy(n.accden[:], sl)
                else:
                    acc_i = ve.tensor_tensor(n.accden[:], n.accden[:], sl,
                                             ALU.add)
                ctx.inc(acc_i, "sP")
            n.block.vector(vAcc)

            def tMain(te, i=i, kc=kc, dm=dm):
                if dm is not None:
                    te.wait_ge(ctx.sems["sP"], p0 + i + 1)
                else:
                    expland(te, i)
                first, last = i == 0, i == nk - 1
                sl = pslot(i)
                if first:
                    n.PAV.prod_wait(te)
                mm = te.matmul(n.psAV[:, 0:512], n.Vtok[:, kc:kc + 128], sl,
                               start=first, stop=last)
                if last:
                    n.PAV.prod_done(mm)
                if i + look < nk:
                    tS(te, i + look)
            n.block.tensor(tMain)

    def emit_drain():
        def tD(te):
            te.wait_ge(ctx.sems["sP"], p0 + nk)
            n.PD.prod_wait(te)
            n.PD.prod_done(te.matmul(n.psD[0:1, 0:512], n.onesf[:],
                                     n.accden[:], start=True, stop=True))
        n.block.tensor(tD)

        def vE(ve):
            n.PD.cons_wait(ve)
            n.PD.cons_done(ve.tensor_copy(n.rowz[:], n.psD[0:1, 0:512]))
            ve.reciprocal(n.rowz[:], n.rowz[:])
            ctx.inc(ve.tensor_copy(n.rrepA[:], n.rrepA[:]), "sV")
        n.block.vector(vE)

        def tR(te):
            ctx.wait(te, "sV")
            n.PD.prod_wait(te)
            n.PD.prod_done(te.matmul(n.psD[:, 0:512], n.sb_ones_1p[:],
                                     n.rowz[:], start=True, stop=True))
        n.block.tensor(tR)

        def vF(ve):
            n.PD.cons_wait(ve)
            n.PD.cons_done(ve.tensor_copy(n.rrepA[:], n.psD[:, 0:512]))
            n.PAV.cons_wait(ve)
            n.PAV.cons_done(
                ve.tensor_mul(n.sqb[:], n.psAV[:, 0:512], n.rrepA[:]))
            ctx.inc(ve.tensor_copy(n.qtmp[:, 0:512], n.qtmp[:, 0:512]), "sV")
        n.block.vector(vF)

        if not n.fused:
            def so(sq):
                ctx.wait(sq, "sV")
                gcs = _gcol(s)
                ctx.inc(sq.dma_start(
                    out=n.dram["attnout"][128 * qh:128 * qh + 128,
                                          gcs:gcs + 512],
                    in_=n.sqb[:]), "sO", 16)
            n.block.scalar(so)
        else:
            def tT(te):
                ctx.wait(te, "sV")
                n.PT1.prod_wait(te)
                for i in range(4):
                    mm = te.transpose(n.psT1[:, 128 * i:128 * i + 128],
                                      n.sqb[:, 128 * i:128 * i + 128],
                                      n.sb_id[:])
                n.PT1.prod_done(mm)
            n.block.tensor(tT)

            def vT(ve):
                n.PT1.cons_wait(ve)
                ctx.wait(ve, "sO")
                n.PT1.cons_done(ve.tensor_copy(n.tstage[:], n.psT1[:, 0:512]))
                ctx.inc(ve.tensor_copy(n.rrepA[:], n.rrepA[:]), "sV")
            n.block.vector(vT)

            def soT(sq):
                ctx.wait(sq, "sV")
                gcs = _gcol(s)
                for i in range(4):
                    ctx.inc(sq.dma_start(
                        out=n.dram["cc2_in"][gcs + 128 * i:gcs + 128 * i + 128,
                                             128 * qh:128 * qh + 128],
                        in_=n.tstage[:, 128 * i:128 * i + 128]), "sO", 16)
                    if "attdbg" in n.dram:
                        ctx.inc(sq.dma_start(
                            out=n.dram["attdbg"][gcs + 128 * i:gcs + 128 * i + 128,
                                                 128 * qh:128 * qh + 128],
                            in_=n.tstage[:, 128 * i:128 * i + 128]), "sO", 16)
            n.block.scalar(soT)

    return emit_tiles, emit_drain


def emit_l2_taps(n, taps, tap_out):
    ctx = n.ctx

    def f(sq):
        for s_ in ["sV", "sS", "sO", "sW", "sG", "sGt0", "sGt1"]:
            ctx.wait(sq, s_)
        for tname, shape, dt in taps:
            src = resolve_tap(n, tname)
            ctx.inc(sq.dma_start(out=tap_out[tname], in_=src), "sO", 16)
        ctx.wait(sq, "sO")
    n.block.scalar(f)



def build_launch3(taps=()):
    nc = bass.Bass("TRN2", target_bir_lowering=False)
    dram = {}
    din, dout, dint = _mk_dram_helpers(nc, dram)

    din("x1c", [D, TC], BF16)
    din("xn1c", [D, TC], BF16)
    din("attTc", [D, TC], BF16)
    din("w12_blk", [64, 128, D], F32)
    din("b12_pk", [128, 64], F32)
    din("w3_blk", [NDT, 128, D], F32)
    din("b3_pk", [128, 32], F32)
    din("wo_blk", [NDT, 128, D], F32)
    din("bo_pk", [128, 32], F32)
    din("id_bf", [128, 128], BF16)
    dout("y", [TC, D], F32)

    tap_out = {}
    for tname, shape, dt in taps:
        tap_out[tname] = dout("tap_" + tname, list(shape), dt)

    with contextlib.ExitStack() as stack:
        n = types.SimpleNamespace()
        n.nc = nc
        n.dram = dram
        sb = lambda name, shape, dt: stack.enter_context(nc.sbuf_tensor(name, list(shape), dt))
        ps = lambda name, shape, dt: stack.enter_context(nc.psum_tensor(name, list(shape), dt))

        n.bufA = sb("bufA", [128, NDT * TC], BF16)
        n.bufB = sb("bufB", [128, NDT * TC], BF16)
        n.A = lambda d_: n.bufA[:, d_ * TC:(d_ + 1) * TC]
        n.B = lambda d_: n.bufB[:, d_ * TC:(d_ + 1) * TC]
        n.wbuf = sb("wbuf", [128, 4 * D], BF16)
        n.silu = sb("silu", [128, TC], BF16)
        n.rstage = sb("rstage", [128, 2 * TC], BF16)
        n.ostage = sb("ostage", [128, 2 * D], F32)
        n.b12b = sb("b12b", [128, 64], F32)
        n.b3b = sb("b3b", [128, NDT], F32)
        n.bob = sb("bob", [128, NDT], F32)
        n.sb_id = sb("sb_id", [128, 128], BF16)
        n.sspacer = n.b12b[0:1, 0:64]

        n.psE = ps("psE", [128, 1024], F32)
        n.psF = ps("psF", [128, 1024], F32)
        n.psT1 = ps("psT1", [128, 1024], BF16)
        n.psT2 = ps("psT2", [128, 1024], BF16)

        ctx = n.ctx = Ctx(nc, stack)
        for s_ in ["sW", "sG", "sO", "sS", "sV", "sT", "sWl", "sVx",
                   "sWl0", "sWl1", "sWx0", "sWx1", "sGg0", "sGg1"]:
            ctx.sem(s_)
        n.PE = PsRes(ctx, n.psE)
        n.PF = PsRes(ctx, n.psF)
        n.PT1 = PsRes(ctx, n.psT1)
        n.PT2 = PsRes(ctx, n.psT2)

        n.fused = False
        with nc.Block() as block:
            n.block = block
            emit_l3_consts(n)
            emit_l3(n, taps, tap_out)
    return nc, dram


def emit_l3_consts(n):
    ctx, d = n.ctx, n.dram

    def consts(sy):
        if "id_bf" in d and not n.fused:
            ctx.inc(sy.dma_start(out=n.sb_id[:], in_=d["id_bf"]), "sW", 16)
        ctx.inc(sy.dma_start(out=n.b12b[:], in_=d["b12_pk"]), "sW", 16)
        ctx.inc(sy.dma_start(out=n.b3b[:], in_=d["b3_pk"]), "sW", 16)
        ctx.inc(sy.dma_start(out=n.bob[:], in_=d["bo_pk"]), "sW", 16)
    n.block.sync(consts)


def emit_l3(n, taps, tap_out, l2done=None):
    ctx, d = n.ctx, n.dram
    paranoid = getattr(n, "paranoid", set()) or set()

    def loadxn(sy):
        if n.fused:
            for nm in (n.PS1.tname, n.PS2.tname, n.PE.tname, n.PAV.tname):
                ctx.wait(sy, nm, l2done[nm])
        for dt_ in range(NDT):
            ctx.inc(sy.dma_start(out=n.A(dt_),
                                 in_=d["xn1c"][128 * dt_:128 * dt_ + 128, :]), "sW", 16)
    n.block.sync(loadxn)
    n.c_sW = ctx.val("sW")
    if "l3a" in paranoid:
        emit_barrier(n, "l3a")

    wl0 = getattr(n, "wl0_pre", None) if n.fused else None
    if wl0 is None:
        wl0 = [ctx.val("sWl0"), ctx.val("sWl1")]
    f10 = ctx.val(n.PF.tname)
    wslot = lambda mc: n.wbuf[:, 2 * D * (mc % 2):2 * D * (mc % 2) + 2 * D]

    def g1_part(mcs):
        def g1(gq):
            for mc in mcs:
                if mc == 0 and n.fused:
                    ctx.wait(gq, "sP", l2done["sP"])
                if mc >= 2:
                    gq.wait_ge(ctx.sems[n.PF.tname], f10 + mc - 1)
                ctx.inc(gq.dma_start(out=wslot(mc)[:, 0:D],
                                     in_=d["w12_blk"][mc]),
                        f"sWl{mc % 2}", 16)
                ctx.inc(gq.dma_start(out=wslot(mc)[:, D:2 * D],
                                     in_=d["w12_blk"][32 + mc]),
                        f"sWl{mc % 2}", 16)
        return g1
    if n.fused:
        n.block.gpsimd(g1_part(range(2, NDT)))
    else:
        n.block.gpsimd(g1_part(range(NDT)))

    for mc in range(NDT):
        def t1(te, mc=mc):
            if mc == 0:
                ctx.wait(te, "sW", n.c_sW)
                if n.fused:
                    ctx.wait(te, n.PAV.cname, l2done[n.PAV.cname])
                    ctx.wait(te, n.PD.cname, l2done[n.PD.cname])
            te.wait_ge(ctx.sems[f"sWl{mc % 2}"], wl0[mc % 2] + 32 * (mc // 2 + 1))
            n.PE.prod_wait(te)
            n.PF.prod_wait(te)
            ws = wslot(mc)
            for dt_ in range(NDT):
                first, last = dt_ == 0, dt_ == NDT - 1
                w1 = ws[:, 128 * dt_:128 * dt_ + 128]
                w2 = ws[:, D + 128 * dt_:D + 128 * dt_ + 128]
                mm1a = te.matmul(n.psE[:, 0:512], w1, n.A(dt_)[:, 0:512],
                                 start=first, stop=last)
                mm1b = te.matmul(n.psE[:, 512:1024], w1, n.A(dt_)[:, 512:1024],
                                 start=first, stop=last)
                mm2a = te.matmul(n.psF[:, 0:512], w2, n.A(dt_)[:, 0:512],
                                 start=first, stop=last)
                mm2b = te.matmul(n.psF[:, 512:1024], w2, n.A(dt_)[:, 512:1024],
                                 start=first, stop=last)
            n.PE.prod_done(mm1b)
            n.PF.prod_done(mm2b)
        n.block.tensor(t1)

        def s1(sc, mc=mc):
            n.PE.cons_wait(sc)
            sc.activation(n.silu[:], n.psE[:], AF.Silu, bias=n.b12b[:, mc:mc + 1])
            n.PE.cons_done(sc.copy(n.sspacer[:], n.sspacer[:]))
        n.block.scalar(s1)

        def v1(ve, mc=mc):
            ctx.wait(ve, n.PE.cname)
            n.PF.cons_wait(ve)
            n.PF.cons_done(
                ve.scalar_tensor_tensor(n.B(mc), n.psF[:], n.b12b[:, 32 + mc:33 + mc],
                                        n.silu[:], ALU.add, ALU.mult))
        n.block.vector(v1)
    if "l3ffn" in paranoid:
        emit_barrier(n, "l3ffn")

    e20 = ctx.val(n.PE.tname)
    f20 = ctx.val(n.PF.tname)
    wl3 = [ctx.val("sWl0"), ctx.val("sWl1")]
    w3slot = lambda oc: n.wbuf[:, 2 * D * (oc % 2):2 * D * (oc % 2) + D]

    def g2(gq):
        for oc in range(NDT):
            if oc < 2:
                gq.wait_ge(ctx.sems[n.PE.tname], e20)
                gq.wait_ge(ctx.sems[n.PF.tname], f20)
            elif oc % 2 == 0:
                gq.wait_ge(ctx.sems[n.PE.tname], e20 + oc // 2)
            else:
                gq.wait_ge(ctx.sems[n.PF.tname], f20 + (oc - 1) // 2)
            ctx.inc(gq.dma_start(out=w3slot(oc), in_=d["w3_blk"][oc]),
                    f"sWl{oc % 2}", 16)
    n.block.gpsimd(g2)

    for oc in range(NDT):
        res = n.PE if oc % 2 == 0 else n.PF

        def t2(te, oc=oc, res=res):
            te.wait_ge(ctx.sems[f"sWl{oc % 2}"], wl3[oc % 2] + 16 * (oc // 2 + 1))
            res.prod_wait(te)
            ws = w3slot(oc)
            for mc in range(NDT):
                first, last = mc == 0, mc == NDT - 1
                mma = te.matmul(res.ap[:, 0:512], ws[:, 128 * mc:128 * mc + 128],
                                n.B(mc)[:, 0:512], start=first, stop=last)
                mmb = te.matmul(res.ap[:, 512:1024], ws[:, 128 * mc:128 * mc + 128],
                                n.B(mc)[:, 512:1024], start=first, stop=last)
            res.prod_done(mmb)
        n.block.tensor(t2)

        def v2(ve, oc=oc, res=res):
            res.cons_wait(ve)
            res.cons_done(ve.tensor_scalar(n.A(oc), res.ap[:], n.b3b[:, oc:oc + 1],
                                           None, ALU.add))
        n.block.vector(v2)
    acc_sV = ctx.val(n.PE.cname) + ctx.val(n.PF.cname)
    if "l3w3" in paranoid:
        emit_barrier(n, "l3w3")

    t2e = ctx.val(n.PE.tname)
    t2f = ctx.val(n.PF.tname)

    if not n.fused:
        def syA(sy_):
            sy_.wait_ge(ctx.sems[n.PE.tname], t2e)
            sy_.wait_ge(ctx.sems[n.PF.tname], t2f)
            for dt_ in range(NDT):
                ctx.inc(sy_.dma_start(out=n.B(dt_),
                                      in_=d["attTc"][128 * dt_:128 * dt_ + 128, :]),
                        "sW", 16)
        n.block.sync(syA)
        attT_sW = ctx.val("sW")
    else:
        def gg(gq):
            ctx.wait(gq, "sC2")
            gq.wait_ge(ctx.sems[n.PE.tname], t2e)
            gq.wait_ge(ctx.sems[n.PF.tname], t2f)
            for rr in range(NCORE):
                if rr >= 2:
                    ctx.wait(gq, "sTgg", rr - 1)
                ctx.inc(gq.indirect_dma_start(
                    n.gslot(rr)[:, 0:4096], None,
                    d["cc2_out"],
                    bass.IndirectOffsetOnAxis(n.gidx[:, rr:rr + 1], 0)),
                    f"sGg{rr % 2}", 16)
        n.block.gpsimd(gg)
        gg0 = [ctx.val("sGg0") - 16 * (NCORE // 2), ctx.val("sGg1") - 16 * (NCORE // 2)]

        for rr in range(NCORE):
            res = n.PT1 if rr % 2 == 0 else n.PT2
            for j in range(8):
                def tg(te, rr=rr, j=j, res=res):
                    if j == 0:
                        te.wait_ge(ctx.sems[f"sGg{rr % 2}"],
                                   gg0[rr % 2] + 16 * (rr // 2 + 1))
                    res.prod_wait(te)
                    for k in range(4):
                        mm = te.transpose(
                            res.ap[:, 128 * k:128 * k + 128],
                            n.gslot(rr)[:, 512 * j + 128 * k:512 * j + 128 * k + 128],
                            n.sb_id[:])
                    res.prod_done(mm)
                n.block.tensor(tg)

                def vg(ve, rr=rr, j=j, res=res):
                    res.cons_wait(ve)
                    for k in range(4):
                        dt_ = 4 * rr + k
                        last = ve.tensor_copy(
                            n.bufB[:, dt_ * TC + j:dt_ * TC + j + 1017:8],
                            res.ap[:, 128 * k:128 * k + 128])
                    res.cons_done(last)
                    if j == 7:
                        ctx.inc(ve.tensor_copy(n.rstage[:, 0:512],
                                               n.rstage[:, 0:512]), "sTgg")
                n.block.vector(vg)
        attT_sW = ctx.val("sW")
    if "l3gather" in paranoid:
        emit_barrier(n, "l3gather")

    e30 = ctx.val(n.PE.tname)
    f30 = ctx.val(n.PF.tname)
    wl4 = [ctx.val("sWl0"), ctx.val("sWl1")]

    def g3(gq):
        for oc in range(NDT):
            if oc < 2:
                gq.wait_ge(ctx.sems[n.PE.tname], e30)
                gq.wait_ge(ctx.sems[n.PF.tname], f30)
            elif oc % 2 == 0:
                gq.wait_ge(ctx.sems[n.PE.tname], e30 + oc // 2)
            else:
                gq.wait_ge(ctx.sems[n.PF.tname], f30 + (oc - 1) // 2)
            ctx.inc(gq.dma_start(out=w3slot(oc), in_=d["wo_blk"][oc]),
                    f"sWl{oc % 2}", 16)
    n.block.gpsimd(g3)

    for oc in range(NDT):
        res = n.PE if oc % 2 == 0 else n.PF

        def t3(te, oc=oc, res=res):
            if oc == 0:
                if n.fused:
                    ctx.wait(te, "sTgg")
                else:
                    te.wait_ge(ctx.sems["sW"], attT_sW)
            te.wait_ge(ctx.sems[f"sWl{oc % 2}"], wl4[oc % 2] + 16 * (oc // 2 + 1))
            res.prod_wait(te)
            ws = w3slot(oc)
            for mc in range(NDT):
                first, last = mc == 0, mc == NDT - 1
                mma = te.matmul(res.ap[:, 0:512], ws[:, 128 * mc:128 * mc + 128],
                                n.B(mc)[:, 0:512], start=first, stop=last)
                mmb = te.matmul(res.ap[:, 512:1024], ws[:, 128 * mc:128 * mc + 128],
                                n.B(mc)[:, 512:1024], start=first, stop=last)
            res.prod_done(mmb)
        n.block.tensor(t3)

        def v3(ve, oc=oc, res=res):
            res.cons_wait(ve)
            ve.tensor_scalar(n.rstage[:, 0:TC], res.ap[:], n.bob[:, oc:oc + 1],
                             None, ALU.add)
            res.cons_done(ve.tensor_add(n.A(oc), n.A(oc), n.rstage[:, 0:TC]))
        n.block.vector(v3)
    wo_done_sV = ctx.val(n.PE.cname)
    if "l3wo" in paranoid:
        emit_barrier(n, "l3wo")

    w5 = [ctx.val("sWx0"), ctx.val("sWx1")]
    vx0 = ctx.val("sVx")

    def syx(sy_):
        ctx.wait(sy_, n.PE.cname)
        ctx.wait(sy_, n.PF.cname)
        for dt_ in range(NDT):
            if dt_ >= 2:
                ctx.wait(sy_, "sVx", vx0 + dt_ - 1)
            ctx.inc(sy_.dma_start(out=n.rstage[:, TC * (dt_ % 2):TC * (dt_ % 2) + TC],
                                  in_=d["x1c"][128 * dt_:128 * dt_ + 128, :]),
                    f"sWx{dt_ % 2}", 16)
    n.block.sync(syx)

    def vx(ve):
        for dt_ in range(NDT):
            ve.wait_ge(ctx.sems[f"sWx{dt_ % 2}"], w5[dt_ % 2] + 16 * (dt_ // 2 + 1))
            sl = n.rstage[:, TC * (dt_ % 2):TC * (dt_ % 2) + TC]
            ctx.inc(ve.tensor_add(n.A(dt_), n.A(dt_), sl), "sVx")
    n.block.vector(vx)
    x2_sVx = ctx.val("sVx")

    for tcb in range(NTT):
        for grp in range(4):
            res = n.PT1 if grp % 2 == 0 else n.PT2

            def tt(te, tcb=tcb, grp=grp, res=res):
                if tcb == 0 and grp == 0:
                    ctx.wait(te, "sVx", x2_sVx)
                res.prod_wait(te)
                for k in range(8):
                    dt_ = grp * 8 + k
                    mm = te.transpose(res.ap[:, 128 * k:128 * k + 128],
                                      n.A(dt_)[:, 128 * tcb:128 * tcb + 128],
                                      n.sb_id[:])
                res.prod_done(mm)
            n.block.tensor(tt)

            def tv(ve, tcb=tcb, grp=grp, res=res):
                res.cons_wait(ve)
                if grp == 0:
                    ctx.wait(ve, "sO")
                for k in range(8):
                    dt_ = grp * 8 + k
                    last = ve.tensor_copy(
                        n.ostage[:, D * (tcb % 2) + 128 * (grp * 8 + k):
                                 D * (tcb % 2) + 128 * (grp * 8 + k) + 128],
                        res.ap[:, 128 * k:128 * k + 128])
                res.cons_done(last)
                if grp == 3:
                    ctx.inc(ve.tensor_copy(n.rstage[:, 0:512],
                                           n.rstage[:, 0:512]), "sV")
            n.block.vector(tv)

        def so(sq, tcb=tcb):
            ctx.wait(sq, "sV")
            ctx.inc(sq.dma_start(out=d["y"][128 * tcb:128 * tcb + 128, :],
                                 in_=n.ostage[:, D * (tcb % 2):D * (tcb % 2) + D]),
                    "sO", 16)
        if n.fused:
            n.block.gpsimd(so)
        else:
            n.block.scalar(so)

    def fin(sq):
        for s_ in ["sV", "sS", "sO", "sW", "sG", "sWl", "sWl0", "sWl1",
                   "sWx0", "sWx1"] + (["sGg0", "sGg1", "sWt0", "sWt1",
                                       "sWd0", "sWd1", "sG0", "sG1",
                                       "sGt0", "sGt1"] if n.fused else []):
            ctx.wait(sq, s_)
        for tname, shape, dt in taps:
            src_ap = resolve_tap(n, tname)
            ctx.inc(sq.dma_start(out=tap_out[tname], in_=src_ap), "sO", 16)
        ctx.wait(sq, "sO")
    n.block.scalar(fin)



def _prep_launch2_inputs(inputs, xn1g):
    wqkv = np.asarray(inputs["L1_attn_w_qkv"]).astype(np.float32)
    bqkv = np.asarray(inputs["L1_attn_b_qkv"]).astype(np.float32)
    wT = np.ascontiguousarray(wqkv.T)

    g = np.arange(T)
    r, loc = g // 1024, g % 1024
    strip = np.where(loc < 512, r, 15 - r)
    pos = 512 * strip + (loc % 512)
    inv = 1.0 / (10000.0 ** (np.arange(0, 128, 2) / 128.0))
    ang = pos[None, :] * np.repeat(inv, 2)[:, None]
    ctabK = np.cos(ang).astype(np.float32)
    stabK = np.sin(ang).astype(np.float32)
    sgn = np.where(np.arange(128) % 2 == 0, -1.0, 1.0).astype(np.float32)
    stabK = stabK * sgn[:, None]
    sc = 1.0 / np.sqrt(128.0)
    J = np.zeros((128, 128), dtype=np.float32)
    for f in range(64):
        J[2 * f + 1, 2 * f] = 1.0
        J[2 * f, 2 * f + 1] = 1.0
    dmask = np.zeros((4, 128, 512), dtype=np.float32)
    for u in range(4):
        dmask[u] = (128 * u + np.arange(128)[:, None]) <= np.arange(512)[None, :]

    base = {
        "xn1g": xn1g,
        "ctabK": ctabK.astype(BF), "stabK": stabK.astype(BF),
        "ctabQ": (ctabK * sc).astype(BF), "stabQ": (stabK * sc).astype(BF),
        "Jmat": J.astype(BF), "dmask": dmask.astype(BF),
        "id_bf": np.eye(128).astype(BF),
        "ones_p1_bf": np.ones((128, 1), dtype=BF),
        "ones_1p_f32": np.ones((1, 128), dtype=np.float32),
    }
    in_maps = []
    for c in range(NCORE):
        m = dict(base)
        qcols = slice(512 * c, 512 * c + 512)
        m["wq_blk"] = pack_lhsT(np.ascontiguousarray(wT[:, qcols]))
        m["wk_blk"] = pack_lhsT(np.ascontiguousarray(wT[:, 4096 + 128 * c:4096 + 128 * c + 128]))
        m["wv_blk"] = pack_lhsT(np.ascontiguousarray(wT[:, 5120 + 128 * c:5120 + 128 * c + 128]))
        m["bq"] = np.ascontiguousarray(bqkv[qcols].reshape(512, 1))
        m["bk"] = np.ascontiguousarray(bqkv[4096 + 128 * c:4096 + 128 * c + 128].reshape(128, 1))
        m["bv"] = np.ascontiguousarray(bqkv[5120 + 128 * c:5120 + 128 * c + 128].reshape(128, 1))
        in_maps.append(m)
    return in_maps


def _prep_launch3_inputs(inputs, x1s, xn1s, attTs):
    w12T = np.ascontiguousarray(np.asarray(inputs["L1_ffn_w12"]).astype(np.float32).T)
    w3T = np.ascontiguousarray(np.asarray(inputs["L1_ffn_w3"]).astype(np.float32).T)
    woT = np.ascontiguousarray(np.asarray(inputs["L1_attn_w_o"]).astype(np.float32).T)
    base = {
        "w12_blk": pack_lhsT(w12T),
        "b12_pk": np.ascontiguousarray(
            np.asarray(inputs["L1_ffn_b12"]).astype(np.float32).reshape(64, 128).T),
        "w3_blk": pack_lhsT(w3T),
        "b3_pk": np.ascontiguousarray(
            np.asarray(inputs["L1_ffn_b3"]).astype(np.float32).reshape(32, 128).T),
        "wo_blk": pack_lhsT(woT),
        "bo_pk": np.ascontiguousarray(
            np.asarray(inputs["L1_attn_b_o"]).astype(np.float32).reshape(32, 128).T),
        "id_bf": np.eye(128).astype(BF),
    }
    in_maps = []
    for c in range(NCORE):
        m = dict(base)
        m["x1c"] = x1s[c]
        m["xn1c"] = xn1s[c]
        m["attTc"] = attTs[c]
        in_maps.append(m)
    return in_maps


_BUILD_CACHE = {}


def _get(name, fn):
    if name not in _BUILD_CACHE:
        _BUILD_CACHE[name] = fn()
    return _BUILD_CACHE[name]





def emit_barrier(n, tag):
    ctx = n.ctx
    names = [s_ for s_ in ctx.sems if ctx.cnt[s_] > 0]

    def mk(eng):
        for s_ in names:
            ctx.wait(eng, s_)
    n.block.sync(mk)
    n.block.gpsimd(mk)
    n.block.vector(mk)
    n.block.scalar(mk)
    n.block.tensor(mk)


def build_fused(taps=(), dbg_l1=False, paranoid=()):
    nc = bass.Bass("TRN2", target_bir_lowering=False)
    dram = {}
    din, dout, dint = _mk_dram_helpers(nc, dram)

    din("x0c", [TC, D], F32)
    for nm, shp in [("lam", [128, 1]), ("vc_rT", [64, 64]), ("vc_iT", [64, 64]),
                    ("bpv", [64, 1]), ("ct_r", [64, 1]), ("ct_i", [64, 1]),
                    ("p_r", [64, 1]), ("p_i", [64, 1]), ("q_r", [64, 1]),
                    ("q_i", [64, 1]), ("dpar", [1, 1]), ("logstep", [1, 1]),
                    ("b_base_pk", [128, 32]), ("wrou_pk", [128, 256]),
                    ("aab_pk", [128, 4096]),
                    ("ball", [128, D]), ("t_r_row", [1, D]), ("t_i_row", [1, D]),
                    ("cl_r", [128, 32]), ("cl_i", [128, 32]),
                    ("ones_1p_f32", [1, 128])]:
        din(nm, shp, F32)
    din("cA", [D, 128], BF16)
    din("sA", [D, 128], BF16)
    din("wbase_blk", [NDT, 128, D], F32)
    for nm, shp in [("csM", [D, 2 * D]), ("trimask", [128, 128]),
                    ("id_bf", [128, 128]), ("ones_p1_bf", [128, 1]),
                    ("E8", [8, 128])]:
        din(nm, shp, BF16)
    din("wq_blk", [4, 128, D], F32)
    din("wk_blk", [1, 128, D], F32)
    din("wv_blk", [1, 128, D], F32)
    din("bq", [512, 1], F32)
    din("bk", [128, 1], F32)
    din("bv", [128, 1], F32)
    din("ctabK", [128, T], BF16)
    din("stabK", [128, T], BF16)
    din("ctabQ", [128, T], BF16)
    din("stabQ", [128, T], BF16)
    din("Jmat", [128, 128], BF16)
    din("dmask", [4, 128, 512], BF16)
    din("gidx", [128, 8], mybir.dt.int32)
    din("w12_blk", [64, 128, D], F32)
    din("b12_pk", [128, 64], F32)
    din("w3_blk", [NDT, 128, D], F32)
    din("b3_pk", [128, 32], F32)
    din("wo_blk", [NDT, 128, D], F32)
    din("bo_pk", [128, 32], F32)

    dout("y", [TC, D], F32)

    dint("kbounce", [8, D], F32)
    dint("xs0", [D, TC], BF16)
    if dbg_l1:
        xs1 = dout("xs1", [D, TC], BF16)
        dout("attdbg", [T, 512], BF16)
    else:
        xs1 = dint("xs1", [D, TC], BF16)
    cc1_in = dint("cc1_in", [D, TC], BF16)
    cc1_out = dint("cc1_out", [NCORE * D, TC], BF16, addr_space="Shared")
    cc2_in = dint("cc2_in", [T, 512], BF16)
    cc2_out = dint("cc2_out", [NCORE * T, 512], BF16, addr_space="Shared")
    dram["x1_out"] = xs1
    dram["xn1_out"] = cc1_in
    dram["xn1g"] = cc1_out
    dram["x1c"] = xs1
    dram["xn1c"] = cc1_in

    tap_out = {}
    for tname, shape, dt in taps:
        tap_out[tname] = dout("tap_" + tname, list(shape), dt)

    with contextlib.ExitStack() as stack:
        n = _alloc_l1(nc, stack, dram, fused=True)
        n.fused = True
        sb = lambda name, shape, dt: stack.enter_context(nc.sbuf_tensor(name, list(shape), dt))

        n.Kbuf = n.bufA[:, 0:T]
        n.Vtok = n.bufA[:, T:2 * T]
        n.wqb = n.bufA[:, 2 * T:2 * T + 4 * D]
        n.xbig = n.bufB[:, 0:16384]
        n.wkb = n.bufB[:, 16384:16384 + D]
        n.wvb = n.bufB[:, 16384 + D:16384 + 2 * D]
        n.qro = n.rstage[:, 0:2048]
        n.pbuf = n.sqst[:, 0:1536]
        n.pxtra = n.mrep[:, 512:1024]
        n.sqb = n.sqst[:, 1536:2048]
        n.silu = n.rstage[:, 0:TC]
        n.gslot = lambda rr: (n.wbuf[:, D:2 * D] if rr % 2 == 0
                              else n.wbuf[:, 3 * D:4 * D])
        n.tstg = n.bufB[:, 24576:24576 + 2048]
        n.kst = n.cscr[:, 0:512]
        n.vst = n.mrep[:, 0:512]
        n.qtmp = n.cscr[:, 512:1024]
        n.rowz = n.cscr[0:1, 1024:1536]
        n.sc2 = sb("sc2", [1, 16], F32)
        n.sc2b = sb("sc2b", [1, 32], F32)
        n.negb = sb("negb", [128, 16], F32)
        n.accden = n.cscr[:, 2048:2560]
        n.rrepA = n.krr[:, :]
        n.tstage = n.rrep[:, 512:1024]
        n.bqb = sb("bqb", [128, 4], F32)
        n.bkb = sb("bkb", [128, 1], F32)
        n.bvb = sb("bvb", [128, 1], F32)
        n.sb_J = sb("sb_J", [128, 128], BF16)
        n.gidx = sb("gidx_sb", [128, 8], mybir.dt.int32)
        n.sb_dm = n.Tmat[:, 0:2048]
        n.b12b = sb("b12b", [128, 64], F32)
        n.b3b = sb("b3b", [128, NDT], F32)
        n.bob = sb("bob", [128, NDT], F32)
        n.ostage = n.wbuf[:, 0:2 * D]

        ctx = n.ctx = Ctx(nc, stack)
        for s_ in ["sW", "sG", "sO", "sS", "sV", "sT", "sWt", "sVq", "sWd",
                   "sWl", "sWf", "sTg", "sVp", "sTp", "sVx", "sOx", "sWr",
                   "sVr", "sWk", "sVk", "sC1", "sC2", "sGg", "sTgg", "sP",
                   "sX", "sKV", "sWt0", "sWt1", "sWd0", "sWd1", "sG0", "sG1",
                   "sWl0", "sWl1", "sWx0", "sWx1", "sGt0", "sGt1",
                   "sGg0", "sGg1"]:
            ctx.sem(s_)
        n.PA = PsRes(ctx, n.psA)
        n.PB = PsRes(ctx, n.psB)
        n.PE = PsRes(ctx, n.psE)
        n.PF = PsRes(ctx, n.psF)
        n.PT1 = PsRes(ctx, n.psT1)
        n.PT2 = PsRes(ctx, n.psT2)
        n.psQ = n.psE
        n.PQ = n.PE
        n.psS1 = n.psA
        n.PS1 = n.PA
        n.psS2 = n.psB
        n.PS2 = n.PB
        n.psAV = n.psF[:, 0:512]
        n.psD = n.psF[:, 512:1024]
        n.PAV = PsRes(ctx, n.psAV)
        n.PD = PsRes(ctx, n.psD)
        n.psVT = n.psT1[:, 0:512]
        n.PVT = n.PT1

        with nc.Block() as block:
            n.block = block
            emit_const_loads(n)
            emit_l2_consts_sm(n, with_dm=False)
            emit_l3_consts(n)
            bset = paranoid if isinstance(paranoid, (set, list, tuple)) else (
                {"ssm", "tb", "pro", "ln0", "ap0", "ffn0", "conv", "x1", "postL1",
                 "postAG1w", "postKV", "postATT", "postAG2", "l3a", "l3ffn",
                 "l3w3", "l3gather", "l3wo"} if paranoid else set())
            n.bset = bset
            P = lambda tag: (emit_barrier(n, tag) if tag in bset else None)
            emit_ssm_prep(n)
            P("ssm")
            emit_tbuild(n)
            P("tb")
            emit_prologue(n)
            P("pro")
            emit_lnstats(n, n.A)
            P("ln0")
            emit_apply_ln(n, n.A, n.B)
            P("ap0")
            emit_ffn0(n)
            P("ffn0")
            emit_conv(n)
            P("conv")
            emit_x1(n)
            P("x1")
            emit_l2_dm(n)
            P("postL1")
            xn1_sOx = ctx.val("sOx")

            def ag1(gq):
                ctx.wait(gq, "sOx", xn1_sOx)
                ctx.inc(gq.collective_compute(
                    "AllGather", ALU.bypass,
                    replica_groups=[list(range(NCORE))],
                    ins=[dram["cc1_in"].opt()], outs=[dram["cc1_out"].opt()]), "sC1")
            block.gpsimd(ag1)
            emit_l2_wload(n)
            P("postAG1w")
            emit_l2_kv(n)
            P("postKV")
            emit_l2_attn(n)
            P("postATT")
            sP_att = ctx.val("sP")
            n.wl0_pre = [ctx.val("sWl0"), ctx.val("sWl1")]

            def g1pre(gq):
                ctx.wait(gq, "sP", sP_att)
                for mc in range(2):
                    ctx.inc(gq.dma_start(
                        out=n.wbuf[:, 2 * D * mc:2 * D * mc + D],
                        in_=dram["w12_blk"][mc]), f"sWl{mc % 2}", 16)
                    ctx.inc(gq.dma_start(
                        out=n.wbuf[:, 2 * D * mc + D:2 * D * mc + 2 * D],
                        in_=dram["w12_blk"][32 + mc]), f"sWl{mc % 2}", 16)
            block.gpsimd(g1pre)

            att_sO = ctx.val("sO")

            def ag2(gq):
                ctx.wait(gq, "sO", att_sO)
                ctx.inc(gq.collective_compute(
                    "AllGather", ALU.bypass,
                    replica_groups=[list(range(NCORE))],
                    ins=[dram["cc2_in"].opt()], outs=[dram["cc2_out"].opt()]), "sC2")
            block.gpsimd(ag2)
            n.paranoid = bset
            if "postAG2" in bset:
                emit_barrier(n, "postAG2")
            l2done = {n.PS1.tname: ctx.val(n.PS1.tname),
                      n.PS2.tname: ctx.val(n.PS2.tname),
                      n.PE.tname: ctx.val(n.PE.tname),
                      n.PAV.tname: ctx.val(n.PAV.tname),
                      n.PAV.cname: ctx.val(n.PAV.cname),
                      n.PD.cname: ctx.val(n.PD.cname),
                      n.PT1.cname: ctx.val(n.PT1.cname),
                      "sP": ctx.val("sP"),
                      "sO": ctx.val("sO")}
            emit_l3(n, taps, tap_out, l2done=l2done)
    return nc, dram


def _prep_fused_inputs(inputs):
    in1 = _prep_launch1_inputs(inputs)
    in2 = _prep_launch2_inputs(inputs, None)
    in3 = _prep_launch3_inputs(inputs, [None] * NCORE, [None] * NCORE,
                               [None] * NCORE)
    in_maps = []
    for c in range(NCORE):
        m = dict(in1[c])
        m.update({k: v for k, v in in2[c].items() if v is not None and k != "xn1g"})
        m.update({k: v for k, v in in3[c].items() if v is not None})
        p = np.arange(128, dtype=np.int64)
        idx = (8192 * np.arange(8)[None, :] + 1024 * c + 8 * p[:, None])
        m["gidx"] = idx.astype(np.int32)
        in_maps.append(m)
    return in_maps


def kernel_unfused(**inputs):
    nc1, _ = _get("l1", lambda: build_launch1())
    in1 = _prep_launch1_inputs(inputs)
    r1 = run_bass_kernel_spmd(nc1, in1, core_ids=list(range(NCORE)))
    x1s = [r1.results[c]["x1_out"] for c in range(NCORE)]
    xn1s = [r1.results[c]["xn1_out"] for c in range(NCORE)]

    xn1g = np.ascontiguousarray(np.concatenate(xn1s, axis=0))
    nc2, _ = _get("l2", lambda: build_launch2())
    in2 = _prep_launch2_inputs(inputs, xn1g)
    r2 = run_bass_kernel_spmd(nc2, in2, core_ids=list(range(NCORE)))
    att = [r2.results[c]["attnout"] for c in range(NCORE)]

    attTs = [np.ascontiguousarray(np.concatenate(
        [att[r][:, 1024 * c:1024 * c + 1024] for r in range(NCORE)], axis=0))
        for c in range(NCORE)]
    nc3, _ = _get("l3", lambda: build_launch3())
    in3 = _prep_launch3_inputs(inputs, x1s, xn1s, attTs)
    r3 = run_bass_kernel_spmd(nc3, in3, core_ids=list(range(NCORE)))

    out = np.empty((T, D), dtype=np.float32)
    for c in range(NCORE):
        out[_strip_tokens(c), :] = r3.results[c]["y"]
    return out.reshape(1, T, D)


def kernel(**inputs):
    nc, _ = _get("fused", lambda: build_fused())
    in_maps = _prep_fused_inputs(inputs)
    res = run_bass_kernel_spmd(nc, in_maps, core_ids=list(range(NCORE)))
    out = np.empty((T, D), dtype=np.float32)
    for c in range(NCORE):
        out[_strip_tokens(c), :] = res.results[c]["y"]
    return out.reshape(1, T, D)
